# version 33
# baseline (speedup 1.0000x reference)
import math
from contextlib import ExitStack

import numpy as np
import ml_dtypes

import concourse.bass as bass
import concourse.mybir as mybir
from concourse.bass_utils import run_bass_kernel_spmd

F32 = mybir.dt.float32
BF16 = mybir.dt.bfloat16
AF = mybir.ActivationFunctionType
ALU = mybir.AluOpType
AX = mybir.AxisListType

P = 128
MAGIC = 12582912.0
QMAX = 32767.0
FIXQ = 65536.0
WQMAX = 7.0
WBLK = 32
EPS = 1e-5
NEG = -1.0e30


class Cfg:
    def __init__(self, T=4096, E=1024, I=3072, n_cores=8,
                 attn_scale=1.0 / 32.0):
        self.T, self.E, self.I, self.n_cores = T, E, I, n_cores
        self.attn_scale = attn_scale
        self.n_tt = T // P
        self.n_ec = E // P
        self.n_ic = I // P
        self.NW = 512
        self.SBW = 512
        self.n_sb = T // self.SBW
        self.W3 = min(2048, T)
        self.F5S = min(512, T)
        self.SH_E = E // n_cores
        self.SH_I = I // n_cores
        self.n_ln = (E + 511) // 512
        assert T % 512 == 0 and E % P == 0 and I % P == 0
        assert self.SH_E % P == 0 and self.SH_I % P == 0 and self.SH_E % WBLK == 0


SEMKIND = {
    "sync": {"d": "DSY"},
    "gpsimd": {"d": "DGP", "c": "GP", "cc": "GPC"},
    "tensor": {"c": "TE"},
    "vector": {"c": "V"},
    "scalar": {"c": "SC"},
}


class Builder:

    def __init__(self, nc, es):
        self.nc = nc
        self.es = es
        self.streams = {k: [] for k in ("sync", "tensor", "vector", "scalar", "gpsimd")}
        self.sems = {}
        for name in ("GP", "GPC", "TE", "V", "SC"):
            self._mksem(name)
        self.res = {}
        self.sat = {k: {} for k in self.streams}
        self.barrier = {}

    def _mksem(self, name):
        h = self.es.enter_context(self.nc.semaphore("s_" + name.replace(":", "_")))
        self.sems[name] = [h, 0]
        return self.sems[name]

    def snap(self):
        return {k: v[1] for k, v in self.sems.items()}

    def set_barrier(self, snap):
        self.barrier = dict(snap)

    def ins(self, stream, kind, fn, reads=(), writes=(), chan=None):
        if kind == "d":
            assert chan is not None, "DMA needs a channel"
            semname = "D:" + chan
            sem = self.sems.get(semname) or self._mksem(semname)
            inc = 16
        else:
            semname = SEMKIND[stream][kind]
            sem = self.sems[semname]
            inc = 1
        need = {}

        def add(t):
            if t is not None:
                sn, v = t
                if v > 0:
                    need[sn] = max(need.get(sn, 0), v)

        for k in reads:
            r = self.res.get(k)
            if r is None:
                continue
            add(r[0])
        for k in writes:
            r = self.res.get(k)
            if r is None:
                for sn, v in self.barrier.items():
                    add((sn, v))
                continue
            add(r[0])
            for sn, v in r[1].items():
                add((sn, v))
        sat = self.sat[stream]
        waits = []
        for sn in sorted(need):
            v = need[sn]
            if sn.startswith("D:"):
                v = self.sems[sn][1]
            if v > 0 and sat.get(sn, 0) < v:
                waits.append((self.sems[sn][0], v))
                sat[sn] = v
        sem[1] += inc
        tick = sem[1]

        def wrapped(e, fn=fn, waits=tuple(waits), h=sem[0], inc=inc):
            for s, v in waits:
                e.wait_ge(s, v)
            fn(e).then_inc(h, inc)

        self.streams[stream].append(wrapped)
        done = (semname, tick)
        for k in writes:
            self.res[k] = [done, {}]
        for k in reads:
            r = self.res.setdefault(k, [None, {}])
            r[1][semname] = tick
        return done

    def raw(self, stream, fn):
        self.streams[stream].append(fn)


def build_nc(cfg: Cfg):
    T, E, I = cfg.T, cfg.E, cfg.I
    n_tt, n_ec, n_ic = cfg.n_tt, cfg.n_ec, cfg.n_ic
    NW, SBW, n_sb = cfg.NW, cfg.SBW, cfg.n_sb
    n_cores = cfg.n_cores
    n_ln = cfg.n_ln
    ASC = cfg.attn_scale

    nc = bass.Bass("TRN2", target_bir_lowering=False, debug=False)

    x_h = nc.dram_tensor("x", [T, E], F32, kind="ExternalInput")
    wqT_h = nc.dram_tensor("wqT", [E, E], F32, kind="ExternalInput")
    wkT_h = nc.dram_tensor("wkT", [E, E], F32, kind="ExternalInput")
    wvT_h = nc.dram_tensor("wvT", [E, E], F32, kind="ExternalInput")
    bq_h = nc.dram_tensor("bq", [P, E // P], F32, kind="ExternalInput")
    bk_h = nc.dram_tensor("bk", [P, E // P], F32, kind="ExternalInput")
    bvt_h = nc.dram_tensor("bvt", [P, E], F32, kind="ExternalInput")
    wu_sh_h = nc.dram_tensor("wu_sh", [I, cfg.SH_E], F32, kind="ExternalInput")
    wg_sh_h = nc.dram_tensor("wg_sh", [I, cfg.SH_E], F32, kind="ExternalInput")
    wd_sh_h = nc.dram_tensor("wd_sh", [E, cfg.SH_I], F32, kind="ExternalInput")
    mask_h = nc.dram_tensor("mask", [P, P], F32, kind="ExternalInput")
    idb_h = nc.dram_tensor("idb", [P, P], BF16, kind="ExternalInput")
    idf_h = nc.dram_tensor("idf", [P, P], F32, kind="ExternalInput")
    out_h = nc.dram_tensor("out", [T, E], F32, kind="ExternalOutput")

    wqb_s = nc.dram_tensor("wqb_s", [E // P, E // P, P, P], BF16)
    wkb_s = nc.dram_tensor("wkb_s", [E // P, E // P, P, P], BF16)
    wvb_s = nc.dram_tensor("wvb_s", [E, E], BF16)
    k_scr = nc.dram_tensor("k_scr", [n_tt, n_ec, P, P], BF16)
    p_scr = nc.dram_tensor("p_scr", [n_tt, n_tt, P, P], BF16)
    xm_scr = nc.dram_tensor("xm_scr", [T, E], F32)
    u_scr = nc.dram_tensor("u_scr", [n_ic, P, T], F32)
    g_scr = nc.dram_tensor("g_scr", [n_ic, P, T], F32)
    n_esh = cfg.SH_E // P
    n_ish = cfg.SH_I // P
    agu_in = nc.dram_tensor("agu_in", [n_esh, n_ic, P, P], BF16)
    agg_in = nc.dram_tensor("agg_in", [n_esh, n_ic, P, P], BF16)
    agd_in = nc.dram_tensor("agd_in", [n_ish, P, E], BF16)
    wu_g = nc.dram_tensor("wu_g", [n_cores * n_esh, n_ic, P, P], BF16)
    wg_g = nc.dram_tensor("wg_g", [n_cores * n_esh, n_ic, P, P], BF16)
    wd_g = nc.dram_tensor("wd_g", [n_cores * n_ish, P, E], BF16)
    cc_in = [nc.dram_tensor(f"cc_in{i}", [1, 2], F32) for i in range(4)]
    cc_out = [nc.dram_tensor(f"cc_out{i}", [1, 2], F32) for i in range(4)]

    es = ExitStack()
    A = es.enter_context
    B = Builder(nc, es)
    I_ = B.ins

    sb_mask = A(nc.sbuf_tensor("c_mask", [P, P], F32))
    sb_idb = A(nc.sbuf_tensor("c_idb", [P, P], BF16))
    sb_idf = A(nc.sbuf_tensor("c_idf", [P, P], F32))
    sb_ones = A(nc.sbuf_tensor("c_ones", [1, P], F32))
    sb_bq = A(nc.sbuf_tensor("c_bq", [P, n_ec], F32))
    sb_bk = A(nc.sbuf_tensor("c_bk", [P, n_ec], F32))
    sb_m2 = A(nc.sbuf_tensor("c_m2", [P, n_tt], F32))
    sb_rs2 = A(nc.sbuf_tensor("c_rs2", [P, n_tt], F32))
    sb_rs2r1 = A(nc.sbuf_tensor("c_rs2r1", [P, n_tt], F32))
    sb_cand = A(nc.sbuf_tensor("c_cand", [P, n_tt], F32))
    sb_gv = A(nc.sbuf_tensor("c_gv", [P, 32], F32))
    sb_sc = A(nc.sbuf_tensor("c_sc", [1, 2], F32))
    sb_sm = A(nc.sbuf_tensor("c_sm", [P, 4], F32))
    sb_smr = A(nc.sbuf_tensor("c_smr", [1, P], F32))
    sb_kc = A(nc.sbuf_tensor("c_kc", [P, 8], F32))
    nc.const_aps.register(F32, MAGIC, sb_kc[:, 0:1])
    DQW = max(4 * cfg.SH_E, cfg.SH_I + P)
    sb_dqw = A(nc.sbuf_tensor("dq_w", [P, DQW], F32))
    sb_dqs = A(nc.sbuf_tensor("dq_s", [P, DQW // WBLK], F32))
    sb_dqr = A(nc.sbuf_tensor("dq_r", [P, DQW // WBLK], F32))
    sb_dqq = A(nc.sbuf_tensor("dq_q", [P, DQW], F32))
    sb_dqb = A(nc.sbuf_tensor("dq_qb", [P, DQW], BF16))
    sb_dqt = A(nc.sbuf_tensor("dq_tr", [P, DQW], BF16))

    ps_a = A(nc.psum_tensor("ps_a", [P, 512], F32))
    ps_b = A(nc.psum_tensor("ps_b", [P, 512], F32))
    ps_e = A(nc.psum_tensor("ps_e", [P, 1024], F32))
    ps_f = A(nc.psum_tensor("ps_f", [P, 1024], F32))
    ps_g = A(nc.psum_tensor("ps_g", [P, 1024], BF16))
    ps_h = A(nc.psum_tensor("ps_h", [P, 512], F32))

    attspan = ExitStack()
    sb_qT = attspan.enter_context(nc.sbuf_tensor("qT", [P, n_ec * T], BF16))
    sb_v = attspan.enter_context(nc.sbuf_tensor("v", [P, n_tt * E], BF16))

    I_("sync", "d", lambda e: e.dma_start(out=sb_mask[:, :], in_=mask_h.ap()), writes=["mask"], chan="lcon")
    I_("sync", "d", lambda e: e.dma_start(out=sb_idb[:, :], in_=idb_h.ap()), writes=["idb"], chan="lcon")
    I_("sync", "d", lambda e: e.dma_start(out=sb_idf[:, :], in_=idf_h.ap()), writes=["idf"], chan="lcon")
    I_("sync", "d", lambda e: e.dma_start(out=sb_bq[:, :], in_=bq_h.ap()), writes=["bq"], chan="lcon")
    I_("sync", "d", lambda e: e.dma_start(out=sb_bk[:, :], in_=bk_h.ap()), writes=["bk"], chan="lcon")
    I_("vector", "c", lambda e: e.memset(sb_ones[:, :], 1.0), writes=["ones"])
    I_("vector", "c", lambda e: e.memset(sb_sc[:, :], 0.0), writes=["sc"])
    for _col, _val in ((0, MAGIC), (1, -0.5), (2, EPS), (3, FIXQ), (4, ASC), (5, 0.0), (6, 1.0), (7, 65535.0 + MAGIC)):
        I_("vector", "c", lambda e, c=_col, v=_val: e.memset(sb_kc[:, c:c + 1], v), writes=["kc"])
    I_("vector", "c", lambda e: e.tensor_scalar(sb_bq[:, :], sb_bq[:, :], ASC, None, op0=ALU.mult),
       reads=["bq"], writes=["bq"])
    for _ft in range(E // P):
        I_("gpsimd", "d", lambda e, ft=_ft: e.dma_start(
            out=wqb_s.ap()[ft, :, :, :],
            in_=wqT_h.ap()[:, ft * P:(ft + 1) * P].rearrange("(c p) fc -> c p fc", p=P)),
           writes=[("wqb_s", _ft)], chan="dqb")
        I_("gpsimd", "d", lambda e, ft=_ft: e.dma_start(
            out=wkb_s.ap()[ft, :, :, :],
            in_=wkT_h.ap()[:, ft * P:(ft + 1) * P].rearrange("(c p) fc -> c p fc", p=P)),
           writes=[("wkb_s", _ft)], chan="dqb")
    I_("gpsimd", "d", lambda e: e.dma_start(out=wvb_s.ap(), in_=wvT_h.ap()), writes=["wvb_s"], chan="dqb")

    qkv = ExitStack()
    Q = qkv.enter_context
    sb_xin = [Q(nc.sbuf_tensor(f"xin{i}", [P, E], F32)) for i in range(2)]
    sb_ln1 = [Q(nc.sbuf_tensor(f"ln1_{i}", [P, E], BF16)) for i in range(2)]
    sb_xTb = [Q(nc.sbuf_tensor(f"xTb{i}", [P, n_ec * SBW], BF16)) for i in range(2)]
    sb_bn = [Q(nc.sbuf_tensor(f"bn_t{i}", [P, 6 * n_ln + 4], F32)) for i in range(2)]
    sb_bvt = Q(nc.sbuf_tensor("bvt_t", [P, E], F32))
    sb_wqk = [Q(nc.sbuf_tensor(f"wqk{i}", [P, n_ec * P], BF16)) for i in range(4)]
    sb_wv = [Q(nc.sbuf_tensor(f"wvb{i}", [P, E], BF16)) for i in range(4)]
    sb_kst = [Q(nc.sbuf_tensor(f"kst{i}", [P, NW], BF16)) for i in range(2)]

    I_("sync", "d", lambda e: e.dma_start(out=sb_bvt[:, :], in_=bvt_h.ap()), writes=["bvt"], chan="lbvt")
    nts = SBW // P

    def feeder_tile(sb, tsub):
        sp = sb % 2
        if True:
            tt = sb * nts + tsub
            xb = tt % 2
            lb = tt % 2
            bn = sb_bn[lb]

            def ldx(e, tt=tt, xb=xb):
                return e.dma_start(out=sb_xin[xb][:, :], in_=x_h.ap()[tt * P:(tt + 1) * P, :])
            I_("sync", "d", ldx, writes=[("xin", xb)], chan=f"xin{xb}")

            def bns(e, xb=xb, bn=bn):
                for c in range(n_ln):
                    w = min(512, E - c * 512)
                    ii = e.bn_stats(bn[:, 6 * c:6 * c + 6], sb_xin[xb][:, c * 512:c * 512 + w])
                return ii
            I_("vector", "c", bns, reads=[("xin", xb)], writes=[("bn", lb)])
            I_("vector", "c", lambda e, bn=bn: e.bn_aggr(bn[:, 6 * n_ln:6 * n_ln + 2], bn[:, 0:6 * n_ln]),
               reads=[("bn", lb)], writes=[("bn", lb)])
            I_("scalar", "c", lambda e, bn=bn: e.activation(bn[:, 6 * n_ln + 2:6 * n_ln + 3],
                                                            bn[:, 6 * n_ln + 1:6 * n_ln + 2],
                                                            AF.Ln, bias=sb_kc[:, 2:3], scale=sb_kc[:, 6:7]),
               reads=[("bn", lb)], writes=[("bn", lb)])
            I_("scalar", "c", lambda e, bn=bn: e.activation(bn[:, 6 * n_ln + 3:6 * n_ln + 4],
                                                            bn[:, 6 * n_ln + 2:6 * n_ln + 3],
                                                            AF.Exp, bias=sb_kc[:, 5:6], scale=sb_kc[:, 1:2]),
               reads=[("bn", lb)], writes=[("bn", lb)])

            def nrm(e, xb=xb, lb=lb, bn=bn):
                return e.tensor_scalar(sb_ln1[lb][:, :], sb_xin[xb][:, :],
                                       bn[:, 6 * n_ln:6 * n_ln + 1],
                                       bn[:, 6 * n_ln + 3:6 * n_ln + 4],
                                       op0=ALU.subtract, op1=ALU.mult)
            I_("vector", "c", nrm, reads=[("xin", xb), ("bn", lb)], writes=[("ln1", lb)])

            def tp(e, lb=lb):
                for c in range(n_ec):
                    ii = e.transpose(ps_g[:, c * P:(c + 1) * P], sb_ln1[lb][:, c * P:(c + 1) * P],
                                     sb_idb[:, :])
                return ii
            I_("tensor", "c", tp, reads=[("ln1", lb), "idb"], writes=["psG"])

            def tev(e, tsub=tsub, sp=sp):
                for c in range(n_ec):
                    ii = e.tensor_copy(sb_xTb[sp][:, c * SBW + tsub * P: c * SBW + (tsub + 1) * P],
                                       ps_g[:, c * P:(c + 1) * P])
                return ii
            I_("vector", "c", tev, reads=["psG"], writes=[("xTb", sp, tsub)])

    def feeder(sb):
        for tsub in range(nts):
            feeder_tile(sb, tsub)

    def qk_section(sb, feed=None):
        cnt = [0]
        sp = sb % 2
        xTb_keys = [("xTb", sp, i) for i in range(nts)]
        for wsrc, srckey, bias_t, bkey, is_k in (
                (wqb_s, "wqb_s", sb_bq, "bq", False),
                (wkb_s, "wkb_s", sb_bk, "bk", True)):
            for f in range(n_ec):
                wb = f % 4

                def ldw(e, wsrc=wsrc, f=f, wb=wb):
                    return e.dma_start(out=sb_wqk[wb][:, :].rearrange("p (c f) -> p c f", f=P),
                                       in_=wsrc.ap()[f, :, :, :].rearrange("c p f -> p c f"))
                I_("gpsimd", "d", ldw, reads=[(srckey, f)], writes=[("wqk", wb)], chan=f"wqk{wb}")
                bank, ps = (["psA", ps_a] if f % 2 == 0 else ["psB", ps_b])

                def mm(e, wb=wb, ps=ps, sp=sp):
                    for c in range(n_ec):
                        ii = e.matmul(ps[:, 0:SBW], sb_wqk[wb][:, c * P:(c + 1) * P],
                                      sb_xTb[sp][:, c * SBW:(c + 1) * SBW],
                                      start=(c == 0), stop=(c == n_ec - 1))
                    return ii
                I_("tensor", "c", mm, reads=[("wqk", wb)] + xTb_keys, writes=[bank])

                if not is_k:
                    def ev(e, f=f, sb=sb, ps=ps, bias_t=bias_t):
                        return e.activation(sb_qT[:, f * T + sb * SBW: f * T + (sb + 1) * SBW],
                                            ps[:, 0:SBW], AF.Identity,
                                            bias=bias_t[:, f:f + 1], scale=sb_kc[:, 4:5])
                    I_("scalar", "c", ev, reads=[bank, bkey, "kc"], writes=["qT"])
                else:
                    kb = f % 2

                    def ev2(e, ps=ps, bias_t=bias_t, f=f, kb=kb):
                        return e.activation(sb_kst[kb][:, :], ps[:, 0:SBW], AF.Identity,
                                            bias=bias_t[:, f:f + 1], scale=sb_kc[:, 6:7])
                    I_("scalar", "c", ev2, reads=[bank, bkey, "kc"], writes=[("kst", kb)])

                    def kstore(e, sb=sb, f=f, kb=kb):
                        return e.dma_start(
                            out=k_scr.ap()[sb * nts:(sb + 1) * nts, f, :, :].rearrange("J p c -> p J c"),
                            in_=sb_kst[kb][:, :].rearrange("p (J c) -> p J c", c=P))
                    I_("scalar", "d", kstore, reads=[("kst", kb)], writes=["kscr"], chan="dks")
                cnt[0] += 1
                if feed is not None and cnt[0] % 3 == 0 and (cnt[0] // 3 - 1) < nts:
                    feeder_tile(feed, cnt[0] // 3 - 1)

    def v_section(sb):
        sp = sb % 2
        nhalf = (E + NW - 1) // NW
        vunits = [("psA", ps_a[:, :]), ("psB", ps_b[:, :]),
                  ("psE0", ps_e[:, 0:512]), ("psE1", ps_e[:, 512:1024]),
                  ("psF0", ps_f[:, 0:512]), ("psF1", ps_f[:, 512:1024]),
                  ("psG", ps_g[:, :].bitcast(F32)), ("psH", ps_h[:, :])]
        assert nts * nhalf <= len(vunits)
        for ci in range(n_ec):
            wb = ci % 4

            def ldwv(e, ci=ci, wb=wb):
                return e.dma_start(out=sb_wv[wb][:, :], in_=wvb_s.ap()[ci * P:(ci + 1) * P, :])
            I_("gpsimd", "d", ldwv, reads=["wvb_s"], writes=[("wv", wb)], chan=f"wv{wb}")
            for tsub in range(nts):
                for h in range(nhalf):
                    w = min(NW, E - h * NW)
                    uk, ups = vunits[tsub * nhalf + h]

                    def mmv(e, ci=ci, wb=wb, tsub=tsub, h=h, w=w, ups=ups, sp=sp):
                        return e.matmul(ups[:, 0:w],
                                        sb_xTb[sp][:, ci * SBW + tsub * P: ci * SBW + (tsub + 1) * P],
                                        sb_wv[wb][:, h * NW: h * NW + w],
                                        start=(ci == 0), stop=(ci == n_ec - 1))
                    I_("tensor", "c", mmv, reads=[("wv", wb), ("xTb", sp, tsub)], writes=[uk])
                    if ci == n_ec - 1:
                        def evv(e, tt=sb * nts + tsub, h=h, w=w, ups=ups):
                            return e.scalar_tensor_tensor(
                                sb_v[:, tt * E + h * NW: tt * E + h * NW + w], ups[:, 0:w],
                                1.0, sb_bvt[:, h * NW: h * NW + w], op0=ALU.mult, op1=ALU.add)
                        I_("vector", "c", evv, reads=[uk, "bvt"], writes=[("v", sb * nts + tsub)])

    feeder(0)
    for sb in range(n_sb):
        qk_section(sb, feed=(sb + 1) if sb + 1 < n_sb else None)
        v_section(sb)

    P_QKV = B.snap()
    qkv.close()

    dq_emitters = []

    def dequant(w_h, n_otiles, ncols, batch_ot, store_ap_fn, tag):
        n_ecs = ncols // P
        for b0 in range(0, n_otiles, batch_ot):
          def _batch(b0=b0, w_h=w_h, n_otiles=n_otiles, ncols=ncols, batch_ot=batch_ot,
                     store_ap_fn=store_ap_fn, tag=tag, n_ecs=n_ecs):
            bt = min(batch_ot, n_otiles - b0)
            w = bt * ncols
            nblk = w // WBLK

            def ld(e, b0=b0, bt=bt, ncols=ncols, w_h=w_h):
                return e.dma_start(
                    out=sb_dqw[:, 0:bt * ncols].rearrange("p (g e) -> p g e", e=ncols),
                    in_=w_h.ap()[b0 * P:(b0 + bt) * P, :].rearrange("(g p) e -> p g e", p=P))
            I_("gpsimd", "d", ld, writes=["dqw"], chan="dqw")

            def v1(e, w=w, nblk=nblk):
                return e.tensor_reduce(sb_dqs[:, 0:nblk],
                                       sb_dqw[:, 0:w].rearrange("p (b k) -> p b k", k=WBLK),
                                       axis=AX.X, op=ALU.max, apply_absolute_value=True)
            I_("vector", "c", v1, reads=["dqw"], writes=["dqs"])
            I_("vector", "c", lambda e, nblk=nblk: e.tensor_scalar(
                sb_dqs[:, 0:nblk], sb_dqs[:, 0:nblk], 1.0 / WQMAX, 1e-12, op0=ALU.mult, op1=ALU.max),
               reads=["dqs"], writes=["dqs"])
            I_("vector", "c", lambda e, nblk=nblk: e.reciprocal(sb_dqr[:, 0:nblk], sb_dqs[:, 0:nblk]),
               reads=["dqs"], writes=["dqr"])

            def v4(e, w=w, nblk=nblk):
                return e.tensor_tensor(sb_dqq[:, 0:w].rearrange("p (b k) -> p b k", k=WBLK),
                                       sb_dqw[:, 0:w].rearrange("p (b k) -> p b k", k=WBLK),
                                       sb_dqr[:, 0:nblk].unsqueeze(2).broadcast_to([P, nblk, WBLK]),
                                       op=ALU.mult)
            I_("vector", "c", v4, reads=["dqw", "dqr"], writes=["dqq"])
            I_("vector", "c", lambda e, w=w: e.tensor_scalar(
                sb_dqq[:, 0:w], sb_dqq[:, 0:w], MAGIC, MAGIC, op0=ALU.add, op1=ALU.subtract),
               reads=["dqq"], writes=["dqq"])

            def v6(e, w=w, nblk=nblk):
                return e.tensor_tensor(sb_dqb[:, 0:w].rearrange("p (b k) -> p b k", k=WBLK),
                                       sb_dqq[:, 0:w].rearrange("p (b k) -> p b k", k=WBLK),
                                       sb_dqs[:, 0:nblk].unsqueeze(2).broadcast_to([P, nblk, WBLK]),
                                       op=ALU.mult)
            I_("vector", "c", v6, reads=["dqq", "dqs"], writes=["dqb"])

            tiles = [(g, ec) for ec in range(n_ecs) for g in range(bt)]
            for t0 in range(0, len(tiles), 8):
                grp = tiles[t0:t0 + 8]

                def tp(e, grp=tuple(grp), ncols=ncols):
                    for si_, (g, ec) in enumerate(grp):
                        ii = e.transpose(ps_g[:, si_ * P:(si_ + 1) * P],
                                         sb_dqb[:, g * ncols + ec * P: g * ncols + (ec + 1) * P],
                                         sb_idb[:, :])
                    return ii
                I_("tensor", "c", tp, reads=["dqb", "idb"], writes=["psG"])

                def ev(e, grp=tuple(grp), t0=t0):
                    return e.tensor_copy(sb_dqt[:, t0 * P:(t0 + len(grp)) * P],
                                         ps_g[:, 0:len(grp) * P])
                I_("vector", "c", ev, reads=["psG"], writes=["dqt"])

            for ec in range(n_ecs):
                def st(e, ec=ec, b0=b0, bt=bt):
                    return e.dma_start(out=store_ap_fn(ec, b0, bt),
                                       in_=sb_dqt[:, ec * bt * P:(ec + 1) * bt * P])
                I_("gpsimd", "d", st, reads=["dqt"], writes=[tag], chan="dag")
          dq_emitters.append(_batch)


    dequant(wu_sh_h, n_ic, cfg.SH_E, 4,
            lambda ec, b0, bt: agu_in.ap()[ec, b0:b0 + bt, :, :].rearrange("t p c -> p t c"), "agu")
    dequant(wg_sh_h, n_ic, cfg.SH_E, 4,
            lambda ec, b0, bt: agg_in.ap()[ec, b0:b0 + bt, :, :].rearrange("t p c -> p t c"), "agg")
    dequant(wd_sh_h, n_ec, cfg.SH_I, 1,
            lambda ec, b0, bt: agd_in.ap()[ec, :, b0 * P:(b0 + bt) * P], "agd")


    def _ags():
        I_("gpsimd", "cc", lambda e: e.collective_compute(
            "AllGather", ALU.bypass, replica_groups=[list(range(n_cores))],
            ins=[agu_in.ap().opt()], outs=[wu_g.ap().opt()]), reads=["agu"], writes=["wug"])
        I_("gpsimd", "cc", lambda e: e.collective_compute(
            "AllGather", ALU.bypass, replica_groups=[list(range(n_cores))],
            ins=[agg_in.ap().opt()], outs=[wg_g.ap().opt()]), reads=["agg"], writes=["wgg"])
        I_("gpsimd", "cc", lambda e: e.collective_compute(
            "AllGather", ALU.bypass, replica_groups=[list(range(n_cores))],
            ins=[agd_in.ap().opt()], outs=[wd_g.ap().opt()]), reads=["agd"], writes=["wdg"])
    dq_emitters.append(_ags)

    att = ExitStack()
    AA = att.enter_context
    sb_stage = AA(nc.sbuf_tensor("stage", [P, T], F32))
    sb_pb = [AA(nc.sbuf_tensor(f"pb{i}", [P, T], BF16)) for i in range(2)]
    sb_pt = [AA(nc.sbuf_tensor(f"pt{i}", [P, max(P, (n_tt - 1) * P)], BF16)) for i in range(2)]
    sb_ksl = [AA(nc.sbuf_tensor(f"ksl{i}", [P, n_ec * P], BF16)) for i in range(2)]
    sb_xres = [AA(nc.sbuf_tensor(f"xres{i}", [P, E], F32)) for i in range(2)]
    sb_xm = AA(nc.sbuf_tensor("xmt", [P, E], F32))
    sb_junk = AA(nc.sbuf_tensor("junk", [P, E], F32))
    sb_zst = AA(nc.sbuf_tensor("zst", [P, 16], F32))
    sb_bn2 = AA(nc.sbuf_tensor("bn2", [P, 6 * n_ln + 8], F32))
    B.set_barrier(P_QKV)

    def strip_chunks(L):
        out = [(0, P)]
        off = P
        while off < L:
            w = min(NW, L - off)
            out.append((off, w))
            off += w
        return out

    def emit_av_epi(J):
        kb = J % 2
        avkeys, av_ps = ((["psE0", "psE1"], ps_e) if J % 2 == 0 else (["psF0", "psF1"], ps_f))
        vkeys = [("v", Jp) for Jp in range(J + 1)]

        def av(e, J=J, kb=kb, ps=av_ps):
            for Jp in range(J + 1):
                lhs = sb_pt[kb][:, Jp * P:(Jp + 1) * P] if Jp < J else sb_pb[kb][:, 0:P]
                for h in range((E + NW - 1) // NW):
                    w = min(NW, E - h * NW)
                    ii = e.matmul(ps[:, h * NW:h * NW + w], lhs,
                                  sb_v[:, Jp * E + h * NW: Jp * E + h * NW + w],
                                  start=(Jp == 0), stop=(Jp == J))
            return ii
        I_("tensor", "c", av, reads=[("pt", kb), ("pb", kb)] + vkeys, writes=avkeys)

        def bns2(e, ps=av_ps):
            for c in range(n_ln):
                w = min(512, E - c * 512)
                ii = e.bn_stats(sb_bn2[:, 6 * c:6 * c + 6], ps[:, c * 512:c * 512 + w])
            return ii
        I_("vector", "c", bns2, reads=avkeys, writes=["bn2"])
        I_("vector", "c", lambda e: e.bn_aggr(sb_bn2[:, 6 * n_ln:6 * n_ln + 2], sb_bn2[:, 0:6 * n_ln]),
           reads=["bn2"], writes=["bn2"])
        I_("scalar", "c", lambda e: e.activation(sb_bn2[:, 6 * n_ln + 2:6 * n_ln + 3],
                                                 sb_bn2[:, 6 * n_ln + 1:6 * n_ln + 2],
                                                 AF.Ln, bias=sb_kc[:, 2:3], scale=sb_kc[:, 6:7]),
           reads=["bn2"], writes=["bn2"])
        I_("scalar", "c", lambda e: e.activation(sb_bn2[:, 6 * n_ln + 3:6 * n_ln + 4],
                                                 sb_bn2[:, 6 * n_ln + 2:6 * n_ln + 3],
                                                 AF.Exp, bias=sb_kc[:, 5:6], scale=sb_kc[:, 1:2]),
           reads=["bn2"], writes=["bn2"])

        I_("vector", "c", lambda e: e.scalar_tensor_tensor(sb_bn2[:, 6 * n_ln + 7:6 * n_ln + 8],
                                                           sb_bn2[:, 6 * n_ln:6 * n_ln + 1], -1.0,
                                                           sb_bn2[:, 6 * n_ln + 3:6 * n_ln + 4],
                                                           op0=ALU.mult, op1=ALU.mult),
           reads=["bn2"], writes=["bn2"])

        def lnav(e, ps=av_ps):
            return e.activation(sb_junk[:, :], ps[:, 0:E], AF.Identity,
                                bias=sb_bn2[:, 6 * n_ln + 7:6 * n_ln + 8],
                                scale=sb_bn2[:, 6 * n_ln + 3:6 * n_ln + 4])
        I_("scalar", "c", lnav, reads=avkeys + ["bn2"], writes=["junk"])
        I_("vector", "c", lambda e: e.tensor_tensor(sb_xm[:, :], sb_junk[:, :], sb_xres[kb][:, :], op=ALU.add),
           reads=["junk", ("xres", kb)], writes=["xm"])

        def bns3(e):
            for c in range(n_ln):
                w = min(512, E - c * 512)
                ii = e.bn_stats(sb_bn2[:, 6 * c:6 * c + 6], sb_xm[:, c * 512:c * 512 + w])
            return ii
        I_("vector", "c", bns3, reads=["xm"], writes=["bn2"])
        I_("vector", "c", lambda e: e.bn_aggr(sb_bn2[:, 6 * n_ln + 4:6 * n_ln + 6], sb_bn2[:, 0:6 * n_ln]),
           reads=["bn2"], writes=["bn2"])
        I_("vector", "c", lambda e, J=J: e.tensor_copy(sb_m2[:, J:J + 1],
                                                       sb_bn2[:, 6 * n_ln + 4:6 * n_ln + 5]),
           reads=["bn2"], writes=["m2"])
        I_("scalar", "c", lambda e: e.activation(sb_bn2[:, 6 * n_ln + 6:6 * n_ln + 7],
                                                 sb_bn2[:, 6 * n_ln + 5:6 * n_ln + 6],
                                                 AF.Ln, bias=sb_kc[:, 2:3], scale=sb_kc[:, 6:7]),
           reads=["bn2"], writes=["bn2"])
        I_("scalar", "c", lambda e, J=J: e.activation(sb_rs2[:, J:J + 1],
                                                      sb_bn2[:, 6 * n_ln + 6:6 * n_ln + 7],
                                                      AF.Exp, bias=sb_kc[:, 5:6], scale=sb_kc[:, 1:2]),
           reads=["bn2"], writes=["rs2"])

        def cand(e, J=J):
            return e.tensor_scalar(sb_junk[:, :], sb_xm[:, :], sb_m2[:, J:J + 1], None,
                                   op0=ALU.subtract)
        I_("vector", "c", cand, reads=["xm", "m2"], writes=["junk"])
        I_("vector", "c", lambda e, J=J: e.reduce_max(sb_cand[:, J:J + 1], sb_junk[:, :],
                                                      axis=AX.X, apply_absolute_value=True),
           reads=["junk"], writes=["cand"])
        I_("vector", "c", lambda e, J=J: e.tensor_scalar(sb_cand[:, J:J + 1], sb_cand[:, J:J + 1],
                                                         sb_rs2[:, J:J + 1], None, op0=ALU.mult),
           reads=["cand", "rs2"], writes=["cand"])
        I_("sync", "d", lambda e, J=J: e.dma_start(out=xm_scr.ap()[J * P:(J + 1) * P, :], in_=sb_xm[:, :]),
           reads=["xm"], writes=["xmscr"], chan="dxm")

    for J in range(n_tt):
        L = T - J * P
        chunks = strip_chunks(L)
        kb = J % 2

        if J < 2:
            def ldk(e, J=J, kb=kb):
                return e.dma_start(out=sb_ksl[kb][:, :].rearrange("p (c j) -> p c j", j=P),
                                   in_=k_scr.ap()[J, :, :, :].rearrange("c p j -> p c j"))
            I_("sync", "d", ldk, reads=["kscr"], writes=[("ksl", kb)], chan=f"ksl{kb}")

        if J > 0:
            def ldpt(e, J=J):
                return e.dma_start(out=sb_pt[J % 2][:, 0:J * P].rearrange("p (t c) -> p t c", c=P),
                                   in_=p_scr.ap()[0:J, J, :, :].rearrange("t p c -> p t c"))
            I_("sync", "d", ldpt, reads=["pscr"], writes=[("pt", J % 2)], chan=f"pt{J % 2}")

        def ldxr(e, J=J):
            return e.dma_start(out=sb_xres[J % 2][:, :], in_=x_h.ap()[J * P:(J + 1) * P, :])
        I_("sync", "d", ldxr, writes=[("xres", J % 2)], chan=f"xres{J % 2}")

        for ci, (off, w) in enumerate(chunks):
            bank, ps = (["psA", ps_a] if ci % 2 == 0 else ["psB", ps_b])

            def smm(e, J=J, kb=kb, off=off, w=w, ps=ps):
                for c in range(n_ec):
                    ii = e.matmul(ps[:, 0:w], sb_ksl[kb][:, c * P:(c + 1) * P],
                                  sb_qT[:, c * T + J * P + off: c * T + J * P + off + w],
                                  start=(c == 0), stop=(c == n_ec - 1))
                return ii
            I_("tensor", "c", smm, reads=[("ksl", kb), "qT"], writes=[bank])

            if ci == 0:
                def sev(e, off=off, w=w, ps=ps):
                    return e.scalar_tensor_tensor(sb_stage[:, off:off + w], ps[:, 0:w],
                                                  1.0, sb_mask[:, :], op0=ALU.mult, op1=ALU.add)
                I_("vector", "c", sev, reads=[bank, "mask"], writes=["stage"])
            else:
                def sev(e, off=off, w=w, ps=ps):
                    return e.tensor_copy(sb_stage[:, off:off + w], ps[:, 0:w])
                I_("vector", "c", sev, reads=[bank], writes=["stage"])

        I_("vector", "c", lambda e, L=L: e.reduce_max(sb_zst[:, 14:15], sb_stage[:, 0:L],
                                                      axis=AX.X, negate=True),
           reads=["stage"], writes=["zst"])

        def expf(e, kb=kb, L=L):
            return e.activation(sb_pb[kb][:, 0:L], sb_stage[:, 0:L], AF.Exp,
                                bias=sb_zst[:, 14:15], scale=1.0,
                                accum_out=sb_zst[:, 15:16])
        I_("scalar", "c", expf, reads=["stage", "zst"], writes=[("pb", kb), "zst"])

        I_("vector", "c", lambda e: e.reciprocal(sb_zst[:, 13:14], sb_zst[:, 15:16]),
           reads=["zst"], writes=["zst"])
        I_("vector", "c", lambda e, J=J: e.tensor_scalar(
            sb_v[:, J * E:(J + 1) * E], sb_v[:, J * E:(J + 1) * E],
            sb_zst[:, 13:14], None, op0=ALU.mult),
           reads=[("v", J), "zst"], writes=[("v", J)])

        def pwr(e, J=J, kb=kb, L=L):
            return e.dma_start(out=p_scr.ap()[J, J:n_tt, :, :].rearrange("t p c -> p t c"),
                               in_=sb_pb[kb][:, 0:L].rearrange("p (t c) -> p t c", c=P))
        I_("sync", "d", pwr, reads=[("pb", kb)], writes=["pscr"], chan="dps")

        if J + 2 < n_tt:
            kb2 = (J + 2) % 2

            def ldk2(e, J=J, kb2=kb2):
                return e.dma_start(out=sb_ksl[kb2][:, :].rearrange("p (c j) -> p c j", j=P),
                                   in_=k_scr.ap()[J + 2, :, :, :].rearrange("c p j -> p c j"))
            I_("sync", "d", ldk2, reads=["kscr"], writes=[("ksl", kb2)], chan=f"ksl{kb2}")

        if J > 0:
            emit_av_epi(J - 1)
        if dq_emitters:
            dq_emitters.pop(0)()

    while dq_emitters:
        dq_emitters.pop(0)()
    emit_av_epi(n_tt - 1)

    P_ATT = B.snap()
    att.close()
    attspan.close()

    def gmax_pack(src_ap, src_keys, col, neg_ap=None, neg_keys=()):
        I_("vector", "c", lambda e: e.reduce_max(sb_sm[:, 0:1], src_ap, axis=AX.X),
           reads=src_keys, writes=["sm"])
        if neg_ap is not None:
            I_("vector", "c", lambda e: e.tensor_reduce(sb_sm[:, 1:2], neg_ap, axis=AX.X,
                                                        op=ALU.min, negate=True),
               reads=list(neg_keys), writes=["sm"])
            I_("vector", "c", lambda e: e.tensor_tensor(sb_sm[:, 0:1], sb_sm[:, 0:1],
                                                        sb_sm[:, 1:2], op=ALU.max),
               reads=["sm"], writes=["sm"])
        I_("tensor", "c", lambda e: e.transpose(ps_h[0:1, 0:P], sb_sm[:, 0:1], sb_idf[:, :]),
           reads=["sm", "idf"], writes=["psH"])
        I_("vector", "c", lambda e: e.tensor_copy(sb_smr[:, :], ps_h[0:1, 0:P]),
           reads=["psH"], writes=["smr"])
        I_("vector", "c", lambda e, col=col: e.reduce_max(sb_sc[:, col:col + 1], sb_smr[:, :], axis=AX.X),
           reads=["smr"], writes=["sc"])

    def allreduce_bcast(idx, gcol):
        I_("gpsimd", "d", lambda e, idx=idx: e.dma_start(out=cc_in[idx].ap(), in_=sb_sc[:, :]),
           reads=["sc"], writes=[("ccin", idx)], chan="ccio")
        I_("gpsimd", "cc", lambda e, idx=idx: e.collective_compute(
            "AllReduce", ALU.max, replica_groups=[list(range(n_cores))],
            ins=[cc_in[idx].ap().opt()], outs=[cc_out[idx].ap().opt()]),
           reads=[("ccin", idx)], writes=[("ccout", idx)])
        I_("gpsimd", "d", lambda e, idx=idx: e.dma_start(out=sb_sc[:, :], in_=cc_out[idx].ap()),
           reads=[("ccout", idx)], writes=["sc"], chan="ccio")
        I_("tensor", "c", lambda e: e.matmul(ps_h[:, 0:2], sb_ones[:, :], sb_sc[:, :], start=True, stop=True),
           reads=["sc", "ones"], writes=["psH"])
        I_("vector", "c", lambda e, gcol=gcol: e.tensor_copy(sb_gv[:, gcol:gcol + 2], ps_h[:, 0:2]),
           reads=["psH"], writes=["gv"])

    def gvts(dst, src, s1, s2, o1, o2):
        if s2 is None:
            I_("vector", "c", lambda e: e.tensor_scalar(sb_gv[:, dst:dst + 1],
                                                        sb_gv[:, src:src + 1], s1, None, op0=o1),
               reads=["gv"], writes=["gv"])
        else:
            I_("vector", "c", lambda e: e.tensor_scalar(sb_gv[:, dst:dst + 1],
                                                        sb_gv[:, src:src + 1], s1, s2, op0=o1, op1=o2),
               reads=["gv"], writes=["gv"])

    def gvtt(dst, a, b):
        I_("vector", "c", lambda e: e.tensor_tensor(sb_gv[:, dst:dst + 1], sb_gv[:, a:a + 1],
                                                    sb_gv[:, b:b + 1], op=ALU.mult),
           reads=["gv"], writes=["gv"])

    def gvrec(dst, src):
        I_("vector", "c", lambda e: e.reciprocal(sb_gv[:, dst:dst + 1], sb_gv[:, src:src + 1]),
           reads=["gv"], writes=["gv"])

    gmax_pack(sb_cand[:, 0:n_tt], ["cand"], 0)
    allreduce_bcast(0, 0)
    gvts(2, 0, 1.0 / QMAX, 1e-12, ALU.mult, ALU.max)
    gvrec(3, 2)
    gvts(28, 2, -MAGIC, None, ALU.mult, ALU.mult)
    I_("vector", "c", lambda e: e.tensor_scalar(sb_rs2r1[:, :], sb_rs2[:, :], sb_gv[:, 3:4], None,
                                                op0=ALU.mult),
       reads=["rs2", "gv"], writes=["rs2r1"])

    f2 = ExitStack()
    F = f2.enter_context
    sb_xqT = F(nc.sbuf_tensor("xqT", [P, n_ec * T], BF16))
    sb_xmr = [F(nc.sbuf_tensor(f"xmr{i}", [P, E], F32)) for i in range(4)]
    sb_xq = [F(nc.sbuf_tensor(f"xq{i}", [P, E], F32)) for i in range(2)]
    sb_xqb = [F(nc.sbuf_tensor(f"xqb{i}", [P, E], BF16)) for i in range(2)]
    sb_bias2 = F(nc.sbuf_tensor("bias2", [P, n_tt], F32))
    sb_wfu = [F(nc.sbuf_tensor(f"wfu{i}", [P, n_ec * P], BF16)) for i in range(3)]
    sb_wfg = [F(nc.sbuf_tensor(f"wfg{i}", [P, n_ec * P], BF16)) for i in range(3)]
    sb_ugst = [F(nc.sbuf_tensor(f"ugst{i}", [P, 1024], F32)) for i in range(8)]
    n_mx = n_ic * (T // NW)
    sb_um = F(nc.sbuf_tensor("um", [P, n_mx], F32))
    sb_gm = F(nc.sbuf_tensor("gm", [P, n_mx], F32))
    B.set_barrier(P_ATT)

    I_("vector", "c", lambda e: e.tensor_tensor(sb_bias2[:, :], sb_m2[:, :], sb_rs2r1[:, :],
                                                op=ALU.mult),
       reads=["m2", "rs2r1"], writes=["bias2"])
    I_("vector", "c", lambda e: e.tensor_scalar(sb_bias2[:, :], sb_bias2[:, :], -1.0, MAGIC,
                                                op0=ALU.mult, op1=ALU.add),
       reads=["bias2"], writes=["bias2"])

    def chain(tt):
        xb = tt % 4
        qb = tt % 2
        if tt < 4:
            I_("sync", "d", lambda e, tt=tt, xb=xb: e.dma_start(
                out=sb_xmr[xb][:, :], in_=xm_scr.ap()[tt * P:(tt + 1) * P, :]),
               reads=["xmscr"], writes=[("xmr", xb)], chan=f"xmr{xb}")
        I_("scalar", "c", lambda e, tt=tt, xb=xb, qb=qb: e.activation(
            sb_xq[qb][:, :], sb_xmr[xb][:, :], AF.Identity,
            bias=sb_bias2[:, tt:tt + 1], scale=sb_rs2r1[:, tt:tt + 1]),
           reads=[("xmr", xb), "bias2", "rs2r1"], writes=[("xq", qb)])
        if tt + 4 < n_tt:
            I_("scalar", "d", lambda e, tt=tt, xb=xb: e.dma_start(
                out=sb_xmr[xb][:, :], in_=xm_scr.ap()[(tt + 4) * P:(tt + 5) * P, :]),
               reads=["xmscr"], writes=[("xmr", xb)], chan=f"xmr{xb}")
        I_("scalar", "c", lambda e, qb=qb: e.activation(
            sb_xqb[qb][:, :], sb_xq[qb][:, :], AF.Identity,
            bias=sb_gv[:, 28:29], scale=sb_gv[:, 2:3]),
           reads=[("xq", qb), "gv"], writes=[("xqb", qb)])

        def xtp(e, qb=qb):
            for c in range(n_ec):
                ii = e.transpose(ps_g[:, c * P:(c + 1) * P], sb_xqb[qb][:, c * P:(c + 1) * P],
                                 sb_idb[:, :])
            return ii
        I_("tensor", "c", xtp, reads=[("xqb", qb), "idb"], writes=["psG"])

        def xte(e, tt=tt):
            for c in range(n_ec):
                ii = e.tensor_copy(sb_xqT[:, c * T + tt * P: c * T + (tt + 1) * P],
                                   ps_g[:, c * P:(c + 1) * P])
            return ii
        I_("vector", "c", xte, reads=["psG"], writes=[("xqT", tt)])

    def gsk(it, off, w):
        return [("gscr", it, (off + k * 512) // 512) for k in range((w + 511) // 512)]

    SW2 = min(1024, T)
    n_h = T // NW
    n_grp = max(1, n_h // 4)
    nh4 = min(4, n_h)
    tpg = nh4 * (NW // P)
    halfgroups = [
        [("psA", ps_a[:, :]), ("psB", ps_b[:, :]),
         ("psE0", ps_e[:, 0:512]), ("psE1", ps_e[:, 512:1024])],
        [("psF0", ps_f[:, 0:512]), ("psF1", ps_f[:, 512:1024]),
         ("psG", ps_g[:, :].bitcast(F32)), ("psH", ps_h[:, :])],
    ]
    tis = ((sb_wfu, "wfu", sb_um, "um", u_scr, "uscr"),
           (sb_wfg, "wfg", sb_gm, "gm", g_scr, "gscr"))

    def emit_w(it):
        wb = it % 3
        I_("gpsimd", "d", lambda e, it=it, wb=wb: e.dma_start(
            out=sb_wfu[wb][:, :].rearrange("p (c f) -> p c f", f=P),
            in_=wu_g.ap()[:, it, :, :].rearrange("c p f -> p c f")),
           reads=["wug"], writes=[("wfu", wb)], chan=f"wfu{wb}")
        I_("gpsimd", "d", lambda e, it=it, wb=wb: e.dma_start(
            out=sb_wfg[wb][:, :].rearrange("p (c f) -> p c f", f=P),
            in_=wg_g.ap()[:, it, :, :].rearrange("c p f -> p c f")),
           reads=["wgg"], writes=[("wfg", wb)], chan=f"wfg{wb}")

    def group(it, ti, half, hs=None):
        wb = it % 3
        wf, wkn, mx, mxkey, scr, stag = tis[ti]
        wkey = (wkn, wb)
        grp = halfgroups[half % 2][:nh4]
        if hs is None:
            hs = list(range(nh4))
        gkeys = [grp[h4][0] for h4 in hs]
        tkeys = []
        for h4 in hs:
            h = half * nh4 + h4
            tkeys += [("xqT", t) for t in range(h * (NW // P), min((h + 1) * (NW // P), n_tt))]

        def umm(e, wf=wf, wb=wb, half=half, hs=tuple(hs), grp=tuple(grp)):
            for c in range(n_ec):
                for h4 in hs:
                    h = half * nh4 + h4
                    ii = e.matmul(grp[h4][1][:, :], wf[wb][:, c * P:(c + 1) * P],
                                  sb_xqT[:, c * T + h * NW: c * T + (h + 1) * NW],
                                  start=(c == 0), stop=(c == n_ec - 1))
            return ii
        I_("tensor", "c", umm, reads=[wkey] + tkeys, writes=gkeys)

        hi = 0
        while hi < len(hs):
            h4 = hs[hi]
            h = half * nh4 + h4
            hh = h // (SW2 // NW)
            h2 = h % (SW2 // NW)
            sbuf_i = ti * 4 + hh
            gk, gps = grp[h4]
            merged = False
            if (hi + 1 < len(hs)) and hs[hi + 1] == h4 + 1 and h2 == 0 and (gk[:3] in ("psE", "psF")):
                gk2 = grp[h4 + 1][0]
                big = ps_e if gk[:3] == "psE" else ps_f

                def uev2(e, big=big, sbuf_i=sbuf_i):
                    return e.tensor_copy(sb_ugst[sbuf_i][:, 0:2 * NW], big[:, 0:2 * NW])
                I_("vector", "c", uev2, reads=[gk, gk2], writes=[("ugst", sbuf_i)])
                merged = True
            else:
                def uev(e, gps=gps, sbuf_i=sbuf_i, h2=h2):
                    return e.tensor_copy(sb_ugst[sbuf_i][:, h2 * NW:(h2 + 1) * NW], gps[:, :])
                I_("vector", "c", uev, reads=[gk], writes=[("ugst", sbuf_i)])
            reached = h2 + (1 if merged else 0)
            if reached == (SW2 // NW) - 1:
                slot = it * (T // SW2) + hh

                def umax(e, sbuf_i=sbuf_i, mx=mx, slot=slot, SW2=SW2):
                    return e.reduce_max(mx[:, slot:slot + 1],
                                        sb_ugst[sbuf_i][:, 0:SW2],
                                        axis=AX.X, apply_absolute_value=True)
                I_("vector", "c", umax, reads=[("ugst", sbuf_i)], writes=[mxkey])

                def ust(e, it=it, hh=hh, sbuf_i=sbuf_i, scr=scr, SW2=SW2):
                    return e.dma_start(out=scr.ap()[it, :, hh * SW2:(hh + 1) * SW2],
                                       in_=sb_ugst[sbuf_i][:, 0:SW2])
                wkeys = [stag] if stag != "gscr" else gsk(it, hh * SW2, SW2)
                I_("sync", "d", ust, reads=[("ugst", sbuf_i)], writes=wkeys, chan="dug")
            hi += 2 if merged else 1

    if n_grp > 1:
        emit_w(0)
        emit_w(1)
        for j in range(nh4):
            for tt in range(j * (NW // P), (j + 1) * (NW // P)):
                chain(tt)
            group(0, 0, 0, hs=[j])
        group(0, 1, 0)
        for tt in range(tpg, tpg + 4):
            chain(tt)
        group(1, 0, 0)
        for tt in range(tpg + 4, tpg + 8):
            chain(tt)
        group(1, 1, 0)
        for tt in range(tpg + 8, tpg + 12):
            chain(tt)
        emit_w(2)
        group(2, 0, 0)
        for tt in range(tpg + 12, n_tt):
            chain(tt)
        group(2, 1, 0)
        for it in (0, 1, 2):
            for ti in range(2):
                group(it, ti, 1)
        start_it = 3
    else:
        for tt in range(n_tt):
            chain(tt)
        start_it = 0
    for it in range(start_it, n_ic):
        emit_w(it)
        for ti in range(2):
            for half in range(n_grp):
                group(it, ti, half)

    gmax_pack(sb_um[:, 0:n_ic * (T // min(1024, T))], ["um"], 0)
    gmax_pack(sb_gm[:, 0:n_ic * (T // min(1024, T))], ["gm"], 1)
    allreduce_bcast(1, 4)
    gvts(6, 4, 1.0 / QMAX, 1e-12, ALU.mult, ALU.max)
    gvrec(7, 6)
    gvts(8, 5, 1.0 / QMAX, 1e-12, ALU.mult, ALU.max)
    gvrec(9, 8)
    gvts(10, 8, -MAGIC, None, ALU.mult, ALU.mult)
    gvts(11, 8, 1.0 / FIXQ, None, ALU.mult, ALU.mult)
    P_F2 = B.snap()
    f2.close()

    W3 = cfg.W3
    n_w3 = T // W3
    f34 = ExitStack()
    F3 = f34.enter_context
    sb_gin = [F3(nc.sbuf_tensor(f"gin{i}", [P, W3], F32)) for i in range(3)]
    sb_uin = [F3(nc.sbuf_tensor(f"uin{i}", [P, W3], F32)) for i in range(3)]
    sb_t1 = [F3(nc.sbuf_tensor(f"t1_{i}", [P, W3], F32)) for i in range(3)]
    sb_t2 = [F3(nc.sbuf_tensor(f"t2_{i}", [P, W3], F32)) for i in range(3)]
    sb_pr = [F3(nc.sbuf_tensor(f"pr_{i}", [P, W3], F32)) for i in range(3)]
    nch34 = n_ic * n_w3
    sb_pm3 = F3(nc.sbuf_tensor("pm3", [P, nch34], F32))
    sb_pm4 = F3(nc.sbuf_tensor("pm4", [P, nch34], F32))
    B.set_barrier(P_F2)

    def chain_sig(ci, b, with_u):
        I_("scalar", "c", lambda e, b=b: e.activation(sb_t1[b][:, :], sb_gin[b][:, :], AF.Copy,
                                                      bias=MAGIC, scale=sb_gv[:, 9:10]),
           reads=[("gin", b), "gv", "kc"], writes=[("t1", b)])
        I_("scalar", "c", lambda e, b=b: e.activation(sb_t2[b][:, :], sb_t1[b][:, :], AF.Sigmoid,
                                                      bias=sb_gv[:, 10:11], scale=sb_gv[:, 8:9]),
           reads=[("t1", b), "gv"], writes=[("t2", b)])
        I_("scalar", "c", lambda e, b=b: e.activation(sb_t2[b][:, :], sb_t2[b][:, :], AF.Copy,
                                                      bias=MAGIC, scale=sb_kc[:, 3:4]),
           reads=[("t2", b), "kc"], writes=[("t2", b)])
        I_("vector", "c", lambda e, b=b: e.tensor_scalar(sb_t2[b][:, :], sb_t2[b][:, :], MAGIC, 65535.0,
                                                         op0=ALU.subtract, op1=ALU.min),
           reads=[("t2", b)], writes=[("t2", b)])
        I_("vector", "c", lambda e, b=b: e.scalar_tensor_tensor(sb_pr[b][:, :], sb_t1[b][:, :], MAGIC,
                                                                sb_t2[b][:, :], op0=ALU.subtract,
                                                                op1=ALU.mult),
           reads=[("t1", b), ("t2", b)], writes=[("pr", b)])
        if with_u:
            I_("scalar", "c", lambda e, b=b: e.activation(sb_t1[b][:, :], sb_pr[b][:, :],
                                                          AF.Identity, bias=MAGIC,
                                                          scale=sb_gv[:, 16:17]),
               reads=[("pr", b), "gv", "kc"], writes=[("t1", b)])
            I_("vector", "c", lambda e, b=b: e.tensor_scalar(sb_t1[b][:, :], sb_t1[b][:, :], MAGIC, None,
                                                             op0=ALU.subtract),
               reads=[("t1", b)], writes=[("t1", b)])
            I_("scalar", "c", lambda e, b=b: e.activation(sb_t2[b][:, :], sb_uin[b][:, :], AF.Copy,
                                                          bias=MAGIC, scale=sb_gv[:, 7:8]),
               reads=[("uin", b), "gv", "kc"], writes=[("t2", b)])
            I_("vector", "c", lambda e, b=b: e.scalar_tensor_tensor(sb_pr[b][:, :], sb_t2[b][:, :], MAGIC,
                                                                    sb_t1[b][:, :], op0=ALU.subtract,
                                                                    op1=ALU.mult),
               reads=[("t1", b), ("t2", b)], writes=[("pr", b)])

    def f34_stageA(ci, b, with_u):
        it, h = divmod(ci, n_w3)
        I_("sync", "d", lambda e, it=it, h=h, b=b: e.dma_start(
            out=sb_gin[b][:, :], in_=g_scr.ap()[it, :, h * W3:(h + 1) * W3]),
           reads=gsk(it, h * W3, W3), writes=[("gin", b)], chan=f"gin{b}")
        if with_u:
            I_("gpsimd", "d", lambda e, it=it, h=h, b=b: e.dma_start(
                out=sb_uin[b][:, :], in_=u_scr.ap()[it, :, h * W3:(h + 1) * W3]),
               reads=["uscr"], writes=[("uin", b)], chan=f"uin{b}")
            I_("scalar", "c", lambda e, b=b: e.activation(sb_uin[b][:, :], sb_uin[b][:, :], AF.Copy,
                                                          bias=MAGIC, scale=sb_gv[:, 7:8]),
               reads=[("uin", b), "gv", "kc"], writes=[("uin", b)])
            return
        I_("scalar", "c", lambda e, b=b: e.activation(sb_t1[b][:, :], sb_gin[b][:, :], AF.Copy,
                                                      bias=MAGIC, scale=sb_gv[:, 9:10]),
           reads=[("gin", b), "gv", "kc"], writes=[("t1", b)])
        I_("scalar", "c", lambda e, b=b: e.activation(sb_t2[b][:, :], sb_t1[b][:, :], AF.Sigmoid,
                                                      bias=sb_gv[:, 10:11], scale=sb_gv[:, 8:9]),
           reads=[("t1", b), "gv"], writes=[("t2", b)])
        I_("scalar", "c", lambda e, b=b: e.activation(sb_t2[b][:, :], sb_t2[b][:, :], AF.Copy,
                                                      bias=MAGIC, scale=sb_kc[:, 3:4]),
           reads=[("t2", b), "kc"], writes=[("t2", b)])

    def f34_stageB(ci, b, with_u):
        it, h = divmod(ci, n_w3)
        if not with_u:
            I_("vector", "c", lambda e, b=b: e.tensor_scalar(sb_t2[b][:, :], sb_t2[b][:, :], MAGIC,
                                                             65535.0, op0=ALU.subtract, op1=ALU.min),
               reads=[("t2", b)], writes=[("t2", b)])
            I_("vector", "c", lambda e, b=b: e.scalar_tensor_tensor(sb_pr[b][:, :], sb_t1[b][:, :],
                                                                    MAGIC, sb_t2[b][:, :],
                                                                    op0=ALU.subtract, op1=ALU.mult),
               reads=[("t1", b), ("t2", b)], writes=[("pr", b)])
            I_("vector", "c", lambda e, b=b, ci=ci: e.reduce_max(
                sb_pm3[:, ci:ci + 1], sb_pr[b][:, :], axis=AX.X, apply_absolute_value=True),
               reads=[("pr", b)], writes=["pm3"])
            I_("scalar", "d", lambda e, it=it, h=h, b=b: e.dma_start(
                out=g_scr.ap()[it, :, h * W3:(h + 1) * W3], in_=sb_pr[b][:, :]),
               reads=[("pr", b)], writes=gsk(it, h * W3, W3), chan="dp3")
            return
        I_("vector", "c", lambda e, b=b: e.tensor_scalar(sb_t1[b][:, :], sb_gin[b][:, :],
                                                         sb_gv[:, 16:17], MAGIC,
                                                         op0=ALU.mult, op1=ALU.add),
           reads=[("gin", b), "gv"], writes=[("t1", b)])
        I_("vector", "c", lambda e, b=b: e.tensor_scalar(sb_t1[b][:, :], sb_t1[b][:, :], MAGIC, None,
                                                         op0=ALU.subtract),
           reads=[("t1", b)], writes=[("t1", b)])
        I_("vector", "c", lambda e, b=b: e.scalar_tensor_tensor(sb_pr[b][:, :], sb_uin[b][:, :], MAGIC,
                                                                sb_t1[b][:, :], op0=ALU.subtract,
                                                                op1=ALU.mult),
           reads=[("t1", b), ("uin", b)], writes=[("pr", b)])
        I_("vector", "c", lambda e, b=b, ci=ci: e.reduce_max(
            sb_pm4[:, ci:ci + 1], sb_pr[b][:, :], axis=AX.X, apply_absolute_value=True),
           reads=[("pr", b)], writes=["pm4"])
        I_("scalar", "d", lambda e, it=it, h=h, b=b: e.dma_start(
            out=g_scr.ap()[it, :, h * W3:(h + 1) * W3], in_=sb_pr[b][:, :]),
           reads=[("pr", b)], writes=gsk(it, h * W3, W3), chan="dpr")

    for ci in range(nch34):
        f34_stageA(ci, ci % 3, with_u=False)
        if ci >= 1:
            f34_stageB(ci - 1, (ci - 1) % 3, with_u=False)
    f34_stageB(nch34 - 1, (nch34 - 1) % 3, with_u=False)

    gmax_pack(sb_pm3[:, 0:nch34], ["pm3"], 0)
    allreduce_bcast(2, 12)
    gvtt(13, 12, 11)
    gvts(14, 13, 1.0 / QMAX, 1e-12, ALU.mult, ALU.max)
    gvrec(15, 14)
    gvtt(16, 11, 15)

    for ci in range(nch34):
        f34_stageA(ci, ci % 3, with_u=True)
        if ci >= 1:
            f34_stageB(ci - 1, (ci - 1) % 3, with_u=True)
    f34_stageB(nch34 - 1, (nch34 - 1) % 3, with_u=True)

    gmax_pack(sb_pm4[:, 0:nch34], ["pm4"], 0)
    allreduce_bcast(3, 17)
    gvtt(19, 14, 6)
    gvtt(18, 19, 17)
    gvts(20, 18, 1.0 / QMAX, 1e-12, ALU.mult, ALU.max)
    gvrec(21, 20)
    gvtt(22, 19, 21)
    gvts(23, 20, -MAGIC, None, ALU.mult, ALU.mult)
    P_F4 = B.snap()
    f34.close()

    F5S = cfg.F5S
    n_f5 = T // F5S
    f5 = ExitStack()
    F5 = f5.enter_context
    GW_ = min(1024, T)
    sb_osup = [F5(nc.sbuf_tensor(f"osup{i}", [P, n_ic * GW_], BF16)) for i in range(2)]
    sb_wd = F5(nc.sbuf_tensor("wd", [P, n_ic * E], BF16))
    sb_g5 = [F5(nc.sbuf_tensor(f"g5_{i}", [P, GW_], F32)) for i in range(4)]
    sb_pr5 = [F5(nc.sbuf_tensor(f"pr5_{i}", [P, GW_], F32)) for i in range(2)]
    sb_xm5 = [F5(nc.sbuf_tensor(f"xm5_{i}", [P, E], F32)) for i in range(2)]
    sb_ot = [F5(nc.sbuf_tensor(f"ot{i}", [P, E], F32)) for i in range(2)]
    sb_bn5 = F5(nc.sbuf_tensor("bn5", [P, 6 * n_ln + 8], F32))
    B.set_barrier(P_F4)

    I_("gpsimd", "d", lambda e: e.dma_start(out=sb_wd[:, :].rearrange("p (i e) -> p i e", e=E),
                                            in_=wd_g.ap().rearrange("i p e -> p i e")),
       reads=["wdg"], writes=["wd"], chan="wd")

    dpairs = [
        (["psE0", "psE1"], [(ps_e[:, 0:512], 0), (ps_e[:, 512:1024], 512)]),
        (["psF0", "psF1"], [(ps_f[:, 0:512], 0), (ps_f[:, 512:1024], 512)]),
        (["psA", "psB"], [(ps_a[:, :], 0), (ps_b[:, :], 512)]),
        (["psG", "psH"], [(ps_g[:, :].bitcast(F32), 0), (ps_h[:, :], 512)]),
    ]
    nhalf5 = (E + NW - 1) // NW
    GW = min(1024, T)
    n_g5 = T // GW
    gtiles = GW // P

    def f5_elem(g, it0, it1):
        gb = g % 2
        for it in range(it0, it1):
            b = (g * n_ic + it) % 4
            I_("sync", "d", lambda e, it=it, g=g, b=b: e.dma_start(
                out=sb_g5[b][:, :], in_=g_scr.ap()[it, :, g * GW:(g + 1) * GW]),
               reads=gsk(it, g * GW, GW), writes=[("g5", b)], chan=f"g5{b}")
            I_("scalar", "c", lambda e, b=b: e.activation(sb_pr5[b % 2][:, :], sb_g5[b][:, :],
                                                          AF.Identity, bias=MAGIC,
                                                          scale=sb_gv[:, 22:23]),
               reads=[("g5", b), "gv", "kc"], writes=[("pr5", b % 2)])
            I_("vector", "c", lambda e, b=b, gb=gb, it=it: e.tensor_scalar(
                sb_osup[gb][:, it * GW:(it + 1) * GW], sb_pr5[b % 2][:, :],
                sb_gv[:, 20:21], sb_gv[:, 23:24], op0=ALU.mult, op1=ALU.add),
               reads=[("pr5", b % 2), "gv"], writes=[("osup", gb, it)])

    def f5_mm(g, tsub):
        gb = g % 2
        tt = g * gtiles + tsub
        okeys = [("osup", gb, it) for it in range(n_ic)]
        dkeys, dhalves = dpairs[tt % 4]

        def dmm(e, gb=gb, tsub=tsub, dhalves=tuple(dhalves)):
            for it in range(n_ic):
                for h in range(nhalf5):
                    w = min(NW, E - h * NW)
                    pp, _ = dhalves[h]
                    ii = e.matmul(pp[:, 0:w],
                                  sb_osup[gb][:, it * GW + tsub * P: it * GW + (tsub + 1) * P],
                                  sb_wd[:, it * E + h * NW: it * E + h * NW + w],
                                  start=(it == 0), stop=(it == n_ic - 1))
            return ii
        I_("tensor", "c", dmm, reads=okeys + ["wd"], writes=dkeys)

    def f5_epi(tt):
        dkeys, dhalves = dpairs[tt % 4]
        xb5 = tt % 2
        I_("sync", "d", lambda e, tt=tt, xb5=xb5: e.dma_start(
            out=sb_xm5[xb5][:, :], in_=xm_scr.ap()[tt * P:(tt + 1) * P, :]),
           reads=["xmscr"], writes=[("xm5", xb5)], chan=f"xm5{xb5}")

        def bns5(e, dhalves=tuple(dhalves)):
            for c in range(n_ln):
                w = min(512, E - c * 512)
                pp, off = dhalves[c]
                ii = e.bn_stats(sb_bn5[:, 6 * c:6 * c + 6], pp[:, 0:w])
            return ii
        I_("vector", "c", bns5, reads=dkeys, writes=["bn5"])
        I_("vector", "c", lambda e: e.bn_aggr(sb_bn5[:, 6 * n_ln:6 * n_ln + 2], sb_bn5[:, 0:6 * n_ln]),
           reads=["bn5"], writes=["bn5"])
        I_("scalar", "c", lambda e: e.activation(sb_bn5[:, 6 * n_ln + 2:6 * n_ln + 3],
                                                 sb_bn5[:, 6 * n_ln + 1:6 * n_ln + 2],
                                                 AF.Ln, bias=sb_kc[:, 2:3], scale=sb_kc[:, 6:7]),
           reads=["bn5"], writes=["bn5"])
        I_("scalar", "c", lambda e: e.activation(sb_bn5[:, 6 * n_ln + 3:6 * n_ln + 4],
                                                 sb_bn5[:, 6 * n_ln + 2:6 * n_ln + 3],
                                                 AF.Exp, bias=sb_kc[:, 5:6], scale=sb_kc[:, 1:2]),
           reads=["bn5"], writes=["bn5"])

        def lny(e, xb5=xb5, dhalves=tuple(dhalves)):
            for c in range(n_ln):
                w = min(512, E - c * 512)
                pp, off = dhalves[c]
                ii = e.tensor_scalar(sb_ot[xb5][:, off:off + w], pp[:, 0:w],
                                     sb_bn5[:, 6 * n_ln:6 * n_ln + 1],
                                     sb_bn5[:, 6 * n_ln + 3:6 * n_ln + 4],
                                     op0=ALU.subtract, op1=ALU.mult)
            return ii
        I_("vector", "c", lny, reads=dkeys + ["bn5"], writes=[("ot", xb5)])
        I_("vector", "c", lambda e, xb5=xb5: e.tensor_tensor(sb_ot[xb5][:, :], sb_ot[xb5][:, :],
                                                             sb_xm5[xb5][:, :], op=ALU.add),
           reads=[("ot", xb5), ("xm5", xb5)], writes=[("ot", xb5)])
        I_("sync", "d", lambda e, tt=tt, xb5=xb5: e.dma_start(
            out=out_h.ap()[tt * P:(tt + 1) * P, :], in_=sb_ot[xb5][:, :]),
           reads=[("ot", xb5)], writes=["outh"], chan="dout")

    f5_elem(0, 0, n_ic)
    for g in range(n_g5):
        for tsub in range(gtiles):
            tt = g * gtiles + tsub
            f5_mm(g, tsub)
            if g + 1 < n_g5:
                f5_elem(g + 1, (n_ic * tsub) // gtiles, (n_ic * (tsub + 1)) // gtiles)
            if tt >= 2:
                f5_epi(tt - 2)
    f5_epi(T // P - 2)
    f5_epi(T // P - 1)

    fin = B.sems["D:dout"][1]
    B.raw("sync", lambda e, fin=fin: e.wait_ge(B.sems["D:dout"][0], fin))

    with nc.Block() as block:
        @block.sync
        def _(e):
            for fn in B.streams["sync"]:
                fn(e)

        @block.tensor
        def _(e):
            for fn in B.streams["tensor"]:
                fn(e)

        @block.vector
        def _(e):
            for fn in B.streams["vector"]:
                fn(e)

        @block.scalar
        def _(e):
            for fn in B.streams["scalar"]:
                fn(e)

        @block.gpsimd
        def _(e):
            for fn in B.streams["gpsimd"]:
                fn(e)

    f5.close()
    es.close()
    return nc


def make_inputs(cfg, X, Wq, bq, Wk, bk, Wv, bv, w_gate, w_up, w_down):
    f32 = np.float32
    wqT = np.ascontiguousarray(np.asarray(Wq, f32).T)
    wkT = np.ascontiguousarray(np.asarray(Wk, f32).T)
    wvT = np.ascontiguousarray(np.asarray(Wv, f32).T)
    bq = np.ascontiguousarray(np.asarray(bq, f32).reshape(cfg.n_ec, P).T)
    bk = np.ascontiguousarray(np.asarray(bk, f32).reshape(cfg.n_ec, P).T)
    bvt = np.ascontiguousarray(np.broadcast_to(np.asarray(bv, f32), (P, cfg.E)))
    mask = np.where(np.arange(P)[None, :] >= np.arange(P)[:, None], 0.0, NEG).astype(f32)
    idb = np.eye(P, dtype=ml_dtypes.bfloat16)
    idf = np.eye(P, dtype=f32)
    w_up = np.asarray(w_up, f32)
    w_gate = np.asarray(w_gate, f32)
    w_down = np.asarray(w_down, f32)
    X = np.asarray(X, f32)
    in_maps = []
    for c in range(cfg.n_cores):
        in_maps.append({
            "x": np.ascontiguousarray(X[c]),
            "wqT": wqT, "wkT": wkT, "wvT": wvT,
            "bq": bq, "bk": bk, "bvt": bvt,
            "wu_sh": np.ascontiguousarray(w_up[:, c * cfg.SH_E:(c + 1) * cfg.SH_E]),
            "wg_sh": np.ascontiguousarray(w_gate[:, c * cfg.SH_E:(c + 1) * cfg.SH_E]),
            "wd_sh": np.ascontiguousarray(w_down[:, c * cfg.SH_I:(c + 1) * cfg.SH_I]),
            "mask": mask, "idb": idb, "idf": idf,
        })
    return in_maps


def kernel(L0_attn_X, L0_attn_Wq, L0_attn_bq, L0_attn_Wk, L0_attn_bk,
           L0_attn_Wv, L0_attn_bv, L0_ffn_w_gate, L0_ffn_w_up, L0_ffn_w_down):
    cfg = Cfg(T=4096, E=1024, I=3072, n_cores=8, attn_scale=1.0 / math.sqrt(1024.0))
    nc = build_nc(cfg)
    in_maps = make_inputs(cfg, L0_attn_X, L0_attn_Wq, L0_attn_bq, L0_attn_Wk,
                          L0_attn_bk, L0_attn_Wv, L0_attn_bv,
                          L0_ffn_w_gate, L0_ffn_w_up, L0_ffn_w_down)
    res = run_bass_kernel_spmd(nc, in_maps, core_ids=list(range(cfg.n_cores)))
    out = np.stack([np.asarray(r["out"], dtype=np.float32) for r in res.results])
    return out


# version 34
# speedup vs baseline: 1.0036x; 1.0036x over previous
import math
from contextlib import ExitStack

import numpy as np
import ml_dtypes

import concourse.bass as bass
import concourse.mybir as mybir
from concourse.bass_utils import run_bass_kernel_spmd

F32 = mybir.dt.float32
BF16 = mybir.dt.bfloat16
AF = mybir.ActivationFunctionType
ALU = mybir.AluOpType
AX = mybir.AxisListType

P = 128
MAGIC = 12582912.0
QMAX = 32767.0
FIXQ = 65536.0
WQMAX = 7.0
WBLK = 32
EPS = 1e-5
NEG = -1.0e30


class Cfg:
    def __init__(self, T=4096, E=1024, I=3072, n_cores=8,
                 attn_scale=1.0 / 32.0):
        self.T, self.E, self.I, self.n_cores = T, E, I, n_cores
        self.attn_scale = attn_scale
        self.n_tt = T // P
        self.n_ec = E // P
        self.n_ic = I // P
        self.NW = 512
        self.SBW = 512
        self.n_sb = T // self.SBW
        self.W3 = min(2048, T)
        self.F5S = min(512, T)
        self.SH_E = E // n_cores
        self.SH_I = I // n_cores
        self.n_ln = (E + 511) // 512
        assert T % 512 == 0 and E % P == 0 and I % P == 0
        assert self.SH_E % P == 0 and self.SH_I % P == 0 and self.SH_E % WBLK == 0


SEMKIND = {
    "sync": {"d": "DSY"},
    "gpsimd": {"d": "DGP", "c": "GP", "cc": "GPC"},
    "tensor": {"c": "TE"},
    "vector": {"c": "V"},
    "scalar": {"c": "SC"},
}


class Builder:

    def __init__(self, nc, es):
        self.nc = nc
        self.es = es
        self.streams = {k: [] for k in ("sync", "tensor", "vector", "scalar", "gpsimd")}
        self.sems = {}
        for name in ("GP", "GPC", "TE", "V", "SC"):
            self._mksem(name)
        self.res = {}
        self.sat = {k: {} for k in self.streams}
        self.barrier = {}

    def _mksem(self, name):
        h = self.es.enter_context(self.nc.semaphore("s_" + name.replace(":", "_")))
        self.sems[name] = [h, 0]
        return self.sems[name]

    def snap(self):
        return {k: v[1] for k, v in self.sems.items()}

    def set_barrier(self, snap):
        self.barrier = dict(snap)

    def ins(self, stream, kind, fn, reads=(), writes=(), chan=None):
        if kind == "d":
            assert chan is not None, "DMA needs a channel"
            semname = "D:" + chan
            sem = self.sems.get(semname) or self._mksem(semname)
            inc = 16
        else:
            semname = SEMKIND[stream][kind]
            sem = self.sems[semname]
            inc = 1
        need = {}

        def add(t):
            if t is not None:
                sn, v = t
                if v > 0:
                    need[sn] = max(need.get(sn, 0), v)

        for k in reads:
            r = self.res.get(k)
            if r is None:
                continue
            add(r[0])
        for k in writes:
            r = self.res.get(k)
            if r is None:
                for sn, v in self.barrier.items():
                    add((sn, v))
                continue
            add(r[0])
            for sn, v in r[1].items():
                add((sn, v))
        sat = self.sat[stream]
        waits = []
        for sn in sorted(need):
            v = need[sn]
            if sn.startswith("D:"):
                v = self.sems[sn][1]
            if v > 0 and sat.get(sn, 0) < v:
                waits.append((self.sems[sn][0], v))
                sat[sn] = v
        sem[1] += inc
        tick = sem[1]

        def wrapped(e, fn=fn, waits=tuple(waits), h=sem[0], inc=inc):
            for s, v in waits:
                e.wait_ge(s, v)
            fn(e).then_inc(h, inc)

        self.streams[stream].append(wrapped)
        done = (semname, tick)
        for k in writes:
            self.res[k] = [done, {}]
        for k in reads:
            r = self.res.setdefault(k, [None, {}])
            r[1][semname] = tick
        return done

    def raw(self, stream, fn):
        self.streams[stream].append(fn)


def build_nc(cfg: Cfg):
    T, E, I = cfg.T, cfg.E, cfg.I
    n_tt, n_ec, n_ic = cfg.n_tt, cfg.n_ec, cfg.n_ic
    NW, SBW, n_sb = cfg.NW, cfg.SBW, cfg.n_sb
    n_cores = cfg.n_cores
    n_ln = cfg.n_ln
    ASC = cfg.attn_scale

    nc = bass.Bass("TRN2", target_bir_lowering=False, debug=False)

    x_h = nc.dram_tensor("x", [T, E], F32, kind="ExternalInput")
    wqT_h = nc.dram_tensor("wqT", [E, E], F32, kind="ExternalInput")
    wkT_h = nc.dram_tensor("wkT", [E, E], F32, kind="ExternalInput")
    wvT_h = nc.dram_tensor("wvT", [E, E], F32, kind="ExternalInput")
    bq_h = nc.dram_tensor("bq", [P, E // P], F32, kind="ExternalInput")
    bk_h = nc.dram_tensor("bk", [P, E // P], F32, kind="ExternalInput")
    bvt_h = nc.dram_tensor("bvt", [P, E], F32, kind="ExternalInput")
    wu_sh_h = nc.dram_tensor("wu_sh", [I, cfg.SH_E], F32, kind="ExternalInput")
    wg_sh_h = nc.dram_tensor("wg_sh", [I, cfg.SH_E], F32, kind="ExternalInput")
    wd_sh_h = nc.dram_tensor("wd_sh", [E, cfg.SH_I], F32, kind="ExternalInput")
    mask_h = nc.dram_tensor("mask", [P, P], F32, kind="ExternalInput")
    idb_h = nc.dram_tensor("idb", [P, P], BF16, kind="ExternalInput")
    idf_h = nc.dram_tensor("idf", [P, P], F32, kind="ExternalInput")
    out_h = nc.dram_tensor("out", [T, E], F32, kind="ExternalOutput")

    wqb_s = nc.dram_tensor("wqb_s", [E // P, E // P, P, P], BF16)
    wkb_s = nc.dram_tensor("wkb_s", [E // P, E // P, P, P], BF16)
    wvb_s = nc.dram_tensor("wvb_s", [E, E], BF16)
    k_scr = nc.dram_tensor("k_scr", [n_tt, n_ec, P, P], BF16)
    p_scr = nc.dram_tensor("p_scr", [n_tt, n_tt, P, P], BF16)
    xm_scr = nc.dram_tensor("xm_scr", [T, E], F32)
    u_scr = nc.dram_tensor("u_scr", [n_ic, P, T], F32)
    g_scr = nc.dram_tensor("g_scr", [n_ic, P, T], F32)
    n_esh = cfg.SH_E // P
    n_ish = cfg.SH_I // P
    agu_in = nc.dram_tensor("agu_in", [n_esh, n_ic, P, P], BF16)
    agg_in = nc.dram_tensor("agg_in", [n_esh, n_ic, P, P], BF16)
    agd_in = nc.dram_tensor("agd_in", [n_ish, P, E], BF16)
    wu_g = nc.dram_tensor("wu_g", [n_cores * n_esh, n_ic, P, P], BF16)
    wg_g = nc.dram_tensor("wg_g", [n_cores * n_esh, n_ic, P, P], BF16)
    wd_g = nc.dram_tensor("wd_g", [n_cores * n_ish, P, E], BF16)
    cc_in = [nc.dram_tensor(f"cc_in{i}", [1, 2], F32) for i in range(4)]
    cc_out = [nc.dram_tensor(f"cc_out{i}", [1, 2], F32) for i in range(4)]

    es = ExitStack()
    A = es.enter_context
    B = Builder(nc, es)
    I_ = B.ins

    sb_mask = A(nc.sbuf_tensor("c_mask", [P, P], F32))
    sb_idb = A(nc.sbuf_tensor("c_idb", [P, P], BF16))
    sb_idf = A(nc.sbuf_tensor("c_idf", [P, P], F32))
    sb_ones = A(nc.sbuf_tensor("c_ones", [1, P], F32))
    sb_bq = A(nc.sbuf_tensor("c_bq", [P, n_ec], F32))
    sb_bk = A(nc.sbuf_tensor("c_bk", [P, n_ec], F32))
    sb_m2 = A(nc.sbuf_tensor("c_m2", [P, n_tt], F32))
    sb_rs2 = A(nc.sbuf_tensor("c_rs2", [P, n_tt], F32))
    sb_rs2r1 = A(nc.sbuf_tensor("c_rs2r1", [P, n_tt], F32))
    sb_cand = A(nc.sbuf_tensor("c_cand", [P, n_tt], F32))
    sb_gv = A(nc.sbuf_tensor("c_gv", [P, 32], F32))
    sb_sc = A(nc.sbuf_tensor("c_sc", [1, 2], F32))
    sb_sm = A(nc.sbuf_tensor("c_sm", [P, 4], F32))
    sb_smr = A(nc.sbuf_tensor("c_smr", [1, P], F32))
    sb_kc = A(nc.sbuf_tensor("c_kc", [P, 8], F32))
    nc.const_aps.register(F32, MAGIC, sb_kc[:, 0:1])
    DQW = max(4 * cfg.SH_E, cfg.SH_I + P)
    sb_dqw = A(nc.sbuf_tensor("dq_w", [P, DQW], F32))
    sb_dqs = A(nc.sbuf_tensor("dq_s", [P, DQW // WBLK], F32))
    sb_dqr = A(nc.sbuf_tensor("dq_r", [P, DQW // WBLK], F32))
    sb_dqq = A(nc.sbuf_tensor("dq_q", [P, DQW], F32))
    sb_dqb = A(nc.sbuf_tensor("dq_qb", [P, DQW], BF16))
    sb_dqt = A(nc.sbuf_tensor("dq_tr", [P, DQW], BF16))

    ps_a = A(nc.psum_tensor("ps_a", [P, 512], F32))
    ps_b = A(nc.psum_tensor("ps_b", [P, 512], F32))
    ps_e = A(nc.psum_tensor("ps_e", [P, 1024], F32))
    ps_f = A(nc.psum_tensor("ps_f", [P, 1024], F32))
    ps_g = A(nc.psum_tensor("ps_g", [P, 1024], BF16))
    ps_h = A(nc.psum_tensor("ps_h", [P, 512], F32))

    attspan = ExitStack()
    sb_qT = attspan.enter_context(nc.sbuf_tensor("qT", [P, n_ec * T], BF16))
    sb_v = attspan.enter_context(nc.sbuf_tensor("v", [P, n_tt * E], BF16))

    I_("sync", "d", lambda e: e.dma_start(out=sb_mask[:, :], in_=mask_h.ap()), writes=["mask"], chan="lcon")
    I_("sync", "d", lambda e: e.dma_start(out=sb_idb[:, :], in_=idb_h.ap()), writes=["idb"], chan="lcon")
    I_("sync", "d", lambda e: e.dma_start(out=sb_idf[:, :], in_=idf_h.ap()), writes=["idf"], chan="lcon")
    I_("sync", "d", lambda e: e.dma_start(out=sb_bq[:, :], in_=bq_h.ap()), writes=["bq"], chan="lcon")
    I_("sync", "d", lambda e: e.dma_start(out=sb_bk[:, :], in_=bk_h.ap()), writes=["bk"], chan="lcon")
    I_("vector", "c", lambda e: e.memset(sb_ones[:, :], 1.0), writes=["ones"])
    I_("vector", "c", lambda e: e.memset(sb_sc[:, :], 0.0), writes=["sc"])
    for _col, _val in ((0, MAGIC), (1, -0.5), (2, EPS), (3, FIXQ), (4, ASC), (5, 0.0), (6, 1.0), (7, 65535.0 + MAGIC)):
        I_("vector", "c", lambda e, c=_col, v=_val: e.memset(sb_kc[:, c:c + 1], v), writes=["kc"])
    I_("vector", "c", lambda e: e.tensor_scalar(sb_bq[:, :], sb_bq[:, :], ASC, None, op0=ALU.mult),
       reads=["bq"], writes=["bq"])
    for _ft in range(E // P):
        I_("gpsimd", "d", lambda e, ft=_ft: e.dma_start(
            out=wqb_s.ap()[ft, :, :, :],
            in_=wqT_h.ap()[:, ft * P:(ft + 1) * P].rearrange("(c p) fc -> c p fc", p=P)),
           writes=[("wqb_s", _ft)], chan="dqb")
        I_("gpsimd", "d", lambda e, ft=_ft: e.dma_start(
            out=wkb_s.ap()[ft, :, :, :],
            in_=wkT_h.ap()[:, ft * P:(ft + 1) * P].rearrange("(c p) fc -> c p fc", p=P)),
           writes=[("wkb_s", _ft)], chan="dqb")
    I_("gpsimd", "d", lambda e: e.dma_start(out=wvb_s.ap(), in_=wvT_h.ap()), writes=["wvb_s"], chan="dqb")

    qkv = ExitStack()
    Q = qkv.enter_context
    sb_xin = [Q(nc.sbuf_tensor(f"xin{i}", [P, E], F32)) for i in range(2)]
    sb_ln1 = [Q(nc.sbuf_tensor(f"ln1_{i}", [P, E], BF16)) for i in range(4)]
    sb_xTb = [Q(nc.sbuf_tensor(f"xTb{i}", [P, n_ec * SBW], BF16)) for i in range(2)]
    sb_bn = [Q(nc.sbuf_tensor(f"bn_t{i}", [P, 6 * n_ln + 4], F32)) for i in range(2)]
    sb_bvt = Q(nc.sbuf_tensor("bvt_t", [P, E], F32))
    sb_wqk = [Q(nc.sbuf_tensor(f"wqk{i}", [P, n_ec * P], BF16)) for i in range(4)]
    sb_wv = [Q(nc.sbuf_tensor(f"wvb{i}", [P, E], BF16)) for i in range(4)]
    sb_kst = [Q(nc.sbuf_tensor(f"kst{i}", [P, NW], BF16)) for i in range(2)]

    I_("sync", "d", lambda e: e.dma_start(out=sb_bvt[:, :], in_=bvt_h.ap()), writes=["bvt"], chan="lbvt")
    nts = SBW // P

    def feeder_ln(sb, tsub):
        if True:
            tt = sb * nts + tsub
            xb = tt % 2
            lb = tt % 4
            bn = sb_bn[tt % 2]

            def ldx(e, tt=tt, xb=xb):
                return e.dma_start(out=sb_xin[xb][:, :], in_=x_h.ap()[tt * P:(tt + 1) * P, :])
            I_("sync", "d", ldx, writes=[("xin", xb)], chan=f"xin{xb}")

            def bns(e, xb=xb, bn=bn):
                for c in range(n_ln):
                    w = min(512, E - c * 512)
                    ii = e.bn_stats(bn[:, 6 * c:6 * c + 6], sb_xin[xb][:, c * 512:c * 512 + w])
                return ii
            I_("vector", "c", bns, reads=[("xin", xb)], writes=[("bn", tt % 2)])
            I_("vector", "c", lambda e, bn=bn: e.bn_aggr(bn[:, 6 * n_ln:6 * n_ln + 2], bn[:, 0:6 * n_ln]),
               reads=[("bn", tt % 2)], writes=[("bn", tt % 2)])
            I_("scalar", "c", lambda e, bn=bn: e.activation(bn[:, 6 * n_ln + 2:6 * n_ln + 3],
                                                            bn[:, 6 * n_ln + 1:6 * n_ln + 2],
                                                            AF.Ln, bias=sb_kc[:, 2:3], scale=sb_kc[:, 6:7]),
               reads=[("bn", tt % 2)], writes=[("bn", tt % 2)])
            I_("scalar", "c", lambda e, bn=bn: e.activation(bn[:, 6 * n_ln + 3:6 * n_ln + 4],
                                                            bn[:, 6 * n_ln + 2:6 * n_ln + 3],
                                                            AF.Exp, bias=sb_kc[:, 5:6], scale=sb_kc[:, 1:2]),
               reads=[("bn", tt % 2)], writes=[("bn", tt % 2)])

            def nrm(e, xb=xb, lb=lb, bn=bn):
                return e.tensor_scalar(sb_ln1[lb][:, :], sb_xin[xb][:, :],
                                       bn[:, 6 * n_ln:6 * n_ln + 1],
                                       bn[:, 6 * n_ln + 3:6 * n_ln + 4],
                                       op0=ALU.subtract, op1=ALU.mult)
            I_("vector", "c", nrm, reads=[("xin", xb), ("bn", tt % 2)], writes=[("ln1", lb)])

    def feeder_tp(sb, tsub):
        sp = sb % 2
        lb = (sb * nts + tsub) % 4
        tkey, tdst = [("psG", ps_g[:, :]),
                      ("psH", ps_h[:, :].bitcast(BF16)),
                      ("psE0", ps_e[:, :].bitcast(BF16)[:, 0:n_ec * P]),
                      ("psF0", ps_f[:, :].bitcast(BF16)[:, 0:n_ec * P])][tsub % 4]

        def tp(e, lb=lb, tdst=tdst):
            for c in range(n_ec):
                ii = e.transpose(tdst[:, c * P:(c + 1) * P], sb_ln1[lb][:, c * P:(c + 1) * P],
                                 sb_idb[:, :])
            return ii
        I_("tensor", "c", tp, reads=[("ln1", lb), "idb"], writes=[tkey])

        def tev(e, tsub=tsub, sp=sp, tdst=tdst):
            for c in range(n_ec):
                ii = e.tensor_copy(sb_xTb[sp][:, c * SBW + tsub * P: c * SBW + (tsub + 1) * P],
                                   tdst[:, c * P:(c + 1) * P])
            return ii
        I_("vector", "c", tev, reads=[tkey], writes=[("xTb", sp, tsub)])

    def feeder(sb):
        for tsub in range(nts):
            feeder_ln(sb, tsub)
        for tsub in range(nts):
            feeder_tp(sb, tsub)

    def qk_section(sb):
        sp = sb % 2
        xTb_keys = [("xTb", sp, i) for i in range(nts)]
        for wsrc, srckey, bias_t, bkey, is_k in (
                (wqb_s, "wqb_s", sb_bq, "bq", False),
                (wkb_s, "wkb_s", sb_bk, "bk", True)):
            for f in range(n_ec):
                wb = f % 4

                def ldw(e, wsrc=wsrc, f=f, wb=wb):
                    return e.dma_start(out=sb_wqk[wb][:, :].rearrange("p (c f) -> p c f", f=P),
                                       in_=wsrc.ap()[f, :, :, :].rearrange("c p f -> p c f"))
                I_("gpsimd", "d", ldw, reads=[(srckey, f)], writes=[("wqk", wb)], chan=f"wqk{wb}")
                bank, ps = (["psA", ps_a] if f % 2 == 0 else ["psB", ps_b])

                def mm(e, wb=wb, ps=ps, sp=sp):
                    for c in range(n_ec):
                        ii = e.matmul(ps[:, 0:SBW], sb_wqk[wb][:, c * P:(c + 1) * P],
                                      sb_xTb[sp][:, c * SBW:(c + 1) * SBW],
                                      start=(c == 0), stop=(c == n_ec - 1))
                    return ii
                I_("tensor", "c", mm, reads=[("wqk", wb)] + xTb_keys, writes=[bank])

                if not is_k:
                    def ev(e, f=f, sb=sb, ps=ps, bias_t=bias_t):
                        return e.activation(sb_qT[:, f * T + sb * SBW: f * T + (sb + 1) * SBW],
                                            ps[:, 0:SBW], AF.Identity,
                                            bias=bias_t[:, f:f + 1], scale=sb_kc[:, 4:5])
                    I_("scalar", "c", ev, reads=[bank, bkey, "kc"], writes=["qT"])
                else:
                    kb = f % 2

                    def ev2(e, ps=ps, bias_t=bias_t, f=f, kb=kb):
                        return e.activation(sb_kst[kb][:, :], ps[:, 0:SBW], AF.Identity,
                                            bias=bias_t[:, f:f + 1], scale=sb_kc[:, 6:7])
                    I_("scalar", "c", ev2, reads=[bank, bkey, "kc"], writes=[("kst", kb)])

                    def kstore(e, sb=sb, f=f, kb=kb):
                        return e.dma_start(
                            out=k_scr.ap()[sb * nts:(sb + 1) * nts, f, :, :].rearrange("J p c -> p J c"),
                            in_=sb_kst[kb][:, :].rearrange("p (J c) -> p J c", c=P))
                    I_("scalar", "d", kstore, reads=[("kst", kb)], writes=["kscr"], chan="dks")

    def v_section(sb):
        sp = sb % 2
        nhalf = (E + NW - 1) // NW
        vunits = [("psA", ps_a[:, :]), ("psB", ps_b[:, :]),
                  ("psE0", ps_e[:, 0:512]), ("psE1", ps_e[:, 512:1024]),
                  ("psF0", ps_f[:, 0:512]), ("psF1", ps_f[:, 512:1024]),
                  ("psG", ps_g[:, :].bitcast(F32)), ("psH", ps_h[:, :])]
        assert nts * nhalf <= len(vunits)
        for ci in range(n_ec):
            wb = ci % 4

            def ldwv(e, ci=ci, wb=wb):
                return e.dma_start(out=sb_wv[wb][:, :], in_=wvb_s.ap()[ci * P:(ci + 1) * P, :])
            I_("gpsimd", "d", ldwv, reads=["wvb_s"], writes=[("wv", wb)], chan=f"wv{wb}")
            for tsub in range(nts):
                for h in range(nhalf):
                    w = min(NW, E - h * NW)
                    uk, ups = vunits[tsub * nhalf + h]

                    def mmv(e, ci=ci, wb=wb, tsub=tsub, h=h, w=w, ups=ups, sp=sp):
                        return e.matmul(ups[:, 0:w],
                                        sb_xTb[sp][:, ci * SBW + tsub * P: ci * SBW + (tsub + 1) * P],
                                        sb_wv[wb][:, h * NW: h * NW + w],
                                        start=(ci == 0), stop=(ci == n_ec - 1))
                    I_("tensor", "c", mmv, reads=[("wv", wb), ("xTb", sp, tsub)], writes=[uk])
                    if ci == n_ec - 1:
                        def evv(e, tt=sb * nts + tsub, h=h, w=w, ups=ups):
                            return e.scalar_tensor_tensor(
                                sb_v[:, tt * E + h * NW: tt * E + h * NW + w], ups[:, 0:w],
                                1.0, sb_bvt[:, h * NW: h * NW + w], op0=ALU.mult, op1=ALU.add)
                        I_("vector", "c", evv, reads=[uk, "bvt"], writes=[("v", sb * nts + tsub)])

    feeder(0)
    for sb in range(n_sb):
        qk_section(sb)
        if sb + 1 < n_sb:
            feeder(sb + 1)
        v_section(sb)

    P_QKV = B.snap()
    qkv.close()

    dq_emitters = []

    def dequant(w_h, n_otiles, ncols, batch_ot, store_ap_fn, tag):
        n_ecs = ncols // P
        for b0 in range(0, n_otiles, batch_ot):
          def _batch(b0=b0, w_h=w_h, n_otiles=n_otiles, ncols=ncols, batch_ot=batch_ot,
                     store_ap_fn=store_ap_fn, tag=tag, n_ecs=n_ecs):
            bt = min(batch_ot, n_otiles - b0)
            w = bt * ncols
            nblk = w // WBLK

            def ld(e, b0=b0, bt=bt, ncols=ncols, w_h=w_h):
                return e.dma_start(
                    out=sb_dqw[:, 0:bt * ncols].rearrange("p (g e) -> p g e", e=ncols),
                    in_=w_h.ap()[b0 * P:(b0 + bt) * P, :].rearrange("(g p) e -> p g e", p=P))
            I_("gpsimd", "d", ld, writes=["dqw"], chan="dqw")

            def v1(e, w=w, nblk=nblk):
                return e.tensor_reduce(sb_dqs[:, 0:nblk],
                                       sb_dqw[:, 0:w].rearrange("p (b k) -> p b k", k=WBLK),
                                       axis=AX.X, op=ALU.max, apply_absolute_value=True)
            I_("vector", "c", v1, reads=["dqw"], writes=["dqs"])
            I_("vector", "c", lambda e, nblk=nblk: e.tensor_scalar(
                sb_dqs[:, 0:nblk], sb_dqs[:, 0:nblk], 1.0 / WQMAX, 1e-12, op0=ALU.mult, op1=ALU.max),
               reads=["dqs"], writes=["dqs"])
            I_("vector", "c", lambda e, nblk=nblk: e.reciprocal(sb_dqr[:, 0:nblk], sb_dqs[:, 0:nblk]),
               reads=["dqs"], writes=["dqr"])

            def v4(e, w=w, nblk=nblk):
                return e.tensor_tensor(sb_dqq[:, 0:w].rearrange("p (b k) -> p b k", k=WBLK),
                                       sb_dqw[:, 0:w].rearrange("p (b k) -> p b k", k=WBLK),
                                       sb_dqr[:, 0:nblk].unsqueeze(2).broadcast_to([P, nblk, WBLK]),
                                       op=ALU.mult)
            I_("vector", "c", v4, reads=["dqw", "dqr"], writes=["dqq"])
            I_("vector", "c", lambda e, w=w: e.tensor_scalar(
                sb_dqq[:, 0:w], sb_dqq[:, 0:w], MAGIC, MAGIC, op0=ALU.add, op1=ALU.subtract),
               reads=["dqq"], writes=["dqq"])

            def v6(e, w=w, nblk=nblk):
                return e.tensor_tensor(sb_dqb[:, 0:w].rearrange("p (b k) -> p b k", k=WBLK),
                                       sb_dqq[:, 0:w].rearrange("p (b k) -> p b k", k=WBLK),
                                       sb_dqs[:, 0:nblk].unsqueeze(2).broadcast_to([P, nblk, WBLK]),
                                       op=ALU.mult)
            I_("vector", "c", v6, reads=["dqq", "dqs"], writes=["dqb"])

            tiles = [(g, ec) for ec in range(n_ecs) for g in range(bt)]
            for t0 in range(0, len(tiles), 8):
                grp = tiles[t0:t0 + 8]

                def tp(e, grp=tuple(grp), ncols=ncols):
                    for si_, (g, ec) in enumerate(grp):
                        ii = e.transpose(ps_g[:, si_ * P:(si_ + 1) * P],
                                         sb_dqb[:, g * ncols + ec * P: g * ncols + (ec + 1) * P],
                                         sb_idb[:, :])
                    return ii
                I_("tensor", "c", tp, reads=["dqb", "idb"], writes=["psG"])

                def ev(e, grp=tuple(grp), t0=t0):
                    return e.tensor_copy(sb_dqt[:, t0 * P:(t0 + len(grp)) * P],
                                         ps_g[:, 0:len(grp) * P])
                I_("vector", "c", ev, reads=["psG"], writes=["dqt"])

            for ec in range(n_ecs):
                def st(e, ec=ec, b0=b0, bt=bt):
                    return e.dma_start(out=store_ap_fn(ec, b0, bt),
                                       in_=sb_dqt[:, ec * bt * P:(ec + 1) * bt * P])
                I_("gpsimd", "d", st, reads=["dqt"], writes=[tag], chan="dag")
          dq_emitters.append(_batch)


    dequant(wu_sh_h, n_ic, cfg.SH_E, 4,
            lambda ec, b0, bt: agu_in.ap()[ec, b0:b0 + bt, :, :].rearrange("t p c -> p t c"), "agu")
    dequant(wg_sh_h, n_ic, cfg.SH_E, 4,
            lambda ec, b0, bt: agg_in.ap()[ec, b0:b0 + bt, :, :].rearrange("t p c -> p t c"), "agg")
    dequant(wd_sh_h, n_ec, cfg.SH_I, 1,
            lambda ec, b0, bt: agd_in.ap()[ec, :, b0 * P:(b0 + bt) * P], "agd")


    def _ags():
        I_("gpsimd", "cc", lambda e: e.collective_compute(
            "AllGather", ALU.bypass, replica_groups=[list(range(n_cores))],
            ins=[agu_in.ap().opt()], outs=[wu_g.ap().opt()]), reads=["agu"], writes=["wug"])
        I_("gpsimd", "cc", lambda e: e.collective_compute(
            "AllGather", ALU.bypass, replica_groups=[list(range(n_cores))],
            ins=[agg_in.ap().opt()], outs=[wg_g.ap().opt()]), reads=["agg"], writes=["wgg"])
        I_("gpsimd", "cc", lambda e: e.collective_compute(
            "AllGather", ALU.bypass, replica_groups=[list(range(n_cores))],
            ins=[agd_in.ap().opt()], outs=[wd_g.ap().opt()]), reads=["agd"], writes=["wdg"])
    dq_emitters.append(_ags)

    att = ExitStack()
    AA = att.enter_context
    sb_stage = AA(nc.sbuf_tensor("stage", [P, T], F32))
    sb_pb = [AA(nc.sbuf_tensor(f"pb{i}", [P, T], BF16)) for i in range(2)]
    sb_pt = [AA(nc.sbuf_tensor(f"pt{i}", [P, max(P, (n_tt - 1) * P)], BF16)) for i in range(2)]
    sb_ksl = [AA(nc.sbuf_tensor(f"ksl{i}", [P, n_ec * P], BF16)) for i in range(2)]
    sb_xres = [AA(nc.sbuf_tensor(f"xres{i}", [P, E], F32)) for i in range(2)]
    sb_xm = AA(nc.sbuf_tensor("xmt", [P, E], F32))
    sb_junk = AA(nc.sbuf_tensor("junk", [P, E], F32))
    sb_zst = AA(nc.sbuf_tensor("zst", [P, 16], F32))
    sb_bn2 = AA(nc.sbuf_tensor("bn2", [P, 6 * n_ln + 8], F32))
    B.set_barrier(P_QKV)

    def strip_chunks(L):
        out = [(0, P)]
        off = P
        while off < L:
            w = min(NW, L - off)
            out.append((off, w))
            off += w
        return out

    def emit_av_epi(J):
        kb = J % 2
        avkeys, av_ps = ((["psE0", "psE1"], ps_e) if J % 2 == 0 else (["psF0", "psF1"], ps_f))
        vkeys = [("v", Jp) for Jp in range(J + 1)]

        def av(e, J=J, kb=kb, ps=av_ps):
            for Jp in range(J + 1):
                lhs = sb_pt[kb][:, Jp * P:(Jp + 1) * P] if Jp < J else sb_pb[kb][:, 0:P]
                for h in range((E + NW - 1) // NW):
                    w = min(NW, E - h * NW)
                    ii = e.matmul(ps[:, h * NW:h * NW + w], lhs,
                                  sb_v[:, Jp * E + h * NW: Jp * E + h * NW + w],
                                  start=(Jp == 0), stop=(Jp == J))
            return ii
        I_("tensor", "c", av, reads=[("pt", kb), ("pb", kb)] + vkeys, writes=avkeys)

        def bns2(e, ps=av_ps):
            for c in range(n_ln):
                w = min(512, E - c * 512)
                ii = e.bn_stats(sb_bn2[:, 6 * c:6 * c + 6], ps[:, c * 512:c * 512 + w])
            return ii
        I_("vector", "c", bns2, reads=avkeys, writes=["bn2"])
        I_("vector", "c", lambda e: e.bn_aggr(sb_bn2[:, 6 * n_ln:6 * n_ln + 2], sb_bn2[:, 0:6 * n_ln]),
           reads=["bn2"], writes=["bn2"])
        I_("scalar", "c", lambda e: e.activation(sb_bn2[:, 6 * n_ln + 2:6 * n_ln + 3],
                                                 sb_bn2[:, 6 * n_ln + 1:6 * n_ln + 2],
                                                 AF.Ln, bias=sb_kc[:, 2:3], scale=sb_kc[:, 6:7]),
           reads=["bn2"], writes=["bn2"])
        I_("scalar", "c", lambda e: e.activation(sb_bn2[:, 6 * n_ln + 3:6 * n_ln + 4],
                                                 sb_bn2[:, 6 * n_ln + 2:6 * n_ln + 3],
                                                 AF.Exp, bias=sb_kc[:, 5:6], scale=sb_kc[:, 1:2]),
           reads=["bn2"], writes=["bn2"])

        I_("vector", "c", lambda e: e.scalar_tensor_tensor(sb_bn2[:, 6 * n_ln + 7:6 * n_ln + 8],
                                                           sb_bn2[:, 6 * n_ln:6 * n_ln + 1], -1.0,
                                                           sb_bn2[:, 6 * n_ln + 3:6 * n_ln + 4],
                                                           op0=ALU.mult, op1=ALU.mult),
           reads=["bn2"], writes=["bn2"])

        def lnav(e, ps=av_ps):
            return e.activation(sb_junk[:, :], ps[:, 0:E], AF.Identity,
                                bias=sb_bn2[:, 6 * n_ln + 7:6 * n_ln + 8],
                                scale=sb_bn2[:, 6 * n_ln + 3:6 * n_ln + 4])
        I_("scalar", "c", lnav, reads=avkeys + ["bn2"], writes=["junk"])
        I_("vector", "c", lambda e: e.tensor_tensor(sb_xm[:, :], sb_junk[:, :], sb_xres[kb][:, :], op=ALU.add),
           reads=["junk", ("xres", kb)], writes=["xm"])

        def bns3(e):
            for c in range(n_ln):
                w = min(512, E - c * 512)
                ii = e.bn_stats(sb_bn2[:, 6 * c:6 * c + 6], sb_xm[:, c * 512:c * 512 + w])
            return ii
        I_("vector", "c", bns3, reads=["xm"], writes=["bn2"])
        I_("vector", "c", lambda e: e.bn_aggr(sb_bn2[:, 6 * n_ln + 4:6 * n_ln + 6], sb_bn2[:, 0:6 * n_ln]),
           reads=["bn2"], writes=["bn2"])
        I_("vector", "c", lambda e, J=J: e.tensor_copy(sb_m2[:, J:J + 1],
                                                       sb_bn2[:, 6 * n_ln + 4:6 * n_ln + 5]),
           reads=["bn2"], writes=["m2"])
        I_("scalar", "c", lambda e: e.activation(sb_bn2[:, 6 * n_ln + 6:6 * n_ln + 7],
                                                 sb_bn2[:, 6 * n_ln + 5:6 * n_ln + 6],
                                                 AF.Ln, bias=sb_kc[:, 2:3], scale=sb_kc[:, 6:7]),
           reads=["bn2"], writes=["bn2"])
        I_("scalar", "c", lambda e, J=J: e.activation(sb_rs2[:, J:J + 1],
                                                      sb_bn2[:, 6 * n_ln + 6:6 * n_ln + 7],
                                                      AF.Exp, bias=sb_kc[:, 5:6], scale=sb_kc[:, 1:2]),
           reads=["bn2"], writes=["rs2"])

        def cand(e, J=J):
            return e.tensor_scalar(sb_junk[:, :], sb_xm[:, :], sb_m2[:, J:J + 1], None,
                                   op0=ALU.subtract)
        I_("vector", "c", cand, reads=["xm", "m2"], writes=["junk"])
        I_("vector", "c", lambda e, J=J: e.reduce_max(sb_cand[:, J:J + 1], sb_junk[:, :],
                                                      axis=AX.X, apply_absolute_value=True),
           reads=["junk"], writes=["cand"])
        I_("vector", "c", lambda e, J=J: e.tensor_scalar(sb_cand[:, J:J + 1], sb_cand[:, J:J + 1],
                                                         sb_rs2[:, J:J + 1], None, op0=ALU.mult),
           reads=["cand", "rs2"], writes=["cand"])
        I_("sync", "d", lambda e, J=J: e.dma_start(out=xm_scr.ap()[J * P:(J + 1) * P, :], in_=sb_xm[:, :]),
           reads=["xm"], writes=["xmscr"], chan="dxm")

    for J in range(n_tt):
        L = T - J * P
        chunks = strip_chunks(L)
        kb = J % 2

        if J < 2:
            def ldk(e, J=J, kb=kb):
                return e.dma_start(out=sb_ksl[kb][:, :].rearrange("p (c j) -> p c j", j=P),
                                   in_=k_scr.ap()[J, :, :, :].rearrange("c p j -> p c j"))
            I_("sync", "d", ldk, reads=["kscr"], writes=[("ksl", kb)], chan=f"ksl{kb}")

        if J > 0:
            def ldpt(e, J=J):
                return e.dma_start(out=sb_pt[J % 2][:, 0:J * P].rearrange("p (t c) -> p t c", c=P),
                                   in_=p_scr.ap()[0:J, J, :, :].rearrange("t p c -> p t c"))
            I_("sync", "d", ldpt, reads=["pscr"], writes=[("pt", J % 2)], chan=f"pt{J % 2}")

        def ldxr(e, J=J):
            return e.dma_start(out=sb_xres[J % 2][:, :], in_=x_h.ap()[J * P:(J + 1) * P, :])
        I_("sync", "d", ldxr, writes=[("xres", J % 2)], chan=f"xres{J % 2}")

        for ci, (off, w) in enumerate(chunks):
            bank, ps = (["psA", ps_a] if ci % 2 == 0 else ["psB", ps_b])

            def smm(e, J=J, kb=kb, off=off, w=w, ps=ps):
                for c in range(n_ec):
                    ii = e.matmul(ps[:, 0:w], sb_ksl[kb][:, c * P:(c + 1) * P],
                                  sb_qT[:, c * T + J * P + off: c * T + J * P + off + w],
                                  start=(c == 0), stop=(c == n_ec - 1))
                return ii
            I_("tensor", "c", smm, reads=[("ksl", kb), "qT"], writes=[bank])

            if ci == 0:
                def sev(e, off=off, w=w, ps=ps):
                    return e.scalar_tensor_tensor(sb_stage[:, off:off + w], ps[:, 0:w],
                                                  1.0, sb_mask[:, :], op0=ALU.mult, op1=ALU.add)
                I_("vector", "c", sev, reads=[bank, "mask"], writes=["stage"])
            else:
                def sev(e, off=off, w=w, ps=ps):
                    return e.tensor_copy(sb_stage[:, off:off + w], ps[:, 0:w])
                I_("vector", "c", sev, reads=[bank], writes=["stage"])

        I_("vector", "c", lambda e, L=L: e.reduce_max(sb_zst[:, 14:15], sb_stage[:, 0:L],
                                                      axis=AX.X, negate=True),
           reads=["stage"], writes=["zst"])

        def expf(e, kb=kb, L=L):
            return e.activation(sb_pb[kb][:, 0:L], sb_stage[:, 0:L], AF.Exp,
                                bias=sb_zst[:, 14:15], scale=1.0,
                                accum_out=sb_zst[:, 15:16])
        I_("scalar", "c", expf, reads=["stage", "zst"], writes=[("pb", kb), "zst"])

        I_("vector", "c", lambda e: e.reciprocal(sb_zst[:, 13:14], sb_zst[:, 15:16]),
           reads=["zst"], writes=["zst"])
        I_("vector", "c", lambda e, J=J: e.tensor_scalar(
            sb_v[:, J * E:(J + 1) * E], sb_v[:, J * E:(J + 1) * E],
            sb_zst[:, 13:14], None, op0=ALU.mult),
           reads=[("v", J), "zst"], writes=[("v", J)])

        def pwr(e, J=J, kb=kb, L=L):
            return e.dma_start(out=p_scr.ap()[J, J:n_tt, :, :].rearrange("t p c -> p t c"),
                               in_=sb_pb[kb][:, 0:L].rearrange("p (t c) -> p t c", c=P))
        I_("sync", "d", pwr, reads=[("pb", kb)], writes=["pscr"], chan="dps")

        if J + 2 < n_tt:
            kb2 = (J + 2) % 2

            def ldk2(e, J=J, kb2=kb2):
                return e.dma_start(out=sb_ksl[kb2][:, :].rearrange("p (c j) -> p c j", j=P),
                                   in_=k_scr.ap()[J + 2, :, :, :].rearrange("c p j -> p c j"))
            I_("sync", "d", ldk2, reads=["kscr"], writes=[("ksl", kb2)], chan=f"ksl{kb2}")

        if J > 0:
            emit_av_epi(J - 1)
        if dq_emitters:
            dq_emitters.pop(0)()

    while dq_emitters:
        dq_emitters.pop(0)()
    emit_av_epi(n_tt - 1)

    P_ATT = B.snap()
    att.close()
    attspan.close()

    def gmax_pack(src_ap, src_keys, col, neg_ap=None, neg_keys=()):
        I_("vector", "c", lambda e: e.reduce_max(sb_sm[:, 0:1], src_ap, axis=AX.X),
           reads=src_keys, writes=["sm"])
        if neg_ap is not None:
            I_("vector", "c", lambda e: e.tensor_reduce(sb_sm[:, 1:2], neg_ap, axis=AX.X,
                                                        op=ALU.min, negate=True),
               reads=list(neg_keys), writes=["sm"])
            I_("vector", "c", lambda e: e.tensor_tensor(sb_sm[:, 0:1], sb_sm[:, 0:1],
                                                        sb_sm[:, 1:2], op=ALU.max),
               reads=["sm"], writes=["sm"])
        I_("tensor", "c", lambda e: e.transpose(ps_h[0:1, 0:P], sb_sm[:, 0:1], sb_idf[:, :]),
           reads=["sm", "idf"], writes=["psH"])
        I_("vector", "c", lambda e: e.tensor_copy(sb_smr[:, :], ps_h[0:1, 0:P]),
           reads=["psH"], writes=["smr"])
        I_("vector", "c", lambda e, col=col: e.reduce_max(sb_sc[:, col:col + 1], sb_smr[:, :], axis=AX.X),
           reads=["smr"], writes=["sc"])

    def allreduce_bcast(idx, gcol):
        I_("gpsimd", "d", lambda e, idx=idx: e.dma_start(out=cc_in[idx].ap(), in_=sb_sc[:, :]),
           reads=["sc"], writes=[("ccin", idx)], chan="ccio")
        I_("gpsimd", "cc", lambda e, idx=idx: e.collective_compute(
            "AllReduce", ALU.max, replica_groups=[list(range(n_cores))],
            ins=[cc_in[idx].ap().opt()], outs=[cc_out[idx].ap().opt()]),
           reads=[("ccin", idx)], writes=[("ccout", idx)])
        I_("gpsimd", "d", lambda e, idx=idx: e.dma_start(out=sb_sc[:, :], in_=cc_out[idx].ap()),
           reads=[("ccout", idx)], writes=["sc"], chan="ccio")
        I_("tensor", "c", lambda e: e.matmul(ps_h[:, 0:2], sb_ones[:, :], sb_sc[:, :], start=True, stop=True),
           reads=["sc", "ones"], writes=["psH"])
        I_("vector", "c", lambda e, gcol=gcol: e.tensor_copy(sb_gv[:, gcol:gcol + 2], ps_h[:, 0:2]),
           reads=["psH"], writes=["gv"])

    def gvts(dst, src, s1, s2, o1, o2):
        if s2 is None:
            I_("vector", "c", lambda e: e.tensor_scalar(sb_gv[:, dst:dst + 1],
                                                        sb_gv[:, src:src + 1], s1, None, op0=o1),
               reads=["gv"], writes=["gv"])
        else:
            I_("vector", "c", lambda e: e.tensor_scalar(sb_gv[:, dst:dst + 1],
                                                        sb_gv[:, src:src + 1], s1, s2, op0=o1, op1=o2),
               reads=["gv"], writes=["gv"])

    def gvtt(dst, a, b):
        I_("vector", "c", lambda e: e.tensor_tensor(sb_gv[:, dst:dst + 1], sb_gv[:, a:a + 1],
                                                    sb_gv[:, b:b + 1], op=ALU.mult),
           reads=["gv"], writes=["gv"])

    def gvrec(dst, src):
        I_("vector", "c", lambda e: e.reciprocal(sb_gv[:, dst:dst + 1], sb_gv[:, src:src + 1]),
           reads=["gv"], writes=["gv"])

    gmax_pack(sb_cand[:, 0:n_tt], ["cand"], 0)
    allreduce_bcast(0, 0)
    gvts(2, 0, 1.0 / QMAX, 1e-12, ALU.mult, ALU.max)
    gvrec(3, 2)
    gvts(28, 2, -MAGIC, None, ALU.mult, ALU.mult)
    I_("vector", "c", lambda e: e.tensor_scalar(sb_rs2r1[:, :], sb_rs2[:, :], sb_gv[:, 3:4], None,
                                                op0=ALU.mult),
       reads=["rs2", "gv"], writes=["rs2r1"])

    f2 = ExitStack()
    F = f2.enter_context
    sb_xqT = F(nc.sbuf_tensor("xqT", [P, n_ec * T], BF16))
    sb_xmr = [F(nc.sbuf_tensor(f"xmr{i}", [P, E], F32)) for i in range(4)]
    sb_xq = [F(nc.sbuf_tensor(f"xq{i}", [P, E], F32)) for i in range(2)]
    sb_xqb = [F(nc.sbuf_tensor(f"xqb{i}", [P, E], BF16)) for i in range(2)]
    sb_bias2 = F(nc.sbuf_tensor("bias2", [P, n_tt], F32))
    sb_wfu = [F(nc.sbuf_tensor(f"wfu{i}", [P, n_ec * P], BF16)) for i in range(3)]
    sb_wfg = [F(nc.sbuf_tensor(f"wfg{i}", [P, n_ec * P], BF16)) for i in range(3)]
    sb_ugst = [F(nc.sbuf_tensor(f"ugst{i}", [P, 1024], F32)) for i in range(8)]
    n_mx = n_ic * (T // NW)
    sb_um = F(nc.sbuf_tensor("um", [P, n_mx], F32))
    sb_gm = F(nc.sbuf_tensor("gm", [P, n_mx], F32))
    B.set_barrier(P_ATT)

    I_("vector", "c", lambda e: e.tensor_tensor(sb_bias2[:, :], sb_m2[:, :], sb_rs2r1[:, :],
                                                op=ALU.mult),
       reads=["m2", "rs2r1"], writes=["bias2"])
    I_("vector", "c", lambda e: e.tensor_scalar(sb_bias2[:, :], sb_bias2[:, :], -1.0, MAGIC,
                                                op0=ALU.mult, op1=ALU.add),
       reads=["bias2"], writes=["bias2"])

    def chain(tt):
        xb = tt % 4
        qb = tt % 2
        if tt < 4:
            I_("sync", "d", lambda e, tt=tt, xb=xb: e.dma_start(
                out=sb_xmr[xb][:, :], in_=xm_scr.ap()[tt * P:(tt + 1) * P, :]),
               reads=["xmscr"], writes=[("xmr", xb)], chan=f"xmr{xb}")
        I_("scalar", "c", lambda e, tt=tt, xb=xb, qb=qb: e.activation(
            sb_xq[qb][:, :], sb_xmr[xb][:, :], AF.Identity,
            bias=sb_bias2[:, tt:tt + 1], scale=sb_rs2r1[:, tt:tt + 1]),
           reads=[("xmr", xb), "bias2", "rs2r1"], writes=[("xq", qb)])
        if tt + 4 < n_tt:
            I_("scalar", "d", lambda e, tt=tt, xb=xb: e.dma_start(
                out=sb_xmr[xb][:, :], in_=xm_scr.ap()[(tt + 4) * P:(tt + 5) * P, :]),
               reads=["xmscr"], writes=[("xmr", xb)], chan=f"xmr{xb}")
        I_("scalar", "c", lambda e, qb=qb: e.activation(
            sb_xqb[qb][:, :], sb_xq[qb][:, :], AF.Identity,
            bias=sb_gv[:, 28:29], scale=sb_gv[:, 2:3]),
           reads=[("xq", qb), "gv"], writes=[("xqb", qb)])

        def xtp(e, qb=qb):
            for c in range(n_ec):
                ii = e.transpose(ps_g[:, c * P:(c + 1) * P], sb_xqb[qb][:, c * P:(c + 1) * P],
                                 sb_idb[:, :])
            return ii
        I_("tensor", "c", xtp, reads=[("xqb", qb), "idb"], writes=["psG"])

        def xte(e, tt=tt):
            for c in range(n_ec):
                ii = e.tensor_copy(sb_xqT[:, c * T + tt * P: c * T + (tt + 1) * P],
                                   ps_g[:, c * P:(c + 1) * P])
            return ii
        I_("vector", "c", xte, reads=["psG"], writes=[("xqT", tt)])

    def gsk(it, off, w):
        return [("gscr", it, (off + k * 512) // 512) for k in range((w + 511) // 512)]

    SW2 = min(1024, T)
    n_h = T // NW
    n_grp = max(1, n_h // 4)
    nh4 = min(4, n_h)
    tpg = nh4 * (NW // P)
    halfgroups = [
        [("psA", ps_a[:, :]), ("psB", ps_b[:, :]),
         ("psE0", ps_e[:, 0:512]), ("psE1", ps_e[:, 512:1024])],
        [("psF0", ps_f[:, 0:512]), ("psF1", ps_f[:, 512:1024]),
         ("psG", ps_g[:, :].bitcast(F32)), ("psH", ps_h[:, :])],
    ]
    tis = ((sb_wfu, "wfu", sb_um, "um", u_scr, "uscr"),
           (sb_wfg, "wfg", sb_gm, "gm", g_scr, "gscr"))

    def emit_w(it):
        wb = it % 3
        I_("gpsimd", "d", lambda e, it=it, wb=wb: e.dma_start(
            out=sb_wfu[wb][:, :].rearrange("p (c f) -> p c f", f=P),
            in_=wu_g.ap()[:, it, :, :].rearrange("c p f -> p c f")),
           reads=["wug"], writes=[("wfu", wb)], chan=f"wfu{wb}")
        I_("gpsimd", "d", lambda e, it=it, wb=wb: e.dma_start(
            out=sb_wfg[wb][:, :].rearrange("p (c f) -> p c f", f=P),
            in_=wg_g.ap()[:, it, :, :].rearrange("c p f -> p c f")),
           reads=["wgg"], writes=[("wfg", wb)], chan=f"wfg{wb}")

    def group(it, ti, half, hs=None):
        wb = it % 3
        wf, wkn, mx, mxkey, scr, stag = tis[ti]
        wkey = (wkn, wb)
        grp = halfgroups[half % 2][:nh4]
        if hs is None:
            hs = list(range(nh4))
        gkeys = [grp[h4][0] for h4 in hs]
        tkeys = []
        for h4 in hs:
            h = half * nh4 + h4
            tkeys += [("xqT", t) for t in range(h * (NW // P), min((h + 1) * (NW // P), n_tt))]

        def umm(e, wf=wf, wb=wb, half=half, hs=tuple(hs), grp=tuple(grp)):
            for c in range(n_ec):
                for h4 in hs:
                    h = half * nh4 + h4
                    ii = e.matmul(grp[h4][1][:, :], wf[wb][:, c * P:(c + 1) * P],
                                  sb_xqT[:, c * T + h * NW: c * T + (h + 1) * NW],
                                  start=(c == 0), stop=(c == n_ec - 1))
            return ii
        I_("tensor", "c", umm, reads=[wkey] + tkeys, writes=gkeys)

        hi = 0
        while hi < len(hs):
            h4 = hs[hi]
            h = half * nh4 + h4
            hh = h // (SW2 // NW)
            h2 = h % (SW2 // NW)
            sbuf_i = ti * 4 + hh
            gk, gps = grp[h4]
            merged = False
            if (hi + 1 < len(hs)) and hs[hi + 1] == h4 + 1 and h2 == 0 and (gk[:3] in ("psE", "psF")):
                gk2 = grp[h4 + 1][0]
                big = ps_e if gk[:3] == "psE" else ps_f

                def uev2(e, big=big, sbuf_i=sbuf_i):
                    return e.tensor_copy(sb_ugst[sbuf_i][:, 0:2 * NW], big[:, 0:2 * NW])
                I_("vector", "c", uev2, reads=[gk, gk2], writes=[("ugst", sbuf_i)])
                merged = True
            else:
                def uev(e, gps=gps, sbuf_i=sbuf_i, h2=h2):
                    return e.tensor_copy(sb_ugst[sbuf_i][:, h2 * NW:(h2 + 1) * NW], gps[:, :])
                I_("vector", "c", uev, reads=[gk], writes=[("ugst", sbuf_i)])
            reached = h2 + (1 if merged else 0)
            if reached == (SW2 // NW) - 1:
                slot = it * (T // SW2) + hh

                def umax(e, sbuf_i=sbuf_i, mx=mx, slot=slot, SW2=SW2):
                    return e.reduce_max(mx[:, slot:slot + 1],
                                        sb_ugst[sbuf_i][:, 0:SW2],
                                        axis=AX.X, apply_absolute_value=True)
                I_("vector", "c", umax, reads=[("ugst", sbuf_i)], writes=[mxkey])

                def ust(e, it=it, hh=hh, sbuf_i=sbuf_i, scr=scr, SW2=SW2):
                    return e.dma_start(out=scr.ap()[it, :, hh * SW2:(hh + 1) * SW2],
                                       in_=sb_ugst[sbuf_i][:, 0:SW2])
                wkeys = [stag] if stag != "gscr" else gsk(it, hh * SW2, SW2)
                I_("sync", "d", ust, reads=[("ugst", sbuf_i)], writes=wkeys, chan="dug")
            hi += 2 if merged else 1

    if n_grp > 1:
        emit_w(0)
        emit_w(1)
        for j in range(nh4):
            for tt in range(j * (NW // P), (j + 1) * (NW // P)):
                chain(tt)
            group(0, 0, 0, hs=[j])
        group(0, 1, 0)
        for tt in range(tpg, tpg + 4):
            chain(tt)
        group(1, 0, 0)
        for tt in range(tpg + 4, tpg + 8):
            chain(tt)
        group(1, 1, 0)
        for tt in range(tpg + 8, tpg + 12):
            chain(tt)
        emit_w(2)
        group(2, 0, 0)
        for tt in range(tpg + 12, n_tt):
            chain(tt)
        group(2, 1, 0)
        for it in (0, 1, 2):
            for ti in range(2):
                group(it, ti, 1)
        start_it = 3
    else:
        for tt in range(n_tt):
            chain(tt)
        start_it = 0
    for it in range(start_it, n_ic):
        emit_w(it)
        for ti in range(2):
            for half in range(n_grp):
                group(it, ti, half)

    gmax_pack(sb_um[:, 0:n_ic * (T // min(1024, T))], ["um"], 0)
    gmax_pack(sb_gm[:, 0:n_ic * (T // min(1024, T))], ["gm"], 1)
    allreduce_bcast(1, 4)
    gvts(6, 4, 1.0 / QMAX, 1e-12, ALU.mult, ALU.max)
    gvrec(7, 6)
    gvts(8, 5, 1.0 / QMAX, 1e-12, ALU.mult, ALU.max)
    gvrec(9, 8)
    gvts(10, 8, -MAGIC, None, ALU.mult, ALU.mult)
    gvts(11, 8, 1.0 / FIXQ, None, ALU.mult, ALU.mult)
    P_F2 = B.snap()
    f2.close()

    W3 = cfg.W3
    n_w3 = T // W3
    f34 = ExitStack()
    F3 = f34.enter_context
    sb_gin = [F3(nc.sbuf_tensor(f"gin{i}", [P, W3], F32)) for i in range(3)]
    sb_uin = [F3(nc.sbuf_tensor(f"uin{i}", [P, W3], F32)) for i in range(3)]
    sb_t1 = [F3(nc.sbuf_tensor(f"t1_{i}", [P, W3], F32)) for i in range(3)]
    sb_t2 = [F3(nc.sbuf_tensor(f"t2_{i}", [P, W3], F32)) for i in range(3)]
    sb_pr = [F3(nc.sbuf_tensor(f"pr_{i}", [P, W3], F32)) for i in range(3)]
    nch34 = n_ic * n_w3
    sb_pm3 = F3(nc.sbuf_tensor("pm3", [P, nch34], F32))
    sb_pm4 = F3(nc.sbuf_tensor("pm4", [P, nch34], F32))
    B.set_barrier(P_F2)

    def chain_sig(ci, b, with_u):
        I_("scalar", "c", lambda e, b=b: e.activation(sb_t1[b][:, :], sb_gin[b][:, :], AF.Copy,
                                                      bias=MAGIC, scale=sb_gv[:, 9:10]),
           reads=[("gin", b), "gv", "kc"], writes=[("t1", b)])
        I_("scalar", "c", lambda e, b=b: e.activation(sb_t2[b][:, :], sb_t1[b][:, :], AF.Sigmoid,
                                                      bias=sb_gv[:, 10:11], scale=sb_gv[:, 8:9]),
           reads=[("t1", b), "gv"], writes=[("t2", b)])
        I_("scalar", "c", lambda e, b=b: e.activation(sb_t2[b][:, :], sb_t2[b][:, :], AF.Copy,
                                                      bias=MAGIC, scale=sb_kc[:, 3:4]),
           reads=[("t2", b), "kc"], writes=[("t2", b)])
        I_("vector", "c", lambda e, b=b: e.tensor_scalar(sb_t2[b][:, :], sb_t2[b][:, :], MAGIC, 65535.0,
                                                         op0=ALU.subtract, op1=ALU.min),
           reads=[("t2", b)], writes=[("t2", b)])
        I_("vector", "c", lambda e, b=b: e.scalar_tensor_tensor(sb_pr[b][:, :], sb_t1[b][:, :], MAGIC,
                                                                sb_t2[b][:, :], op0=ALU.subtract,
                                                                op1=ALU.mult),
           reads=[("t1", b), ("t2", b)], writes=[("pr", b)])
        if with_u:
            I_("scalar", "c", lambda e, b=b: e.activation(sb_t1[b][:, :], sb_pr[b][:, :],
                                                          AF.Identity, bias=MAGIC,
                                                          scale=sb_gv[:, 16:17]),
               reads=[("pr", b), "gv", "kc"], writes=[("t1", b)])
            I_("vector", "c", lambda e, b=b: e.tensor_scalar(sb_t1[b][:, :], sb_t1[b][:, :], MAGIC, None,
                                                             op0=ALU.subtract),
               reads=[("t1", b)], writes=[("t1", b)])
            I_("scalar", "c", lambda e, b=b: e.activation(sb_t2[b][:, :], sb_uin[b][:, :], AF.Copy,
                                                          bias=MAGIC, scale=sb_gv[:, 7:8]),
               reads=[("uin", b), "gv", "kc"], writes=[("t2", b)])
            I_("vector", "c", lambda e, b=b: e.scalar_tensor_tensor(sb_pr[b][:, :], sb_t2[b][:, :], MAGIC,
                                                                    sb_t1[b][:, :], op0=ALU.subtract,
                                                                    op1=ALU.mult),
               reads=[("t1", b), ("t2", b)], writes=[("pr", b)])

    def f34_stageA(ci, b, with_u):
        it, h = divmod(ci, n_w3)
        I_("sync", "d", lambda e, it=it, h=h, b=b: e.dma_start(
            out=sb_gin[b][:, :], in_=g_scr.ap()[it, :, h * W3:(h + 1) * W3]),
           reads=gsk(it, h * W3, W3), writes=[("gin", b)], chan=f"gin{b}")
        if with_u:
            I_("gpsimd", "d", lambda e, it=it, h=h, b=b: e.dma_start(
                out=sb_uin[b][:, :], in_=u_scr.ap()[it, :, h * W3:(h + 1) * W3]),
               reads=["uscr"], writes=[("uin", b)], chan=f"uin{b}")
            I_("scalar", "c", lambda e, b=b: e.activation(sb_uin[b][:, :], sb_uin[b][:, :], AF.Copy,
                                                          bias=MAGIC, scale=sb_gv[:, 7:8]),
               reads=[("uin", b), "gv", "kc"], writes=[("uin", b)])
            return
        I_("scalar", "c", lambda e, b=b: e.activation(sb_t1[b][:, :], sb_gin[b][:, :], AF.Copy,
                                                      bias=MAGIC, scale=sb_gv[:, 9:10]),
           reads=[("gin", b), "gv", "kc"], writes=[("t1", b)])
        I_("scalar", "c", lambda e, b=b: e.activation(sb_t2[b][:, :], sb_t1[b][:, :], AF.Sigmoid,
                                                      bias=sb_gv[:, 10:11], scale=sb_gv[:, 8:9]),
           reads=[("t1", b), "gv"], writes=[("t2", b)])
        I_("scalar", "c", lambda e, b=b: e.activation(sb_t2[b][:, :], sb_t2[b][:, :], AF.Copy,
                                                      bias=MAGIC, scale=sb_kc[:, 3:4]),
           reads=[("t2", b), "kc"], writes=[("t2", b)])

    def f34_stageB(ci, b, with_u):
        it, h = divmod(ci, n_w3)
        if not with_u:
            I_("vector", "c", lambda e, b=b: e.tensor_scalar(sb_t2[b][:, :], sb_t2[b][:, :], MAGIC,
                                                             65535.0, op0=ALU.subtract, op1=ALU.min),
               reads=[("t2", b)], writes=[("t2", b)])
            I_("vector", "c", lambda e, b=b: e.scalar_tensor_tensor(sb_pr[b][:, :], sb_t1[b][:, :],
                                                                    MAGIC, sb_t2[b][:, :],
                                                                    op0=ALU.subtract, op1=ALU.mult),
               reads=[("t1", b), ("t2", b)], writes=[("pr", b)])
            I_("vector", "c", lambda e, b=b, ci=ci: e.reduce_max(
                sb_pm3[:, ci:ci + 1], sb_pr[b][:, :], axis=AX.X, apply_absolute_value=True),
               reads=[("pr", b)], writes=["pm3"])
            I_("scalar", "d", lambda e, it=it, h=h, b=b: e.dma_start(
                out=g_scr.ap()[it, :, h * W3:(h + 1) * W3], in_=sb_pr[b][:, :]),
               reads=[("pr", b)], writes=gsk(it, h * W3, W3), chan="dp3")
            return
        I_("vector", "c", lambda e, b=b: e.tensor_scalar(sb_t1[b][:, :], sb_gin[b][:, :],
                                                         sb_gv[:, 16:17], MAGIC,
                                                         op0=ALU.mult, op1=ALU.add),
           reads=[("gin", b), "gv"], writes=[("t1", b)])
        I_("vector", "c", lambda e, b=b: e.tensor_scalar(sb_t1[b][:, :], sb_t1[b][:, :], MAGIC, None,
                                                         op0=ALU.subtract),
           reads=[("t1", b)], writes=[("t1", b)])
        I_("vector", "c", lambda e, b=b: e.scalar_tensor_tensor(sb_pr[b][:, :], sb_uin[b][:, :], MAGIC,
                                                                sb_t1[b][:, :], op0=ALU.subtract,
                                                                op1=ALU.mult),
           reads=[("t1", b), ("uin", b)], writes=[("pr", b)])
        I_("vector", "c", lambda e, b=b, ci=ci: e.reduce_max(
            sb_pm4[:, ci:ci + 1], sb_pr[b][:, :], axis=AX.X, apply_absolute_value=True),
           reads=[("pr", b)], writes=["pm4"])
        I_("scalar", "d", lambda e, it=it, h=h, b=b: e.dma_start(
            out=g_scr.ap()[it, :, h * W3:(h + 1) * W3], in_=sb_pr[b][:, :]),
           reads=[("pr", b)], writes=gsk(it, h * W3, W3), chan="dpr")

    for ci in range(nch34):
        f34_stageA(ci, ci % 3, with_u=False)
        if ci >= 1:
            f34_stageB(ci - 1, (ci - 1) % 3, with_u=False)
    f34_stageB(nch34 - 1, (nch34 - 1) % 3, with_u=False)

    gmax_pack(sb_pm3[:, 0:nch34], ["pm3"], 0)
    allreduce_bcast(2, 12)
    gvtt(13, 12, 11)
    gvts(14, 13, 1.0 / QMAX, 1e-12, ALU.mult, ALU.max)
    gvrec(15, 14)
    gvtt(16, 11, 15)

    for ci in range(nch34):
        f34_stageA(ci, ci % 3, with_u=True)
        if ci >= 1:
            f34_stageB(ci - 1, (ci - 1) % 3, with_u=True)
    f34_stageB(nch34 - 1, (nch34 - 1) % 3, with_u=True)

    gmax_pack(sb_pm4[:, 0:nch34], ["pm4"], 0)
    allreduce_bcast(3, 17)
    gvtt(19, 14, 6)
    gvtt(18, 19, 17)
    gvts(20, 18, 1.0 / QMAX, 1e-12, ALU.mult, ALU.max)
    gvrec(21, 20)
    gvtt(22, 19, 21)
    gvts(23, 20, -MAGIC, None, ALU.mult, ALU.mult)
    P_F4 = B.snap()
    f34.close()

    F5S = cfg.F5S
    n_f5 = T // F5S
    f5 = ExitStack()
    F5 = f5.enter_context
    GW_ = min(1024, T)
    sb_osup = [F5(nc.sbuf_tensor(f"osup{i}", [P, n_ic * GW_], BF16)) for i in range(2)]
    sb_wd = F5(nc.sbuf_tensor("wd", [P, n_ic * E], BF16))
    sb_g5 = [F5(nc.sbuf_tensor(f"g5_{i}", [P, GW_], F32)) for i in range(4)]
    sb_pr5 = [F5(nc.sbuf_tensor(f"pr5_{i}", [P, GW_], F32)) for i in range(2)]
    sb_xm5 = [F5(nc.sbuf_tensor(f"xm5_{i}", [P, E], F32)) for i in range(2)]
    sb_ot = [F5(nc.sbuf_tensor(f"ot{i}", [P, E], F32)) for i in range(2)]
    sb_bn5 = F5(nc.sbuf_tensor("bn5", [P, 6 * n_ln + 8], F32))
    B.set_barrier(P_F4)

    I_("gpsimd", "d", lambda e: e.dma_start(out=sb_wd[:, :].rearrange("p (i e) -> p i e", e=E),
                                            in_=wd_g.ap().rearrange("i p e -> p i e")),
       reads=["wdg"], writes=["wd"], chan="wd")

    dpairs = [
        (["psE0", "psE1"], [(ps_e[:, 0:512], 0), (ps_e[:, 512:1024], 512)]),
        (["psF0", "psF1"], [(ps_f[:, 0:512], 0), (ps_f[:, 512:1024], 512)]),
        (["psA", "psB"], [(ps_a[:, :], 0), (ps_b[:, :], 512)]),
        (["psG", "psH"], [(ps_g[:, :].bitcast(F32), 0), (ps_h[:, :], 512)]),
    ]
    nhalf5 = (E + NW - 1) // NW
    GW = min(1024, T)
    n_g5 = T // GW
    gtiles = GW // P

    def f5_elem(g, it0, it1):
        gb = g % 2
        for it in range(it0, it1):
            b = (g * n_ic + it) % 4
            I_("sync", "d", lambda e, it=it, g=g, b=b: e.dma_start(
                out=sb_g5[b][:, :], in_=g_scr.ap()[it, :, g * GW:(g + 1) * GW]),
               reads=gsk(it, g * GW, GW), writes=[("g5", b)], chan=f"g5{b}")
            I_("scalar", "c", lambda e, b=b: e.activation(sb_pr5[b % 2][:, :], sb_g5[b][:, :],
                                                          AF.Identity, bias=MAGIC,
                                                          scale=sb_gv[:, 22:23]),
               reads=[("g5", b), "gv", "kc"], writes=[("pr5", b % 2)])
            I_("vector", "c", lambda e, b=b, gb=gb, it=it: e.tensor_scalar(
                sb_osup[gb][:, it * GW:(it + 1) * GW], sb_pr5[b % 2][:, :],
                sb_gv[:, 20:21], sb_gv[:, 23:24], op0=ALU.mult, op1=ALU.add),
               reads=[("pr5", b % 2), "gv"], writes=[("osup", gb, it)])

    def f5_mm(g, tsub):
        gb = g % 2
        tt = g * gtiles + tsub
        okeys = [("osup", gb, it) for it in range(n_ic)]
        dkeys, dhalves = dpairs[tt % 4]

        def dmm(e, gb=gb, tsub=tsub, dhalves=tuple(dhalves)):
            for it in range(n_ic):
                for h in range(nhalf5):
                    w = min(NW, E - h * NW)
                    pp, _ = dhalves[h]
                    ii = e.matmul(pp[:, 0:w],
                                  sb_osup[gb][:, it * GW + tsub * P: it * GW + (tsub + 1) * P],
                                  sb_wd[:, it * E + h * NW: it * E + h * NW + w],
                                  start=(it == 0), stop=(it == n_ic - 1))
            return ii
        I_("tensor", "c", dmm, reads=okeys + ["wd"], writes=dkeys)

    def f5_epi(tt):
        dkeys, dhalves = dpairs[tt % 4]
        xb5 = tt % 2
        I_("sync", "d", lambda e, tt=tt, xb5=xb5: e.dma_start(
            out=sb_xm5[xb5][:, :], in_=xm_scr.ap()[tt * P:(tt + 1) * P, :]),
           reads=["xmscr"], writes=[("xm5", xb5)], chan=f"xm5{xb5}")

        def bns5(e, dhalves=tuple(dhalves)):
            for c in range(n_ln):
                w = min(512, E - c * 512)
                pp, off = dhalves[c]
                ii = e.bn_stats(sb_bn5[:, 6 * c:6 * c + 6], pp[:, 0:w])
            return ii
        I_("vector", "c", bns5, reads=dkeys, writes=["bn5"])
        I_("vector", "c", lambda e: e.bn_aggr(sb_bn5[:, 6 * n_ln:6 * n_ln + 2], sb_bn5[:, 0:6 * n_ln]),
           reads=["bn5"], writes=["bn5"])
        I_("scalar", "c", lambda e: e.activation(sb_bn5[:, 6 * n_ln + 2:6 * n_ln + 3],
                                                 sb_bn5[:, 6 * n_ln + 1:6 * n_ln + 2],
                                                 AF.Ln, bias=sb_kc[:, 2:3], scale=sb_kc[:, 6:7]),
           reads=["bn5"], writes=["bn5"])
        I_("scalar", "c", lambda e: e.activation(sb_bn5[:, 6 * n_ln + 3:6 * n_ln + 4],
                                                 sb_bn5[:, 6 * n_ln + 2:6 * n_ln + 3],
                                                 AF.Exp, bias=sb_kc[:, 5:6], scale=sb_kc[:, 1:2]),
           reads=["bn5"], writes=["bn5"])

        def lny(e, xb5=xb5, dhalves=tuple(dhalves)):
            for c in range(n_ln):
                w = min(512, E - c * 512)
                pp, off = dhalves[c]
                ii = e.tensor_scalar(sb_ot[xb5][:, off:off + w], pp[:, 0:w],
                                     sb_bn5[:, 6 * n_ln:6 * n_ln + 1],
                                     sb_bn5[:, 6 * n_ln + 3:6 * n_ln + 4],
                                     op0=ALU.subtract, op1=ALU.mult)
            return ii
        I_("vector", "c", lny, reads=dkeys + ["bn5"], writes=[("ot", xb5)])
        I_("vector", "c", lambda e, xb5=xb5: e.tensor_tensor(sb_ot[xb5][:, :], sb_ot[xb5][:, :],
                                                             sb_xm5[xb5][:, :], op=ALU.add),
           reads=[("ot", xb5), ("xm5", xb5)], writes=[("ot", xb5)])
        I_("sync", "d", lambda e, tt=tt, xb5=xb5: e.dma_start(
            out=out_h.ap()[tt * P:(tt + 1) * P, :], in_=sb_ot[xb5][:, :]),
           reads=[("ot", xb5)], writes=["outh"], chan="dout")

    f5_elem(0, 0, n_ic)
    for g in range(n_g5):
        for tsub in range(gtiles):
            tt = g * gtiles + tsub
            f5_mm(g, tsub)
            if g + 1 < n_g5:
                f5_elem(g + 1, (n_ic * tsub) // gtiles, (n_ic * (tsub + 1)) // gtiles)
            if tt >= 2:
                f5_epi(tt - 2)
    f5_epi(T // P - 2)
    f5_epi(T // P - 1)

    fin = B.sems["D:dout"][1]
    B.raw("sync", lambda e, fin=fin: e.wait_ge(B.sems["D:dout"][0], fin))

    with nc.Block() as block:
        @block.sync
        def _(e):
            for fn in B.streams["sync"]:
                fn(e)

        @block.tensor
        def _(e):
            for fn in B.streams["tensor"]:
                fn(e)

        @block.vector
        def _(e):
            for fn in B.streams["vector"]:
                fn(e)

        @block.scalar
        def _(e):
            for fn in B.streams["scalar"]:
                fn(e)

        @block.gpsimd
        def _(e):
            for fn in B.streams["gpsimd"]:
                fn(e)

    f5.close()
    es.close()
    return nc


def make_inputs(cfg, X, Wq, bq, Wk, bk, Wv, bv, w_gate, w_up, w_down):
    f32 = np.float32
    wqT = np.ascontiguousarray(np.asarray(Wq, f32).T)
    wkT = np.ascontiguousarray(np.asarray(Wk, f32).T)
    wvT = np.ascontiguousarray(np.asarray(Wv, f32).T)
    bq = np.ascontiguousarray(np.asarray(bq, f32).reshape(cfg.n_ec, P).T)
    bk = np.ascontiguousarray(np.asarray(bk, f32).reshape(cfg.n_ec, P).T)
    bvt = np.ascontiguousarray(np.broadcast_to(np.asarray(bv, f32), (P, cfg.E)))
    mask = np.where(np.arange(P)[None, :] >= np.arange(P)[:, None], 0.0, NEG).astype(f32)
    idb = np.eye(P, dtype=ml_dtypes.bfloat16)
    idf = np.eye(P, dtype=f32)
    w_up = np.asarray(w_up, f32)
    w_gate = np.asarray(w_gate, f32)
    w_down = np.asarray(w_down, f32)
    X = np.asarray(X, f32)
    in_maps = []
    for c in range(cfg.n_cores):
        in_maps.append({
            "x": np.ascontiguousarray(X[c]),
            "wqT": wqT, "wkT": wkT, "wvT": wvT,
            "bq": bq, "bk": bk, "bvt": bvt,
            "wu_sh": np.ascontiguousarray(w_up[:, c * cfg.SH_E:(c + 1) * cfg.SH_E]),
            "wg_sh": np.ascontiguousarray(w_gate[:, c * cfg.SH_E:(c + 1) * cfg.SH_E]),
            "wd_sh": np.ascontiguousarray(w_down[:, c * cfg.SH_I:(c + 1) * cfg.SH_I]),
            "mask": mask, "idb": idb, "idf": idf,
        })
    return in_maps


def kernel(L0_attn_X, L0_attn_Wq, L0_attn_bq, L0_attn_Wk, L0_attn_bk,
           L0_attn_Wv, L0_attn_bv, L0_ffn_w_gate, L0_ffn_w_up, L0_ffn_w_down):
    cfg = Cfg(T=4096, E=1024, I=3072, n_cores=8, attn_scale=1.0 / math.sqrt(1024.0))
    nc = build_nc(cfg)
    in_maps = make_inputs(cfg, L0_attn_X, L0_attn_Wq, L0_attn_bq, L0_attn_Wk,
                          L0_attn_bk, L0_attn_Wv, L0_attn_bv,
                          L0_ffn_w_gate, L0_ffn_w_up, L0_ffn_w_down)
    res = run_bass_kernel_spmd(nc, in_maps, core_ids=list(range(cfg.n_cores)))
    out = np.stack([np.asarray(r["out"], dtype=np.float32) for r in res.results])
    return out


# version 35 (final)
# speedup vs baseline: 1.0186x; 1.0150x over previous
import math
from contextlib import ExitStack

import numpy as np
import ml_dtypes

import concourse.bass as bass
import concourse.mybir as mybir
from concourse.bass_utils import run_bass_kernel_spmd

F32 = mybir.dt.float32
BF16 = mybir.dt.bfloat16
AF = mybir.ActivationFunctionType
ALU = mybir.AluOpType
AX = mybir.AxisListType

P = 128
MAGIC = 12582912.0
QMAX = 32767.0
FIXQ = 65536.0
WQMAX = 7.0
WBLK = 32
EPS = 1e-5
NEG = -1.0e30


class Cfg:
    def __init__(self, T=4096, E=1024, I=3072, n_cores=8,
                 attn_scale=1.0 / 32.0):
        self.T, self.E, self.I, self.n_cores = T, E, I, n_cores
        self.attn_scale = attn_scale
        self.n_tt = T // P
        self.n_ec = E // P
        self.n_ic = I // P
        self.NW = 512
        self.SBW = 512
        self.n_sb = T // self.SBW
        self.W3 = min(2048, T)
        self.F5S = min(512, T)
        self.SH_E = E // n_cores
        self.SH_I = I // n_cores
        self.n_ln = (E + 511) // 512
        assert T % 512 == 0 and E % P == 0 and I % P == 0
        assert self.SH_E % P == 0 and self.SH_I % P == 0 and self.SH_E % WBLK == 0


SEMKIND = {
    "sync": {"d": "DSY"},
    "gpsimd": {"d": "DGP", "c": "GP", "cc": "GPC"},
    "tensor": {"c": "TE"},
    "vector": {"c": "V"},
    "scalar": {"c": "SC"},
}


class Builder:

    def __init__(self, nc, es):
        self.nc = nc
        self.es = es
        self.streams = {k: [] for k in ("sync", "tensor", "vector", "scalar", "gpsimd")}
        self.sems = {}
        for name in ("GP", "GPC", "TE", "V", "SC"):
            self._mksem(name)
        self.res = {}
        self.sat = {k: {} for k in self.streams}
        self.barrier = {}

    def _mksem(self, name):
        h = self.es.enter_context(self.nc.semaphore("s_" + name.replace(":", "_")))
        self.sems[name] = [h, 0]
        return self.sems[name]

    def snap(self):
        return {k: v[1] for k, v in self.sems.items()}

    def set_barrier(self, snap):
        self.barrier = dict(snap)

    def ins(self, stream, kind, fn, reads=(), writes=(), chan=None):
        if kind == "d":
            assert chan is not None, "DMA needs a channel"
            semname = "D:" + chan
            sem = self.sems.get(semname) or self._mksem(semname)
            inc = 16
        else:
            semname = SEMKIND[stream][kind]
            sem = self.sems[semname]
            inc = 1
        need = {}

        def add(t):
            if t is not None:
                sn, v = t
                if v > 0:
                    need[sn] = max(need.get(sn, 0), v)

        for k in reads:
            r = self.res.get(k)
            if r is None:
                continue
            add(r[0])
        for k in writes:
            r = self.res.get(k)
            if r is None:
                for sn, v in self.barrier.items():
                    add((sn, v))
                continue
            add(r[0])
            for sn, v in r[1].items():
                add((sn, v))
        sat = self.sat[stream]
        waits = []
        for sn in sorted(need):
            v = need[sn]
            if sn.startswith("D:"):
                v = self.sems[sn][1]
            if v > 0 and sat.get(sn, 0) < v:
                waits.append((self.sems[sn][0], v))
                sat[sn] = v
        sem[1] += inc
        tick = sem[1]

        def wrapped(e, fn=fn, waits=tuple(waits), h=sem[0], inc=inc):
            for s, v in waits:
                e.wait_ge(s, v)
            fn(e).then_inc(h, inc)

        self.streams[stream].append(wrapped)
        done = (semname, tick)
        for k in writes:
            self.res[k] = [done, {}]
        for k in reads:
            r = self.res.setdefault(k, [None, {}])
            r[1][semname] = tick
        return done

    def raw(self, stream, fn):
        self.streams[stream].append(fn)


def build_nc(cfg: Cfg):
    T, E, I = cfg.T, cfg.E, cfg.I
    n_tt, n_ec, n_ic = cfg.n_tt, cfg.n_ec, cfg.n_ic
    NW, SBW, n_sb = cfg.NW, cfg.SBW, cfg.n_sb
    n_cores = cfg.n_cores
    n_ln = cfg.n_ln
    ASC = cfg.attn_scale

    nc = bass.Bass("TRN2", target_bir_lowering=False, debug=False)

    x_h = nc.dram_tensor("x", [T, E], F32, kind="ExternalInput")
    wqT_h = nc.dram_tensor("wqT", [E, E], F32, kind="ExternalInput")
    wkT_h = nc.dram_tensor("wkT", [E, E], F32, kind="ExternalInput")
    wvT_h = nc.dram_tensor("wvT", [E, E], F32, kind="ExternalInput")
    bq_h = nc.dram_tensor("bq", [P, E // P], F32, kind="ExternalInput")
    bk_h = nc.dram_tensor("bk", [P, E // P], F32, kind="ExternalInput")
    bvt_h = nc.dram_tensor("bvt", [P, E], F32, kind="ExternalInput")
    wu_sh_h = nc.dram_tensor("wu_sh", [I, cfg.SH_E], F32, kind="ExternalInput")
    wg_sh_h = nc.dram_tensor("wg_sh", [I, cfg.SH_E], F32, kind="ExternalInput")
    wd_sh_h = nc.dram_tensor("wd_sh", [E, cfg.SH_I], F32, kind="ExternalInput")
    mask_h = nc.dram_tensor("mask", [P, P], F32, kind="ExternalInput")
    idb_h = nc.dram_tensor("idb", [P, P], BF16, kind="ExternalInput")
    idf_h = nc.dram_tensor("idf", [P, P], F32, kind="ExternalInput")
    out_h = nc.dram_tensor("out", [T, E], F32, kind="ExternalOutput")

    wqb_s = nc.dram_tensor("wqb_s", [E // P, E // P, P, P], BF16)
    wkb_s = nc.dram_tensor("wkb_s", [E // P, E // P, P, P], BF16)
    wvb_s = nc.dram_tensor("wvb_s", [E, E], BF16)
    k_scr = nc.dram_tensor("k_scr", [n_tt, n_ec, P, P], BF16)
    p_scr = nc.dram_tensor("p_scr", [n_tt, n_tt, P, P], BF16)
    xm_scr = nc.dram_tensor("xm_scr", [T, E], F32)
    u_scr = nc.dram_tensor("u_scr", [n_ic, P, T], F32)
    g_scr = nc.dram_tensor("g_scr", [n_ic, P, T], F32)
    n_esh = cfg.SH_E // P
    n_ish = cfg.SH_I // P
    agu_in = nc.dram_tensor("agu_in", [n_esh, n_ic, P, P], BF16)
    agg_in = nc.dram_tensor("agg_in", [n_esh, n_ic, P, P], BF16)
    agd_in = nc.dram_tensor("agd_in", [n_ish, P, E], BF16)
    wu_g = nc.dram_tensor("wu_g", [n_cores * n_esh, n_ic, P, P], BF16)
    wg_g = nc.dram_tensor("wg_g", [n_cores * n_esh, n_ic, P, P], BF16)
    wd_g = nc.dram_tensor("wd_g", [n_cores * n_ish, P, E], BF16)
    cc_in = [nc.dram_tensor(f"cc_in{i}", [1, 2], F32) for i in range(4)]
    cc_out = [nc.dram_tensor(f"cc_out{i}", [1, 2], F32) for i in range(4)]

    es = ExitStack()
    A = es.enter_context
    B = Builder(nc, es)
    I_ = B.ins

    sb_mask = A(nc.sbuf_tensor("c_mask", [P, P], F32))
    sb_idb = A(nc.sbuf_tensor("c_idb", [P, P], BF16))
    sb_idf = A(nc.sbuf_tensor("c_idf", [P, P], F32))
    sb_ones = A(nc.sbuf_tensor("c_ones", [1, P], F32))
    sb_bq = A(nc.sbuf_tensor("c_bq", [P, n_ec], F32))
    sb_bk = A(nc.sbuf_tensor("c_bk", [P, n_ec], F32))
    sb_m2 = A(nc.sbuf_tensor("c_m2", [P, n_tt], F32))
    sb_rs2 = A(nc.sbuf_tensor("c_rs2", [P, n_tt], F32))
    sb_rs2r1 = A(nc.sbuf_tensor("c_rs2r1", [P, n_tt], F32))
    sb_cand = A(nc.sbuf_tensor("c_cand", [P, n_tt], F32))
    sb_gv = A(nc.sbuf_tensor("c_gv", [P, 32], F32))
    sb_sc = A(nc.sbuf_tensor("c_sc", [1, 2], F32))
    sb_sm = A(nc.sbuf_tensor("c_sm", [P, 4], F32))
    sb_smr = A(nc.sbuf_tensor("c_smr", [1, P], F32))
    sb_kc = A(nc.sbuf_tensor("c_kc", [P, 8], F32))
    nc.const_aps.register(F32, MAGIC, sb_kc[:, 0:1])
    DQW = max(4 * cfg.SH_E, cfg.SH_I + P)
    sb_dqw = A(nc.sbuf_tensor("dq_w", [P, DQW], F32))
    sb_dqs = A(nc.sbuf_tensor("dq_s", [P, DQW // WBLK], F32))
    sb_dqr = A(nc.sbuf_tensor("dq_r", [P, DQW // WBLK], F32))
    sb_dqq = A(nc.sbuf_tensor("dq_q", [P, DQW], F32))
    sb_dqb = A(nc.sbuf_tensor("dq_qb", [P, DQW], BF16))
    sb_dqt = A(nc.sbuf_tensor("dq_tr", [P, DQW], BF16))

    ps_a = A(nc.psum_tensor("ps_a", [P, 512], F32))
    ps_b = A(nc.psum_tensor("ps_b", [P, 512], F32))
    ps_e = A(nc.psum_tensor("ps_e", [P, 1024], F32))
    ps_f = A(nc.psum_tensor("ps_f", [P, 1024], F32))
    ps_g = A(nc.psum_tensor("ps_g", [P, 1024], BF16))
    ps_h = A(nc.psum_tensor("ps_h", [P, 512], F32))

    attspan = ExitStack()
    sb_qT = attspan.enter_context(nc.sbuf_tensor("qT", [P, n_ec * T], BF16))
    sb_v = attspan.enter_context(nc.sbuf_tensor("v", [P, n_tt * E], BF16))

    I_("sync", "d", lambda e: e.dma_start(out=sb_mask[:, :], in_=mask_h.ap()), writes=["mask"], chan="lcon")
    I_("sync", "d", lambda e: e.dma_start(out=sb_idb[:, :], in_=idb_h.ap()), writes=["idb"], chan="lcon")
    I_("sync", "d", lambda e: e.dma_start(out=sb_idf[:, :], in_=idf_h.ap()), writes=["idf"], chan="lcon")
    I_("sync", "d", lambda e: e.dma_start(out=sb_bq[:, :], in_=bq_h.ap()), writes=["bq"], chan="lcon")
    I_("sync", "d", lambda e: e.dma_start(out=sb_bk[:, :], in_=bk_h.ap()), writes=["bk"], chan="lcon")
    I_("vector", "c", lambda e: e.memset(sb_ones[:, :], 1.0), writes=["ones"])
    I_("vector", "c", lambda e: e.memset(sb_sc[:, :], 0.0), writes=["sc"])
    for _col, _val in ((0, MAGIC), (1, -0.5), (2, EPS), (3, FIXQ), (4, ASC), (5, 0.0), (6, 1.0), (7, 65535.0 + MAGIC)):
        I_("vector", "c", lambda e, c=_col, v=_val: e.memset(sb_kc[:, c:c + 1], v), writes=["kc"])
    I_("vector", "c", lambda e: e.tensor_scalar(sb_bq[:, :], sb_bq[:, :], ASC, None, op0=ALU.mult),
       reads=["bq"], writes=["bq"])
    for _ft in range(E // P):
        I_("gpsimd", "d", lambda e, ft=_ft: e.dma_start(
            out=wqb_s.ap()[ft, :, :, :],
            in_=wqT_h.ap()[:, ft * P:(ft + 1) * P].rearrange("(c p) fc -> c p fc", p=P)),
           writes=[("wqb_s", _ft)], chan="dqb")
        I_("gpsimd", "d", lambda e, ft=_ft: e.dma_start(
            out=wkb_s.ap()[ft, :, :, :],
            in_=wkT_h.ap()[:, ft * P:(ft + 1) * P].rearrange("(c p) fc -> c p fc", p=P)),
           writes=[("wkb_s", _ft)], chan="dqb")
    I_("gpsimd", "d", lambda e: e.dma_start(out=wvb_s.ap(), in_=wvT_h.ap()), writes=["wvb_s"], chan="dqb")

    qkv = ExitStack()
    Q = qkv.enter_context
    sb_xin = [Q(nc.sbuf_tensor(f"xin{i}", [P, E], F32)) for i in range(2)]
    sb_ln1 = [Q(nc.sbuf_tensor(f"ln1_{i}", [P, E], BF16)) for i in range(4)]
    sb_xTb = [Q(nc.sbuf_tensor(f"xTb{i}", [P, n_ec * SBW], BF16)) for i in range(2)]
    sb_bn = [Q(nc.sbuf_tensor(f"bn_t{i}", [P, 6 * n_ln + 4], F32)) for i in range(2)]
    sb_bvt = Q(nc.sbuf_tensor("bvt_t", [P, E], F32))
    sb_wqk = [Q(nc.sbuf_tensor(f"wqk{i}", [P, n_ec * P], BF16)) for i in range(4)]
    sb_wv = [Q(nc.sbuf_tensor(f"wvb{i}", [P, E], BF16)) for i in range(4)]
    sb_kst = [Q(nc.sbuf_tensor(f"kst{i}", [P, NW], BF16)) for i in range(2)]

    I_("sync", "d", lambda e: e.dma_start(out=sb_bvt[:, :], in_=bvt_h.ap()), writes=["bvt"], chan="lbvt")
    nts = SBW // P

    def feeder_ln(sb, tsub):
        if True:
            tt = sb * nts + tsub
            xb = tt % 2
            lb = tt % 4
            bn = sb_bn[tt % 2]

            def ldx(e, tt=tt, xb=xb):
                return e.dma_start(out=sb_xin[xb][:, :], in_=x_h.ap()[tt * P:(tt + 1) * P, :])
            I_("sync", "d", ldx, writes=[("xin", xb)], chan=f"xin{xb}")

            def bns(e, xb=xb, bn=bn):
                for c in range(n_ln):
                    w = min(512, E - c * 512)
                    ii = e.bn_stats(bn[:, 6 * c:6 * c + 6], sb_xin[xb][:, c * 512:c * 512 + w])
                return ii
            I_("vector", "c", bns, reads=[("xin", xb)], writes=[("bn", tt % 2)])
            I_("vector", "c", lambda e, bn=bn: e.bn_aggr(bn[:, 6 * n_ln:6 * n_ln + 2], bn[:, 0:6 * n_ln]),
               reads=[("bn", tt % 2)], writes=[("bn", tt % 2)])
            I_("scalar", "c", lambda e, bn=bn: e.activation(bn[:, 6 * n_ln + 2:6 * n_ln + 3],
                                                            bn[:, 6 * n_ln + 1:6 * n_ln + 2],
                                                            AF.Ln, bias=sb_kc[:, 2:3], scale=sb_kc[:, 6:7]),
               reads=[("bn", tt % 2)], writes=[("bn", tt % 2)])
            I_("scalar", "c", lambda e, bn=bn: e.activation(bn[:, 6 * n_ln + 3:6 * n_ln + 4],
                                                            bn[:, 6 * n_ln + 2:6 * n_ln + 3],
                                                            AF.Exp, bias=sb_kc[:, 5:6], scale=sb_kc[:, 1:2]),
               reads=[("bn", tt % 2)], writes=[("bn", tt % 2)])

            def nrm(e, xb=xb, lb=lb, bn=bn):
                return e.tensor_scalar(sb_ln1[lb][:, :], sb_xin[xb][:, :],
                                       bn[:, 6 * n_ln:6 * n_ln + 1],
                                       bn[:, 6 * n_ln + 3:6 * n_ln + 4],
                                       op0=ALU.subtract, op1=ALU.mult)
            I_("vector", "c", nrm, reads=[("xin", xb), ("bn", tt % 2)], writes=[("ln1", lb)])

    def feeder_tp(sb, tsub):
        sp = sb % 2
        lb = (sb * nts + tsub) % 4
        tkey, tdst = [("psG", ps_g[:, :]),
                      ("psH", ps_h[:, :].bitcast(BF16)),
                      ("psE0", ps_e[:, :].bitcast(BF16)[:, 0:n_ec * P]),
                      ("psF0", ps_f[:, :].bitcast(BF16)[:, 0:n_ec * P])][tsub % 4]

        def tp(e, lb=lb, tdst=tdst):
            for c in range(n_ec):
                ii = e.transpose(tdst[:, c * P:(c + 1) * P], sb_ln1[lb][:, c * P:(c + 1) * P],
                                 sb_idb[:, :])
            return ii
        I_("tensor", "c", tp, reads=[("ln1", lb), "idb"], writes=[tkey])

        def tev(e, tsub=tsub, sp=sp, tdst=tdst):
            for c in range(n_ec):
                ii = e.tensor_copy(sb_xTb[sp][:, c * SBW + tsub * P: c * SBW + (tsub + 1) * P],
                                   tdst[:, c * P:(c + 1) * P])
            return ii
        I_("vector", "c", tev, reads=[tkey], writes=[("xTb", sp, tsub)])

    def feeder(sb):
        for tsub in range(nts):
            feeder_ln(sb, tsub)
        for tsub in range(nts):
            feeder_tp(sb, tsub)

    def qk_section(sb):
        sp = sb % 2
        xTb_keys = [("xTb", sp, i) for i in range(nts)]
        for wsrc, srckey, bias_t, bkey, is_k in (
                (wqb_s, "wqb_s", sb_bq, "bq", False),
                (wkb_s, "wkb_s", sb_bk, "bk", True)):
            for f in range(n_ec):
                wb = f % 4

                def ldw(e, wsrc=wsrc, f=f, wb=wb):
                    return e.dma_start(out=sb_wqk[wb][:, :].rearrange("p (c f) -> p c f", f=P),
                                       in_=wsrc.ap()[f, :, :, :].rearrange("c p f -> p c f"))
                I_("gpsimd", "d", ldw, reads=[(srckey, f)], writes=[("wqk", wb)], chan=f"wqk{wb}")
                bank, ps = (["psA", ps_a] if f % 2 == 0 else ["psB", ps_b])

                def mm(e, wb=wb, ps=ps, sp=sp):
                    for c in range(n_ec):
                        ii = e.matmul(ps[:, 0:SBW], sb_wqk[wb][:, c * P:(c + 1) * P],
                                      sb_xTb[sp][:, c * SBW:(c + 1) * SBW],
                                      start=(c == 0), stop=(c == n_ec - 1))
                    return ii
                I_("tensor", "c", mm, reads=[("wqk", wb)] + xTb_keys, writes=[bank])

                if not is_k:
                    def ev(e, f=f, sb=sb, ps=ps, bias_t=bias_t):
                        return e.activation(sb_qT[:, f * T + sb * SBW: f * T + (sb + 1) * SBW],
                                            ps[:, 0:SBW], AF.Identity,
                                            bias=bias_t[:, f:f + 1], scale=sb_kc[:, 4:5])
                    I_("scalar", "c", ev, reads=[bank, bkey, "kc"], writes=["qT"])
                else:
                    kb = f % 2

                    def ev2(e, ps=ps, bias_t=bias_t, f=f, kb=kb):
                        return e.activation(sb_kst[kb][:, :], ps[:, 0:SBW], AF.Identity,
                                            bias=bias_t[:, f:f + 1], scale=sb_kc[:, 6:7])
                    I_("scalar", "c", ev2, reads=[bank, bkey, "kc"], writes=[("kst", kb)])

                    def kstore(e, sb=sb, f=f, kb=kb):
                        return e.dma_start(
                            out=k_scr.ap()[sb * nts:(sb + 1) * nts, f, :, :].rearrange("J p c -> p J c"),
                            in_=sb_kst[kb][:, :].rearrange("p (J c) -> p J c", c=P))
                    I_("scalar", "d", kstore, reads=[("kst", kb)], writes=["kscr"], chan="dks")

    def v_section(sb):
        sp = sb % 2
        nhalf = (E + NW - 1) // NW
        vunits = [("psA", ps_a[:, :]), ("psB", ps_b[:, :]),
                  ("psE0", ps_e[:, 0:512]), ("psE1", ps_e[:, 512:1024]),
                  ("psF0", ps_f[:, 0:512]), ("psF1", ps_f[:, 512:1024]),
                  ("psG", ps_g[:, :].bitcast(F32)), ("psH", ps_h[:, :])]
        assert nts * nhalf <= len(vunits)
        for ci in range(n_ec):
            wb = ci % 4

            def ldwv(e, ci=ci, wb=wb):
                return e.dma_start(out=sb_wv[wb][:, :], in_=wvb_s.ap()[ci * P:(ci + 1) * P, :])
            I_("gpsimd", "d", ldwv, reads=["wvb_s"], writes=[("wv", wb)], chan=f"wv{wb}")
            for tsub in range(nts):
                for h in range(nhalf):
                    w = min(NW, E - h * NW)
                    uk, ups = vunits[tsub * nhalf + h]

                    def mmv(e, ci=ci, wb=wb, tsub=tsub, h=h, w=w, ups=ups, sp=sp):
                        return e.matmul(ups[:, 0:w],
                                        sb_xTb[sp][:, ci * SBW + tsub * P: ci * SBW + (tsub + 1) * P],
                                        sb_wv[wb][:, h * NW: h * NW + w],
                                        start=(ci == 0), stop=(ci == n_ec - 1))
                    I_("tensor", "c", mmv, reads=[("wv", wb), ("xTb", sp, tsub)], writes=[uk])
                    if ci == n_ec - 1:
                        def evv(e, tt=sb * nts + tsub, h=h, w=w, ups=ups):
                            return e.scalar_tensor_tensor(
                                sb_v[:, tt * E + h * NW: tt * E + h * NW + w], ups[:, 0:w],
                                1.0, sb_bvt[:, h * NW: h * NW + w], op0=ALU.mult, op1=ALU.add)
                        I_("vector", "c", evv, reads=[uk, "bvt"], writes=[("v", sb * nts + tsub)])

    feeder(0)
    for sb in range(n_sb):
        qk_section(sb)
        if sb + 1 < n_sb:
            feeder(sb + 1)
        v_section(sb)

    P_QKV = B.snap()
    qkv.close()

    dq_emitters = []

    def dequant(w_h, n_otiles, ncols, batch_ot, store_ap_fn, tag):
        n_ecs = ncols // P
        for b0 in range(0, n_otiles, batch_ot):
          def _batch(b0=b0, w_h=w_h, n_otiles=n_otiles, ncols=ncols, batch_ot=batch_ot,
                     store_ap_fn=store_ap_fn, tag=tag, n_ecs=n_ecs):
            bt = min(batch_ot, n_otiles - b0)
            w = bt * ncols
            nblk = w // WBLK

            def ld(e, b0=b0, bt=bt, ncols=ncols, w_h=w_h):
                return e.dma_start(
                    out=sb_dqw[:, 0:bt * ncols].rearrange("p (g e) -> p g e", e=ncols),
                    in_=w_h.ap()[b0 * P:(b0 + bt) * P, :].rearrange("(g p) e -> p g e", p=P))
            I_("gpsimd", "d", ld, writes=["dqw"], chan="dqw")

            def v1(e, w=w, nblk=nblk):
                return e.tensor_reduce(sb_dqs[:, 0:nblk],
                                       sb_dqw[:, 0:w].rearrange("p (b k) -> p b k", k=WBLK),
                                       axis=AX.X, op=ALU.max, apply_absolute_value=True)
            I_("vector", "c", v1, reads=["dqw"], writes=["dqs"])
            I_("vector", "c", lambda e, nblk=nblk: e.tensor_scalar(
                sb_dqs[:, 0:nblk], sb_dqs[:, 0:nblk], 1.0 / WQMAX, 1e-12, op0=ALU.mult, op1=ALU.max),
               reads=["dqs"], writes=["dqs"])
            I_("vector", "c", lambda e, nblk=nblk: e.reciprocal(sb_dqr[:, 0:nblk], sb_dqs[:, 0:nblk]),
               reads=["dqs"], writes=["dqr"])

            def v4(e, w=w, nblk=nblk):
                return e.tensor_tensor(sb_dqq[:, 0:w].rearrange("p (b k) -> p b k", k=WBLK),
                                       sb_dqw[:, 0:w].rearrange("p (b k) -> p b k", k=WBLK),
                                       sb_dqr[:, 0:nblk].unsqueeze(2).broadcast_to([P, nblk, WBLK]),
                                       op=ALU.mult)
            I_("vector", "c", v4, reads=["dqw", "dqr"], writes=["dqq"])
            I_("vector", "c", lambda e, w=w: e.tensor_scalar(
                sb_dqq[:, 0:w], sb_dqq[:, 0:w], MAGIC, MAGIC, op0=ALU.add, op1=ALU.subtract),
               reads=["dqq"], writes=["dqq"])

            def v6(e, w=w, nblk=nblk):
                return e.tensor_tensor(sb_dqb[:, 0:w].rearrange("p (b k) -> p b k", k=WBLK),
                                       sb_dqq[:, 0:w].rearrange("p (b k) -> p b k", k=WBLK),
                                       sb_dqs[:, 0:nblk].unsqueeze(2).broadcast_to([P, nblk, WBLK]),
                                       op=ALU.mult)
            I_("vector", "c", v6, reads=["dqq", "dqs"], writes=["dqb"])

            tiles = [(g, ec) for ec in range(n_ecs) for g in range(bt)]
            for t0 in range(0, len(tiles), 8):
                grp = tiles[t0:t0 + 8]

                def tp(e, grp=tuple(grp), ncols=ncols):
                    for si_, (g, ec) in enumerate(grp):
                        ii = e.transpose(ps_g[:, si_ * P:(si_ + 1) * P],
                                         sb_dqb[:, g * ncols + ec * P: g * ncols + (ec + 1) * P],
                                         sb_idb[:, :])
                    return ii
                I_("tensor", "c", tp, reads=["dqb", "idb"], writes=["psG"])

                def ev(e, grp=tuple(grp), t0=t0):
                    return e.tensor_copy(sb_dqt[:, t0 * P:(t0 + len(grp)) * P],
                                         ps_g[:, 0:len(grp) * P])
                I_("vector", "c", ev, reads=["psG"], writes=["dqt"])

            for ec in range(n_ecs):
                def st(e, ec=ec, b0=b0, bt=bt):
                    return e.dma_start(out=store_ap_fn(ec, b0, bt),
                                       in_=sb_dqt[:, ec * bt * P:(ec + 1) * bt * P])
                I_("gpsimd", "d", st, reads=["dqt"], writes=[tag], chan="dag")
          dq_emitters.append(_batch)


    dequant(wu_sh_h, n_ic, cfg.SH_E, 4,
            lambda ec, b0, bt: agu_in.ap()[ec, b0:b0 + bt, :, :].rearrange("t p c -> p t c"), "agu")
    dequant(wg_sh_h, n_ic, cfg.SH_E, 4,
            lambda ec, b0, bt: agg_in.ap()[ec, b0:b0 + bt, :, :].rearrange("t p c -> p t c"), "agg")
    dequant(wd_sh_h, n_ec, cfg.SH_I, 1,
            lambda ec, b0, bt: agd_in.ap()[ec, :, b0 * P:(b0 + bt) * P], "agd")


    def _ags():
        I_("gpsimd", "cc", lambda e: e.collective_compute(
            "AllGather", ALU.bypass, replica_groups=[list(range(n_cores))],
            ins=[agu_in.ap().opt()], outs=[wu_g.ap().opt()]), reads=["agu"], writes=["wug"])
        I_("gpsimd", "cc", lambda e: e.collective_compute(
            "AllGather", ALU.bypass, replica_groups=[list(range(n_cores))],
            ins=[agg_in.ap().opt()], outs=[wg_g.ap().opt()]), reads=["agg"], writes=["wgg"])
        I_("gpsimd", "cc", lambda e: e.collective_compute(
            "AllGather", ALU.bypass, replica_groups=[list(range(n_cores))],
            ins=[agd_in.ap().opt()], outs=[wd_g.ap().opt()]), reads=["agd"], writes=["wdg"])
    dq_emitters.append(_ags)

    att = ExitStack()
    AA = att.enter_context
    sb_stage = AA(nc.sbuf_tensor("stage", [P, T], F32))
    sb_pb = [AA(nc.sbuf_tensor(f"pb{i}", [P, T], BF16)) for i in range(2)]
    sb_pt = [AA(nc.sbuf_tensor(f"pt{i}", [P, max(P, (n_tt - 1) * P)], BF16)) for i in range(2)]
    sb_ksl = [AA(nc.sbuf_tensor(f"ksl{i}", [P, n_ec * P], BF16)) for i in range(2)]
    sb_xres = [AA(nc.sbuf_tensor(f"xres{i}", [P, E], F32)) for i in range(2)]
    sb_xm = AA(nc.sbuf_tensor("xmt", [P, E], F32))
    sb_junk = AA(nc.sbuf_tensor("junk", [P, E], F32))
    sb_zst = AA(nc.sbuf_tensor("zst", [P, 16], F32))
    sb_bn2 = AA(nc.sbuf_tensor("bn2", [P, 6 * n_ln + 8], F32))
    B.set_barrier(P_QKV)

    def strip_chunks(L):
        out = [(0, P)]
        off = P
        while off < L:
            w = min(NW, L - off)
            out.append((off, w))
            off += w
        return out

    def emit_av_epi(J):
        kb = J % 2
        avkeys, av_ps = ((["psE0", "psE1"], ps_e) if J % 2 == 0 else (["psF0", "psF1"], ps_f))
        vkeys = [("v", Jp) for Jp in range(J + 1)]

        def av(e, J=J, kb=kb, ps=av_ps):
            for Jp in range(J + 1):
                lhs = sb_pt[kb][:, Jp * P:(Jp + 1) * P] if Jp < J else sb_pb[kb][:, 0:P]
                for h in range((E + NW - 1) // NW):
                    w = min(NW, E - h * NW)
                    ii = e.matmul(ps[:, h * NW:h * NW + w], lhs,
                                  sb_v[:, Jp * E + h * NW: Jp * E + h * NW + w],
                                  start=(Jp == 0), stop=(Jp == J))
            return ii
        I_("tensor", "c", av, reads=[("pt", kb), ("pb", kb)] + vkeys, writes=avkeys)

        def bns2(e, ps=av_ps):
            for c in range(n_ln):
                w = min(512, E - c * 512)
                ii = e.bn_stats(sb_bn2[:, 6 * c:6 * c + 6], ps[:, c * 512:c * 512 + w])
            return ii
        I_("vector", "c", bns2, reads=avkeys, writes=["bn2"])
        I_("vector", "c", lambda e: e.bn_aggr(sb_bn2[:, 6 * n_ln:6 * n_ln + 2], sb_bn2[:, 0:6 * n_ln]),
           reads=["bn2"], writes=["bn2"])
        I_("scalar", "c", lambda e: e.activation(sb_bn2[:, 6 * n_ln + 2:6 * n_ln + 3],
                                                 sb_bn2[:, 6 * n_ln + 1:6 * n_ln + 2],
                                                 AF.Ln, bias=sb_kc[:, 2:3], scale=sb_kc[:, 6:7]),
           reads=["bn2"], writes=["bn2"])
        I_("scalar", "c", lambda e: e.activation(sb_bn2[:, 6 * n_ln + 3:6 * n_ln + 4],
                                                 sb_bn2[:, 6 * n_ln + 2:6 * n_ln + 3],
                                                 AF.Exp, bias=sb_kc[:, 5:6], scale=sb_kc[:, 1:2]),
           reads=["bn2"], writes=["bn2"])

        I_("vector", "c", lambda e: e.scalar_tensor_tensor(sb_bn2[:, 6 * n_ln + 7:6 * n_ln + 8],
                                                           sb_bn2[:, 6 * n_ln:6 * n_ln + 1], -1.0,
                                                           sb_bn2[:, 6 * n_ln + 3:6 * n_ln + 4],
                                                           op0=ALU.mult, op1=ALU.mult),
           reads=["bn2"], writes=["bn2"])

        def lnav(e, ps=av_ps):
            return e.activation(sb_junk[:, :], ps[:, 0:E], AF.Identity,
                                bias=sb_bn2[:, 6 * n_ln + 7:6 * n_ln + 8],
                                scale=sb_bn2[:, 6 * n_ln + 3:6 * n_ln + 4])
        I_("scalar", "c", lnav, reads=avkeys + ["bn2"], writes=["junk"])
        I_("vector", "c", lambda e: e.tensor_tensor(sb_xm[:, :], sb_junk[:, :], sb_xres[kb][:, :], op=ALU.add),
           reads=["junk", ("xres", kb)], writes=["xm"])

        def bns3(e):
            for c in range(n_ln):
                w = min(512, E - c * 512)
                ii = e.bn_stats(sb_bn2[:, 6 * c:6 * c + 6], sb_xm[:, c * 512:c * 512 + w])
            return ii
        I_("vector", "c", bns3, reads=["xm"], writes=["bn2"])
        I_("vector", "c", lambda e: e.bn_aggr(sb_bn2[:, 6 * n_ln + 4:6 * n_ln + 6], sb_bn2[:, 0:6 * n_ln]),
           reads=["bn2"], writes=["bn2"])
        I_("vector", "c", lambda e, J=J: e.tensor_copy(sb_m2[:, J:J + 1],
                                                       sb_bn2[:, 6 * n_ln + 4:6 * n_ln + 5]),
           reads=["bn2"], writes=["m2"])
        I_("scalar", "c", lambda e: e.activation(sb_bn2[:, 6 * n_ln + 6:6 * n_ln + 7],
                                                 sb_bn2[:, 6 * n_ln + 5:6 * n_ln + 6],
                                                 AF.Ln, bias=sb_kc[:, 2:3], scale=sb_kc[:, 6:7]),
           reads=["bn2"], writes=["bn2"])
        I_("scalar", "c", lambda e, J=J: e.activation(sb_rs2[:, J:J + 1],
                                                      sb_bn2[:, 6 * n_ln + 6:6 * n_ln + 7],
                                                      AF.Exp, bias=sb_kc[:, 5:6], scale=sb_kc[:, 1:2]),
           reads=["bn2"], writes=["rs2"])

        def cand(e, J=J):
            return e.tensor_scalar(sb_junk[:, :], sb_xm[:, :], sb_m2[:, J:J + 1], None,
                                   op0=ALU.subtract)
        I_("vector", "c", cand, reads=["xm", "m2"], writes=["junk"])
        I_("vector", "c", lambda e, J=J: e.reduce_max(sb_cand[:, J:J + 1], sb_junk[:, :],
                                                      axis=AX.X, apply_absolute_value=True),
           reads=["junk"], writes=["cand"])
        I_("vector", "c", lambda e, J=J: e.tensor_scalar(sb_cand[:, J:J + 1], sb_cand[:, J:J + 1],
                                                         sb_rs2[:, J:J + 1], None, op0=ALU.mult),
           reads=["cand", "rs2"], writes=["cand"])
        I_("sync", "d", lambda e, J=J: e.dma_start(out=xm_scr.ap()[J * P:(J + 1) * P, :], in_=sb_xm[:, :]),
           reads=["xm"], writes=["xmscr"], chan="dxm")

    for J in range(n_tt):
        L = T - J * P
        chunks = strip_chunks(L)
        kb = J % 2

        if J < 2:
            def ldk(e, J=J, kb=kb):
                return e.dma_start(out=sb_ksl[kb][:, :].rearrange("p (c j) -> p c j", j=P),
                                   in_=k_scr.ap()[J, :, :, :].rearrange("c p j -> p c j"))
            I_("sync", "d", ldk, reads=["kscr"], writes=[("ksl", kb)], chan=f"ksl{kb}")

        if J > 0:
            def ldpt(e, J=J):
                return e.dma_start(out=sb_pt[J % 2][:, 0:J * P].rearrange("p (t c) -> p t c", c=P),
                                   in_=p_scr.ap()[0:J, J, :, :].rearrange("t p c -> p t c"))
            I_("sync", "d", ldpt, reads=["pscr"], writes=[("pt", J % 2)], chan=f"pt{J % 2}")

        def ldxr(e, J=J):
            return e.dma_start(out=sb_xres[J % 2][:, :], in_=x_h.ap()[J * P:(J + 1) * P, :])
        I_("sync", "d", ldxr, writes=[("xres", J % 2)], chan=f"xres{J % 2}")

        for ci, (off, w) in enumerate(chunks):
            bank, ps = (["psA", ps_a] if ci % 2 == 0 else ["psB", ps_b])

            def smm(e, J=J, kb=kb, off=off, w=w, ps=ps):
                for c in range(n_ec):
                    ii = e.matmul(ps[:, 0:w], sb_ksl[kb][:, c * P:(c + 1) * P],
                                  sb_qT[:, c * T + J * P + off: c * T + J * P + off + w],
                                  start=(c == 0), stop=(c == n_ec - 1))
                return ii
            I_("tensor", "c", smm, reads=[("ksl", kb), "qT"], writes=[bank])

            if ci == 0:
                def sev(e, off=off, w=w, ps=ps):
                    return e.scalar_tensor_tensor(sb_stage[:, off:off + w], ps[:, 0:w],
                                                  1.0, sb_mask[:, :], op0=ALU.mult, op1=ALU.add)
                I_("vector", "c", sev, reads=[bank, "mask"], writes=["stage"])
            else:
                def sev(e, off=off, w=w, ps=ps):
                    return e.tensor_copy(sb_stage[:, off:off + w], ps[:, 0:w])
                I_("vector", "c", sev, reads=[bank], writes=["stage"])

        I_("vector", "c", lambda e, L=L: e.reduce_max(sb_zst[:, 14:15], sb_stage[:, 0:L],
                                                      axis=AX.X, negate=True),
           reads=["stage"], writes=["zst"])

        def expf(e, kb=kb, L=L):
            return e.activation(sb_pb[kb][:, 0:L], sb_stage[:, 0:L], AF.Exp,
                                bias=sb_zst[:, 14:15], scale=1.0,
                                accum_out=sb_zst[:, 15:16])
        I_("scalar", "c", expf, reads=["stage", "zst"], writes=[("pb", kb), "zst"])

        I_("vector", "c", lambda e: e.reciprocal(sb_zst[:, 13:14], sb_zst[:, 15:16]),
           reads=["zst"], writes=["zst"])
        I_("vector", "c", lambda e, J=J: e.tensor_scalar(
            sb_v[:, J * E:(J + 1) * E], sb_v[:, J * E:(J + 1) * E],
            sb_zst[:, 13:14], None, op0=ALU.mult),
           reads=[("v", J), "zst"], writes=[("v", J)])

        def pwr(e, J=J, kb=kb, L=L):
            return e.dma_start(out=p_scr.ap()[J, J:n_tt, :, :].rearrange("t p c -> p t c"),
                               in_=sb_pb[kb][:, 0:L].rearrange("p (t c) -> p t c", c=P))
        I_("sync", "d", pwr, reads=[("pb", kb)], writes=["pscr"], chan="dps")

        if J + 2 < n_tt:
            kb2 = (J + 2) % 2

            def ldk2(e, J=J, kb2=kb2):
                return e.dma_start(out=sb_ksl[kb2][:, :].rearrange("p (c j) -> p c j", j=P),
                                   in_=k_scr.ap()[J + 2, :, :, :].rearrange("c p j -> p c j"))
            I_("sync", "d", ldk2, reads=["kscr"], writes=[("ksl", kb2)], chan=f"ksl{kb2}")

        if J > 0:
            emit_av_epi(J - 1)
        if dq_emitters:
            dq_emitters.pop(0)()

    while dq_emitters:
        dq_emitters.pop(0)()
    emit_av_epi(n_tt - 1)

    P_ATT = B.snap()
    att.close()
    attspan.close()

    def gmax_pack(src_ap, src_keys, col, neg_ap=None, neg_keys=()):
        I_("vector", "c", lambda e: e.reduce_max(sb_sm[:, 0:1], src_ap, axis=AX.X),
           reads=src_keys, writes=["sm"])
        if neg_ap is not None:
            I_("vector", "c", lambda e: e.tensor_reduce(sb_sm[:, 1:2], neg_ap, axis=AX.X,
                                                        op=ALU.min, negate=True),
               reads=list(neg_keys), writes=["sm"])
            I_("vector", "c", lambda e: e.tensor_tensor(sb_sm[:, 0:1], sb_sm[:, 0:1],
                                                        sb_sm[:, 1:2], op=ALU.max),
               reads=["sm"], writes=["sm"])
        I_("tensor", "c", lambda e: e.transpose(ps_h[0:1, 0:P], sb_sm[:, 0:1], sb_idf[:, :]),
           reads=["sm", "idf"], writes=["psH"])
        I_("vector", "c", lambda e: e.tensor_copy(sb_smr[:, :], ps_h[0:1, 0:P]),
           reads=["psH"], writes=["smr"])
        I_("vector", "c", lambda e, col=col: e.reduce_max(sb_sc[:, col:col + 1], sb_smr[:, :], axis=AX.X),
           reads=["smr"], writes=["sc"])

    def allreduce_bcast(idx, gcol):
        I_("gpsimd", "d", lambda e, idx=idx: e.dma_start(out=cc_in[idx].ap(), in_=sb_sc[:, :]),
           reads=["sc"], writes=[("ccin", idx)], chan="ccio")
        I_("gpsimd", "cc", lambda e, idx=idx: e.collective_compute(
            "AllReduce", ALU.max, replica_groups=[list(range(n_cores))],
            ins=[cc_in[idx].ap().opt()], outs=[cc_out[idx].ap().opt()]),
           reads=[("ccin", idx)], writes=[("ccout", idx)])
        I_("gpsimd", "d", lambda e, idx=idx: e.dma_start(out=sb_sc[:, :], in_=cc_out[idx].ap()),
           reads=[("ccout", idx)], writes=["sc"], chan="ccio")
        I_("tensor", "c", lambda e: e.matmul(ps_h[:, 0:2], sb_ones[:, :], sb_sc[:, :], start=True, stop=True),
           reads=["sc", "ones"], writes=["psH"])
        I_("vector", "c", lambda e, gcol=gcol: e.tensor_copy(sb_gv[:, gcol:gcol + 2], ps_h[:, 0:2]),
           reads=["psH"], writes=["gv"])

    def gvts(dst, src, s1, s2, o1, o2):
        if s2 is None:
            I_("vector", "c", lambda e: e.tensor_scalar(sb_gv[:, dst:dst + 1],
                                                        sb_gv[:, src:src + 1], s1, None, op0=o1),
               reads=["gv"], writes=["gv"])
        else:
            I_("vector", "c", lambda e: e.tensor_scalar(sb_gv[:, dst:dst + 1],
                                                        sb_gv[:, src:src + 1], s1, s2, op0=o1, op1=o2),
               reads=["gv"], writes=["gv"])

    def gvtt(dst, a, b):
        I_("vector", "c", lambda e: e.tensor_tensor(sb_gv[:, dst:dst + 1], sb_gv[:, a:a + 1],
                                                    sb_gv[:, b:b + 1], op=ALU.mult),
           reads=["gv"], writes=["gv"])

    def gvrec(dst, src):
        I_("vector", "c", lambda e: e.reciprocal(sb_gv[:, dst:dst + 1], sb_gv[:, src:src + 1]),
           reads=["gv"], writes=["gv"])

    gmax_pack(sb_cand[:, 0:n_tt], ["cand"], 0)
    allreduce_bcast(0, 0)
    gvts(2, 0, 1.0 / QMAX, 1e-12, ALU.mult, ALU.max)
    gvrec(3, 2)
    gvts(28, 2, -MAGIC, None, ALU.mult, ALU.mult)
    I_("vector", "c", lambda e: e.tensor_scalar(sb_rs2r1[:, :], sb_rs2[:, :], sb_gv[:, 3:4], None,
                                                op0=ALU.mult),
       reads=["rs2", "gv"], writes=["rs2r1"])

    f2 = ExitStack()
    F = f2.enter_context
    sb_xqT = F(nc.sbuf_tensor("xqT", [P, n_ec * T], BF16))
    sb_xmr = [F(nc.sbuf_tensor(f"xmr{i}", [P, E], F32)) for i in range(4)]
    sb_xq = [F(nc.sbuf_tensor(f"xq{i}", [P, E], F32)) for i in range(2)]
    sb_xqb = [F(nc.sbuf_tensor(f"xqb{i}", [P, E], BF16)) for i in range(2)]
    sb_bias2 = F(nc.sbuf_tensor("bias2", [P, n_tt], F32))
    sb_wfu = [F(nc.sbuf_tensor(f"wfu{i}", [P, n_ec * P], BF16)) for i in range(3)]
    sb_wfg = [F(nc.sbuf_tensor(f"wfg{i}", [P, n_ec * P], BF16)) for i in range(3)]
    sb_ugst = [F(nc.sbuf_tensor(f"ugst{i}", [P, 1024], F32)) for i in range(8)]
    n_mx = n_ic * (T // NW)
    sb_um = F(nc.sbuf_tensor("um", [P, n_mx], F32))
    sb_gm = F(nc.sbuf_tensor("gm", [P, n_mx], F32))
    B.set_barrier(P_ATT)

    I_("vector", "c", lambda e: e.tensor_tensor(sb_bias2[:, :], sb_m2[:, :], sb_rs2r1[:, :],
                                                op=ALU.mult),
       reads=["m2", "rs2r1"], writes=["bias2"])
    I_("vector", "c", lambda e: e.tensor_scalar(sb_bias2[:, :], sb_bias2[:, :], -1.0, MAGIC,
                                                op0=ALU.mult, op1=ALU.add),
       reads=["bias2"], writes=["bias2"])

    def chain(tt):
        xb = tt % 4
        qb = tt % 2
        if tt < 4:
            I_("sync", "d", lambda e, tt=tt, xb=xb: e.dma_start(
                out=sb_xmr[xb][:, :], in_=xm_scr.ap()[tt * P:(tt + 1) * P, :]),
               reads=["xmscr"], writes=[("xmr", xb)], chan=f"xmr{xb}")
        I_("scalar", "c", lambda e, tt=tt, xb=xb, qb=qb: e.activation(
            sb_xq[qb][:, :], sb_xmr[xb][:, :], AF.Identity,
            bias=sb_bias2[:, tt:tt + 1], scale=sb_rs2r1[:, tt:tt + 1]),
           reads=[("xmr", xb), "bias2", "rs2r1"], writes=[("xq", qb)])
        if tt + 4 < n_tt:
            I_("scalar", "d", lambda e, tt=tt, xb=xb: e.dma_start(
                out=sb_xmr[xb][:, :], in_=xm_scr.ap()[(tt + 4) * P:(tt + 5) * P, :]),
               reads=["xmscr"], writes=[("xmr", xb)], chan=f"xmr{xb}")
        I_("scalar", "c", lambda e, qb=qb: e.activation(
            sb_xqb[qb][:, :], sb_xq[qb][:, :], AF.Identity,
            bias=sb_gv[:, 28:29], scale=sb_gv[:, 2:3]),
           reads=[("xq", qb), "gv"], writes=[("xqb", qb)])

        tkey, tdst = [("psG", ps_g[:, :]), ("psH", ps_h[:, :].bitcast(BF16))][tt % 2]

        def xtp(e, qb=qb, tdst=tdst):
            for c in range(n_ec):
                ii = e.transpose(tdst[:, c * P:(c + 1) * P], sb_xqb[qb][:, c * P:(c + 1) * P],
                                 sb_idb[:, :])
            return ii
        I_("tensor", "c", xtp, reads=[("xqb", qb), "idb"], writes=[tkey])

        def xte(e, tt=tt, tdst=tdst):
            for c in range(n_ec):
                ii = e.tensor_copy(sb_xqT[:, c * T + tt * P: c * T + (tt + 1) * P],
                                   tdst[:, c * P:(c + 1) * P])
            return ii
        I_("vector", "c", xte, reads=[tkey], writes=[("xqT", tt)])

    def gsk(it, off, w):
        return [("gscr", it, (off + k * 512) // 512) for k in range((w + 511) // 512)]

    SW2 = min(1024, T)
    n_h = T // NW
    n_grp = max(1, n_h // 4)
    nh4 = min(4, n_h)
    tpg = nh4 * (NW // P)
    halfgroups = [
        [("psA", ps_a[:, :]), ("psB", ps_b[:, :]),
         ("psE0", ps_e[:, 0:512]), ("psE1", ps_e[:, 512:1024])],
        [("psF0", ps_f[:, 0:512]), ("psF1", ps_f[:, 512:1024]),
         ("psG", ps_g[:, :].bitcast(F32)), ("psH", ps_h[:, :])],
    ]
    tis = ((sb_wfu, "wfu", sb_um, "um", u_scr, "uscr"),
           (sb_wfg, "wfg", sb_gm, "gm", g_scr, "gscr"))

    def emit_w(it):
        wb = it % 3
        I_("gpsimd", "d", lambda e, it=it, wb=wb: e.dma_start(
            out=sb_wfu[wb][:, :].rearrange("p (c f) -> p c f", f=P),
            in_=wu_g.ap()[:, it, :, :].rearrange("c p f -> p c f")),
           reads=["wug"], writes=[("wfu", wb)], chan=f"wfu{wb}")
        I_("gpsimd", "d", lambda e, it=it, wb=wb: e.dma_start(
            out=sb_wfg[wb][:, :].rearrange("p (c f) -> p c f", f=P),
            in_=wg_g.ap()[:, it, :, :].rearrange("c p f -> p c f")),
           reads=["wgg"], writes=[("wfg", wb)], chan=f"wfg{wb}")

    def group(it, ti, half, hs=None):
        wb = it % 3
        wf, wkn, mx, mxkey, scr, stag = tis[ti]
        wkey = (wkn, wb)
        grp = halfgroups[half % 2][:nh4]
        if hs is None:
            hs = list(range(nh4))
        gkeys = [grp[h4][0] for h4 in hs]
        tkeys = []
        for h4 in hs:
            h = half * nh4 + h4
            tkeys += [("xqT", t) for t in range(h * (NW // P), min((h + 1) * (NW // P), n_tt))]

        def umm(e, wf=wf, wb=wb, half=half, hs=tuple(hs), grp=tuple(grp)):
            for c in range(n_ec):
                for h4 in hs:
                    h = half * nh4 + h4
                    ii = e.matmul(grp[h4][1][:, :], wf[wb][:, c * P:(c + 1) * P],
                                  sb_xqT[:, c * T + h * NW: c * T + (h + 1) * NW],
                                  start=(c == 0), stop=(c == n_ec - 1))
            return ii
        I_("tensor", "c", umm, reads=[wkey] + tkeys, writes=gkeys)

        hi = 0
        while hi < len(hs):
            h4 = hs[hi]
            h = half * nh4 + h4
            hh = h // (SW2 // NW)
            h2 = h % (SW2 // NW)
            sbuf_i = ti * 4 + hh
            gk, gps = grp[h4]
            merged = False
            if (hi + 1 < len(hs)) and hs[hi + 1] == h4 + 1 and h2 == 0 and (gk[:3] in ("psE", "psF")):
                gk2 = grp[h4 + 1][0]
                big = ps_e if gk[:3] == "psE" else ps_f

                def uev2(e, big=big, sbuf_i=sbuf_i):
                    return e.tensor_copy(sb_ugst[sbuf_i][:, 0:2 * NW], big[:, 0:2 * NW])
                I_("vector", "c", uev2, reads=[gk, gk2], writes=[("ugst", sbuf_i)])
                merged = True
            else:
                def uev(e, gps=gps, sbuf_i=sbuf_i, h2=h2):
                    return e.tensor_copy(sb_ugst[sbuf_i][:, h2 * NW:(h2 + 1) * NW], gps[:, :])
                I_("vector", "c", uev, reads=[gk], writes=[("ugst", sbuf_i)])
            reached = h2 + (1 if merged else 0)
            if reached == (SW2 // NW) - 1:
                slot = it * (T // SW2) + hh

                def umax(e, sbuf_i=sbuf_i, mx=mx, slot=slot, SW2=SW2):
                    return e.reduce_max(mx[:, slot:slot + 1],
                                        sb_ugst[sbuf_i][:, 0:SW2],
                                        axis=AX.X, apply_absolute_value=True)
                I_("vector", "c", umax, reads=[("ugst", sbuf_i)], writes=[mxkey])

                def ust(e, it=it, hh=hh, sbuf_i=sbuf_i, scr=scr, SW2=SW2):
                    return e.dma_start(out=scr.ap()[it, :, hh * SW2:(hh + 1) * SW2],
                                       in_=sb_ugst[sbuf_i][:, 0:SW2])
                wkeys = [stag] if stag != "gscr" else gsk(it, hh * SW2, SW2)
                I_("sync", "d", ust, reads=[("ugst", sbuf_i)], writes=wkeys, chan="dug")
            hi += 2 if merged else 1

    if n_grp > 1:
        emit_w(0)
        emit_w(1)
        for j in range(nh4):
            for tt in range(j * (NW // P), (j + 1) * (NW // P)):
                chain(tt)
            group(0, 0, 0, hs=[j])
        group(0, 1, 0)
        for tt in range(tpg, tpg + 4):
            chain(tt)
        group(1, 0, 0)
        for tt in range(tpg + 4, tpg + 8):
            chain(tt)
        group(1, 1, 0)
        for tt in range(tpg + 8, tpg + 12):
            chain(tt)
        emit_w(2)
        group(2, 0, 0)
        for tt in range(tpg + 12, n_tt):
            chain(tt)
        group(2, 1, 0)
        for it in (0, 1, 2):
            for ti in range(2):
                group(it, ti, 1)
        start_it = 3
    else:
        for tt in range(n_tt):
            chain(tt)
        start_it = 0
    for it in range(start_it, n_ic):
        emit_w(it)
        for ti in range(2):
            for half in range(n_grp):
                group(it, ti, half)

    gmax_pack(sb_um[:, 0:n_ic * (T // min(1024, T))], ["um"], 0)
    gmax_pack(sb_gm[:, 0:n_ic * (T // min(1024, T))], ["gm"], 1)
    allreduce_bcast(1, 4)
    gvts(6, 4, 1.0 / QMAX, 1e-12, ALU.mult, ALU.max)
    gvrec(7, 6)
    gvts(8, 5, 1.0 / QMAX, 1e-12, ALU.mult, ALU.max)
    gvrec(9, 8)
    gvts(10, 8, -MAGIC, None, ALU.mult, ALU.mult)
    gvts(11, 8, 1.0 / FIXQ, None, ALU.mult, ALU.mult)
    P_F2 = B.snap()
    f2.close()

    W3 = cfg.W3
    n_w3 = T // W3
    f34 = ExitStack()
    F3 = f34.enter_context
    sb_gin = [F3(nc.sbuf_tensor(f"gin{i}", [P, W3], F32)) for i in range(3)]
    sb_uin = [F3(nc.sbuf_tensor(f"uin{i}", [P, W3], F32)) for i in range(3)]
    sb_t1 = [F3(nc.sbuf_tensor(f"t1_{i}", [P, W3], F32)) for i in range(3)]
    sb_t2 = [F3(nc.sbuf_tensor(f"t2_{i}", [P, W3], F32)) for i in range(3)]
    sb_pr = [F3(nc.sbuf_tensor(f"pr_{i}", [P, W3], F32)) for i in range(3)]
    nch34 = n_ic * n_w3
    sb_pm3 = F3(nc.sbuf_tensor("pm3", [P, nch34], F32))
    sb_pm4 = F3(nc.sbuf_tensor("pm4", [P, nch34], F32))
    B.set_barrier(P_F2)

    def chain_sig(ci, b, with_u):
        I_("scalar", "c", lambda e, b=b: e.activation(sb_t1[b][:, :], sb_gin[b][:, :], AF.Copy,
                                                      bias=MAGIC, scale=sb_gv[:, 9:10]),
           reads=[("gin", b), "gv", "kc"], writes=[("t1", b)])
        I_("scalar", "c", lambda e, b=b: e.activation(sb_t2[b][:, :], sb_t1[b][:, :], AF.Sigmoid,
                                                      bias=sb_gv[:, 10:11], scale=sb_gv[:, 8:9]),
           reads=[("t1", b), "gv"], writes=[("t2", b)])
        I_("scalar", "c", lambda e, b=b: e.activation(sb_t2[b][:, :], sb_t2[b][:, :], AF.Copy,
                                                      bias=MAGIC, scale=sb_kc[:, 3:4]),
           reads=[("t2", b), "kc"], writes=[("t2", b)])
        I_("vector", "c", lambda e, b=b: e.tensor_scalar(sb_t2[b][:, :], sb_t2[b][:, :], MAGIC, 65535.0,
                                                         op0=ALU.subtract, op1=ALU.min),
           reads=[("t2", b)], writes=[("t2", b)])
        I_("vector", "c", lambda e, b=b: e.scalar_tensor_tensor(sb_pr[b][:, :], sb_t1[b][:, :], MAGIC,
                                                                sb_t2[b][:, :], op0=ALU.subtract,
                                                                op1=ALU.mult),
           reads=[("t1", b), ("t2", b)], writes=[("pr", b)])
        if with_u:
            I_("scalar", "c", lambda e, b=b: e.activation(sb_t1[b][:, :], sb_pr[b][:, :],
                                                          AF.Identity, bias=MAGIC,
                                                          scale=sb_gv[:, 16:17]),
               reads=[("pr", b), "gv", "kc"], writes=[("t1", b)])
            I_("vector", "c", lambda e, b=b: e.tensor_scalar(sb_t1[b][:, :], sb_t1[b][:, :], MAGIC, None,
                                                             op0=ALU.subtract),
               reads=[("t1", b)], writes=[("t1", b)])
            I_("scalar", "c", lambda e, b=b: e.activation(sb_t2[b][:, :], sb_uin[b][:, :], AF.Copy,
                                                          bias=MAGIC, scale=sb_gv[:, 7:8]),
               reads=[("uin", b), "gv", "kc"], writes=[("t2", b)])
            I_("vector", "c", lambda e, b=b: e.scalar_tensor_tensor(sb_pr[b][:, :], sb_t2[b][:, :], MAGIC,
                                                                    sb_t1[b][:, :], op0=ALU.subtract,
                                                                    op1=ALU.mult),
               reads=[("t1", b), ("t2", b)], writes=[("pr", b)])

    def f34_stageA(ci, b, with_u):
        it, h = divmod(ci, n_w3)
        I_("sync", "d", lambda e, it=it, h=h, b=b: e.dma_start(
            out=sb_gin[b][:, :], in_=g_scr.ap()[it, :, h * W3:(h + 1) * W3]),
           reads=gsk(it, h * W3, W3), writes=[("gin", b)], chan=f"gin{b}")
        if with_u:
            I_("gpsimd", "d", lambda e, it=it, h=h, b=b: e.dma_start(
                out=sb_uin[b][:, :], in_=u_scr.ap()[it, :, h * W3:(h + 1) * W3]),
               reads=["uscr"], writes=[("uin", b)], chan=f"uin{b}")
            I_("scalar", "c", lambda e, b=b: e.activation(sb_uin[b][:, :], sb_uin[b][:, :], AF.Copy,
                                                          bias=MAGIC, scale=sb_gv[:, 7:8]),
               reads=[("uin", b), "gv", "kc"], writes=[("uin", b)])
            return
        I_("scalar", "c", lambda e, b=b: e.activation(sb_t1[b][:, :], sb_gin[b][:, :], AF.Copy,
                                                      bias=MAGIC, scale=sb_gv[:, 9:10]),
           reads=[("gin", b), "gv", "kc"], writes=[("t1", b)])
        I_("scalar", "c", lambda e, b=b: e.activation(sb_t2[b][:, :], sb_t1[b][:, :], AF.Sigmoid,
                                                      bias=sb_gv[:, 10:11], scale=sb_gv[:, 8:9]),
           reads=[("t1", b), "gv"], writes=[("t2", b)])
        I_("scalar", "c", lambda e, b=b: e.activation(sb_t2[b][:, :], sb_t2[b][:, :], AF.Copy,
                                                      bias=MAGIC, scale=sb_kc[:, 3:4]),
           reads=[("t2", b), "kc"], writes=[("t2", b)])

    def f34_stageB(ci, b, with_u):
        it, h = divmod(ci, n_w3)
        if not with_u:
            I_("vector", "c", lambda e, b=b: e.tensor_scalar(sb_t2[b][:, :], sb_t2[b][:, :], MAGIC,
                                                             65535.0, op0=ALU.subtract, op1=ALU.min),
               reads=[("t2", b)], writes=[("t2", b)])
            I_("vector", "c", lambda e, b=b: e.scalar_tensor_tensor(sb_pr[b][:, :], sb_t1[b][:, :],
                                                                    MAGIC, sb_t2[b][:, :],
                                                                    op0=ALU.subtract, op1=ALU.mult),
               reads=[("t1", b), ("t2", b)], writes=[("pr", b)])
            I_("vector", "c", lambda e, b=b, ci=ci: e.reduce_max(
                sb_pm3[:, ci:ci + 1], sb_pr[b][:, :], axis=AX.X, apply_absolute_value=True),
               reads=[("pr", b)], writes=["pm3"])
            I_("scalar", "d", lambda e, it=it, h=h, b=b: e.dma_start(
                out=g_scr.ap()[it, :, h * W3:(h + 1) * W3], in_=sb_pr[b][:, :]),
               reads=[("pr", b)], writes=gsk(it, h * W3, W3), chan="dp3")
            return
        I_("vector", "c", lambda e, b=b: e.tensor_scalar(sb_t1[b][:, :], sb_gin[b][:, :],
                                                         sb_gv[:, 16:17], MAGIC,
                                                         op0=ALU.mult, op1=ALU.add),
           reads=[("gin", b), "gv"], writes=[("t1", b)])
        I_("vector", "c", lambda e, b=b: e.tensor_scalar(sb_t1[b][:, :], sb_t1[b][:, :], MAGIC, None,
                                                         op0=ALU.subtract),
           reads=[("t1", b)], writes=[("t1", b)])
        I_("vector", "c", lambda e, b=b: e.scalar_tensor_tensor(sb_pr[b][:, :], sb_uin[b][:, :], MAGIC,
                                                                sb_t1[b][:, :], op0=ALU.subtract,
                                                                op1=ALU.mult),
           reads=[("t1", b), ("uin", b)], writes=[("pr", b)])
        I_("vector", "c", lambda e, b=b, ci=ci: e.reduce_max(
            sb_pm4[:, ci:ci + 1], sb_pr[b][:, :], axis=AX.X, apply_absolute_value=True),
           reads=[("pr", b)], writes=["pm4"])
        I_("scalar", "d", lambda e, it=it, h=h, b=b: e.dma_start(
            out=g_scr.ap()[it, :, h * W3:(h + 1) * W3], in_=sb_pr[b][:, :]),
           reads=[("pr", b)], writes=gsk(it, h * W3, W3), chan="dpr")

    for ci in range(nch34):
        f34_stageA(ci, ci % 3, with_u=False)
        if ci >= 1:
            f34_stageB(ci - 1, (ci - 1) % 3, with_u=False)
    f34_stageB(nch34 - 1, (nch34 - 1) % 3, with_u=False)

    gmax_pack(sb_pm3[:, 0:nch34], ["pm3"], 0)
    allreduce_bcast(2, 12)
    gvtt(13, 12, 11)
    gvts(14, 13, 1.0 / QMAX, 1e-12, ALU.mult, ALU.max)
    gvrec(15, 14)
    gvtt(16, 11, 15)

    for ci in range(nch34):
        f34_stageA(ci, ci % 3, with_u=True)
        if ci >= 1:
            f34_stageB(ci - 1, (ci - 1) % 3, with_u=True)
    f34_stageB(nch34 - 1, (nch34 - 1) % 3, with_u=True)

    gmax_pack(sb_pm4[:, 0:nch34], ["pm4"], 0)
    allreduce_bcast(3, 17)
    gvtt(19, 14, 6)
    gvtt(18, 19, 17)
    gvts(20, 18, 1.0 / QMAX, 1e-12, ALU.mult, ALU.max)
    gvrec(21, 20)
    gvtt(22, 19, 21)
    gvts(23, 20, -MAGIC, None, ALU.mult, ALU.mult)
    P_F4 = B.snap()
    f34.close()

    F5S = cfg.F5S
    n_f5 = T // F5S
    f5 = ExitStack()
    F5 = f5.enter_context
    GW_ = min(1024, T)
    sb_osup = [F5(nc.sbuf_tensor(f"osup{i}", [P, n_ic * GW_], BF16)) for i in range(2)]
    sb_wd = F5(nc.sbuf_tensor("wd", [P, n_ic * E], BF16))
    sb_g5 = [F5(nc.sbuf_tensor(f"g5_{i}", [P, GW_], F32)) for i in range(4)]
    sb_pr5 = [F5(nc.sbuf_tensor(f"pr5_{i}", [P, GW_], F32)) for i in range(2)]
    sb_xm5 = [F5(nc.sbuf_tensor(f"xm5_{i}", [P, E], F32)) for i in range(2)]
    sb_ot = [F5(nc.sbuf_tensor(f"ot{i}", [P, E], F32)) for i in range(2)]
    sb_bn5 = F5(nc.sbuf_tensor("bn5", [P, 6 * n_ln + 8], F32))
    B.set_barrier(P_F4)

    I_("gpsimd", "d", lambda e: e.dma_start(out=sb_wd[:, :].rearrange("p (i e) -> p i e", e=E),
                                            in_=wd_g.ap().rearrange("i p e -> p i e")),
       reads=["wdg"], writes=["wd"], chan="wd")

    dpairs = [
        (["psE0", "psE1"], [(ps_e[:, 0:512], 0), (ps_e[:, 512:1024], 512)]),
        (["psF0", "psF1"], [(ps_f[:, 0:512], 0), (ps_f[:, 512:1024], 512)]),
        (["psA", "psB"], [(ps_a[:, :], 0), (ps_b[:, :], 512)]),
        (["psG", "psH"], [(ps_g[:, :].bitcast(F32), 0), (ps_h[:, :], 512)]),
    ]
    nhalf5 = (E + NW - 1) // NW
    GW = min(1024, T)
    n_g5 = T // GW
    gtiles = GW // P

    def f5_elem(g, it0, it1, nch=1):
        gb = g % 2
        W = GW // nch
        for it in range(it0, it1):
            for ch in range(nch):
                b = ((g * n_ic + it) * nch + ch) % 4
                o0 = ch * W
                I_("sync", "d", lambda e, it=it, g=g, b=b, o0=o0, W=W: e.dma_start(
                    out=sb_g5[b][:, 0:W], in_=g_scr.ap()[it, :, g * GW + o0:g * GW + o0 + W]),
                   reads=gsk(it, g * GW + o0, W), writes=[("g5", b)], chan=f"g5{b}")
                I_("scalar", "c", lambda e, b=b, W=W: e.activation(sb_pr5[b % 2][:, 0:W],
                                                                   sb_g5[b][:, 0:W],
                                                                   AF.Identity, bias=MAGIC,
                                                                   scale=sb_gv[:, 22:23]),
                   reads=[("g5", b), "gv", "kc"], writes=[("pr5", b % 2)])
                I_("vector", "c", lambda e, b=b, gb=gb, it=it, o0=o0, W=W: e.tensor_scalar(
                    sb_osup[gb][:, it * GW + o0:it * GW + o0 + W], sb_pr5[b % 2][:, 0:W],
                    sb_gv[:, 20:21], sb_gv[:, 23:24], op0=ALU.mult, op1=ALU.add),
                   reads=[("pr5", b % 2), "gv"], writes=[("osup", gb, it)])

    def f5_mm(g, tsub):
        gb = g % 2
        tt = g * gtiles + tsub
        okeys = [("osup", gb, it) for it in range(n_ic)]
        dkeys, dhalves = dpairs[tt % 4]

        def dmm(e, gb=gb, tsub=tsub, dhalves=tuple(dhalves)):
            for it in range(n_ic):
                for h in range(nhalf5):
                    w = min(NW, E - h * NW)
                    pp, _ = dhalves[h]
                    ii = e.matmul(pp[:, 0:w],
                                  sb_osup[gb][:, it * GW + tsub * P: it * GW + (tsub + 1) * P],
                                  sb_wd[:, it * E + h * NW: it * E + h * NW + w],
                                  start=(it == 0), stop=(it == n_ic - 1))
            return ii
        I_("tensor", "c", dmm, reads=okeys + ["wd"], writes=dkeys)

    def f5_epi(tt):
        dkeys, dhalves = dpairs[tt % 4]
        xb5 = tt % 2
        I_("sync", "d", lambda e, tt=tt, xb5=xb5: e.dma_start(
            out=sb_xm5[xb5][:, :], in_=xm_scr.ap()[tt * P:(tt + 1) * P, :]),
           reads=["xmscr"], writes=[("xm5", xb5)], chan=f"xm5{xb5}")

        def bns5(e, dhalves=tuple(dhalves)):
            for c in range(n_ln):
                w = min(512, E - c * 512)
                pp, off = dhalves[c]
                ii = e.bn_stats(sb_bn5[:, 6 * c:6 * c + 6], pp[:, 0:w])
            return ii
        I_("vector", "c", bns5, reads=dkeys, writes=["bn5"])
        I_("vector", "c", lambda e: e.bn_aggr(sb_bn5[:, 6 * n_ln:6 * n_ln + 2], sb_bn5[:, 0:6 * n_ln]),
           reads=["bn5"], writes=["bn5"])
        I_("scalar", "c", lambda e: e.activation(sb_bn5[:, 6 * n_ln + 2:6 * n_ln + 3],
                                                 sb_bn5[:, 6 * n_ln + 1:6 * n_ln + 2],
                                                 AF.Ln, bias=sb_kc[:, 2:3], scale=sb_kc[:, 6:7]),
           reads=["bn5"], writes=["bn5"])
        I_("scalar", "c", lambda e: e.activation(sb_bn5[:, 6 * n_ln + 3:6 * n_ln + 4],
                                                 sb_bn5[:, 6 * n_ln + 2:6 * n_ln + 3],
                                                 AF.Exp, bias=sb_kc[:, 5:6], scale=sb_kc[:, 1:2]),
           reads=["bn5"], writes=["bn5"])

        def lny(e, xb5=xb5, dhalves=tuple(dhalves)):
            for c in range(n_ln):
                w = min(512, E - c * 512)
                pp, off = dhalves[c]
                ii = e.tensor_scalar(sb_ot[xb5][:, off:off + w], pp[:, 0:w],
                                     sb_bn5[:, 6 * n_ln:6 * n_ln + 1],
                                     sb_bn5[:, 6 * n_ln + 3:6 * n_ln + 4],
                                     op0=ALU.subtract, op1=ALU.mult)
            return ii
        I_("vector", "c", lny, reads=dkeys + ["bn5"], writes=[("ot", xb5)])
        I_("vector", "c", lambda e, xb5=xb5: e.tensor_tensor(sb_ot[xb5][:, :], sb_ot[xb5][:, :],
                                                             sb_xm5[xb5][:, :], op=ALU.add),
           reads=[("ot", xb5), ("xm5", xb5)], writes=[("ot", xb5)])
        I_("sync", "d", lambda e, tt=tt, xb5=xb5: e.dma_start(
            out=out_h.ap()[tt * P:(tt + 1) * P, :], in_=sb_ot[xb5][:, :]),
           reads=[("ot", xb5)], writes=["outh"], chan="dout")

    f5_elem(0, 0, n_ic, nch=2)
    spread = min(6, gtiles)
    for g in range(n_g5):
        for tsub in range(gtiles):
            tt = g * gtiles + tsub
            f5_mm(g, tsub)
            if g + 1 < n_g5 and tsub < spread:
                f5_elem(g + 1, (n_ic * tsub) // spread, (n_ic * (tsub + 1)) // spread)
            if tt >= 3:
                f5_epi(tt - 3)
    f5_epi(T // P - 3)
    f5_epi(T // P - 2)
    f5_epi(T // P - 1)

    fin = B.sems["D:dout"][1]
    B.raw("sync", lambda e, fin=fin: e.wait_ge(B.sems["D:dout"][0], fin))

    with nc.Block() as block:
        @block.sync
        def _(e):
            for fn in B.streams["sync"]:
                fn(e)

        @block.tensor
        def _(e):
            for fn in B.streams["tensor"]:
                fn(e)

        @block.vector
        def _(e):
            for fn in B.streams["vector"]:
                fn(e)

        @block.scalar
        def _(e):
            for fn in B.streams["scalar"]:
                fn(e)

        @block.gpsimd
        def _(e):
            for fn in B.streams["gpsimd"]:
                fn(e)

    f5.close()
    es.close()
    return nc


def make_inputs(cfg, X, Wq, bq, Wk, bk, Wv, bv, w_gate, w_up, w_down):
    f32 = np.float32
    wqT = np.ascontiguousarray(np.asarray(Wq, f32).T)
    wkT = np.ascontiguousarray(np.asarray(Wk, f32).T)
    wvT = np.ascontiguousarray(np.asarray(Wv, f32).T)
    bq = np.ascontiguousarray(np.asarray(bq, f32).reshape(cfg.n_ec, P).T)
    bk = np.ascontiguousarray(np.asarray(bk, f32).reshape(cfg.n_ec, P).T)
    bvt = np.ascontiguousarray(np.broadcast_to(np.asarray(bv, f32), (P, cfg.E)))
    mask = np.where(np.arange(P)[None, :] >= np.arange(P)[:, None], 0.0, NEG).astype(f32)
    idb = np.eye(P, dtype=ml_dtypes.bfloat16)
    idf = np.eye(P, dtype=f32)
    w_up = np.asarray(w_up, f32)
    w_gate = np.asarray(w_gate, f32)
    w_down = np.asarray(w_down, f32)
    X = np.asarray(X, f32)
    in_maps = []
    for c in range(cfg.n_cores):
        in_maps.append({
            "x": np.ascontiguousarray(X[c]),
            "wqT": wqT, "wkT": wkT, "wvT": wvT,
            "bq": bq, "bk": bk, "bvt": bvt,
            "wu_sh": np.ascontiguousarray(w_up[:, c * cfg.SH_E:(c + 1) * cfg.SH_E]),
            "wg_sh": np.ascontiguousarray(w_gate[:, c * cfg.SH_E:(c + 1) * cfg.SH_E]),
            "wd_sh": np.ascontiguousarray(w_down[:, c * cfg.SH_I:(c + 1) * cfg.SH_I]),
            "mask": mask, "idb": idb, "idf": idf,
        })
    return in_maps


def kernel(L0_attn_X, L0_attn_Wq, L0_attn_bq, L0_attn_Wk, L0_attn_bk,
           L0_attn_Wv, L0_attn_bv, L0_ffn_w_gate, L0_ffn_w_up, L0_ffn_w_down):
    cfg = Cfg(T=4096, E=1024, I=3072, n_cores=8, attn_scale=1.0 / math.sqrt(1024.0))
    nc = build_nc(cfg)
    in_maps = make_inputs(cfg, L0_attn_X, L0_attn_Wq, L0_attn_bq, L0_attn_Wk,
                          L0_attn_bk, L0_attn_Wv, L0_attn_bv,
                          L0_ffn_w_gate, L0_ffn_w_up, L0_ffn_w_down)
    res = run_bass_kernel_spmd(nc, in_maps, core_ids=list(range(cfg.n_cores)))
    out = np.stack([np.asarray(r["out"], dtype=np.float32) for r in res.results])
    return out
